# version 11
# baseline (speedup 1.0000x reference)
import contextlib
import numpy as np

import concourse.bass as bass
from concourse.library_overlay import lower_extended_insts
import concourse.mybir as mybir
from concourse.bass_utils import run_bass_kernel_spmd
from concourse import library_config

F32 = mybir.dt.float32
I16 = mybir.dt.int16
U32 = mybir.dt.uint32
AF = mybir.ActivationFunctionType
ALU = mybir.AluOpType
AX = mybir.AxisListType
FMAX = float(np.finfo(np.float32).max)

D = 2048
NCH = 16
TOWN = 2048
HEADS = 16
E = 8
HME = 1408
NHC = 11
HSH = 2816
NSC = 22
NG = 4
CAP = 384
XROW = 2112
SCALE = 1.0 / float(np.sqrt(np.float32(128)))
EPS = 1e-5
INVD = 1.0 / float(np.float32(D))


class Sem:
    def __init__(self, nc, es, name):
        self.h = es.enter_context(nc.semaphore(name))
        self.name = name
        self.n = 0

    def inc(self, instr, k=1):
        instr.then_inc(self.h, k)
        self.n += k
        return self.n


class _DummyInstr:
    def then_inc(self, h, k):
        return self


class _CountEngine:
    def __getattr__(self, name):
        def _f(*a, **k):
            return _DummyInstr()
        return _f


class Env:
    def __init__(self, block, sems):
        self.block = block
        self.sems = sems
        self.marks = {}
        self.counting = False

    def sec(self, engine_name, fn):
        if self.counting:
            fn(_CountEngine())
        else:
            getattr(self.block, engine_name)(fn)

    def wait(self, eng, sem, key):
        if not self.counting:
            nm, val = self.marks[key]
            assert nm == sem.name, ("wait sem mismatch", key, nm, sem.name)
            eng.wait_ge(sem.h, val)

    def mark(self, key, sem):
        if self.counting:
            assert key not in self.marks, ("duplicate mark", key)
            self.marks[key] = (sem.name, sem.n)
        else:
            assert self.marks.get(key) == (sem.name, sem.n), \
                (key, self.marks.get(key), sem.n)

    def run(self, fn):
        snap = [(s, s.n) for s in self.sems]
        self.counting = True
        fn()
        counted = [(s, s.n) for s in self.sems]
        for s, n in snap:
            s.n = n
        self.counting = False
        fn()
        for (s, n) in counted:
            assert s.n == n, (s, s.n, n)


def kbs_of(g):
    return list(range(4 * g, 32))


def _prev_hc(t_i, e, hc, back):
    gi = (t_i * 10 + e) * NHC + hc - back
    if gi < 0:
        return None
    q, hc = divmod(gi, NHC)
    t_i, e = divmod(q, 10)
    return (t_i, e, hc)


def build_nc(stop_after="p3", debug_taps=False):
    nc = bass.Bass("TRN2", target_bir_lowering=False)
    scratch_kind = "ExternalOutput" if debug_taps else "Internal"

    xin = nc.dram_tensor("xin", [2 * TOWN, D], F32, kind="ExternalInput").ap()
    kmask = nc.dram_tensor("kmask", [128, 32], F32, kind="ExternalInput").ap()
    wq = nc.dram_tensor("wq", [D, D], F32, kind="ExternalInput").ap()
    wk = nc.dram_tensor("wk", [D, D], F32, kind="ExternalInput").ap()
    wv = nc.dram_tensor("wv", [D, D], F32, kind="ExternalInput").ap()
    wo = nc.dram_tensor("wo", [D, D], F32, kind="ExternalInput").ap()
    gwT = nc.dram_tensor("gwT", [D, E], F32, kind="ExternalInput").ap()
    wg = nc.dram_tensor("wg", [E, 128, NHC * D], F32, kind="ExternalInput").ap()
    wu = nc.dram_tensor("wu", [E, 128, NHC * D], F32, kind="ExternalInput").ap()
    wd = nc.dram_tensor("wd", [E, HME, D], F32, kind="ExternalInput").ap()
    sgw = nc.dram_tensor("sgw", [D, 1], F32, kind="ExternalInput").ap()
    shg = nc.dram_tensor("shg", [128, NSC * D], F32, kind="ExternalInput").ap()
    shu = nc.dram_tensor("shu", [128, NSC * D], F32, kind="ExternalInput").ap()
    shd = nc.dram_tensor("shd", [HSH, D], F32, kind="ExternalInput").ap()
    out = nc.dram_tensor("out", [TOWN, D], F32, kind="ExternalOutput").ap()

    dump = nc.dram_tensor("dump", [128, HEADS * 512], F32,
                          kind="ExternalOutput" if debug_taps else "Internal").ap()
    xfm_s = nc.dram_tensor("xfm_s", [NCH, 128, 2 * TOWN], F32, kind=scratch_kind).ap()
    q_s = nc.dram_tensor("q_s", [HEADS, 128, TOWN], F32, kind=scratch_kind).ap()
    k_s = nc.dram_tensor("k_s", [HEADS, 128, 2 * TOWN], F32, kind=scratch_kind).ap()
    v_s = nc.dram_tensor("v_s", [HEADS, 128, 4096], F32, kind=scratch_kind).ap()
    x1_s = nc.dram_tensor("x1_s", [NCH, 128, TOWN], F32, kind=scratch_kind).ap()
    x1c_s = nc.dram_tensor("x1c_s", [TOWN + 16, XROW], F32,
                           kind=scratch_kind).ap()
    moe_s = nc.dram_tensor("moe_s", [TOWN + 16, 2048], F32,
                           kind=scratch_kind).ap()
    cfm_s = nc.dram_tensor("cfm_s", [E, TOWN], F32, kind=scratch_kind).ap()

    es = contextlib.ExitStack()
    with es:
        PS = [es.enter_context(nc.psum_tensor(f"ps{i}", [128, 512], F32))
              for i in range(8)]

        sT = Sem(nc, es, "sT")
        sV = Sem(nc, es, "sV")
        sA = Sem(nc, es, "sA")
        sZ = Sem(nc, es, "sZ")
        sLc = Sem(nc, es, "sLc")
        slx = [Sem(nc, es, f"slx{i}") for i in range(2)]
        stv0 = [Sem(nc, es, f"stv0{i}") for i in range(2)]
        slw = Sem(nc, es, "slw")
        slt = [Sem(nc, es, f"slt{i}") for i in range(2)]
        sqk = [Sem(nc, es, f"sqk{i}") for i in range(2)]
        spv = [Sem(nc, es, f"spv{i}") for i in range(2)]
        slk = [Sem(nc, es, f"slk{i}") for i in range(2)]
        slx0 = Sem(nc, es, "semx0")
        slwo = Sem(nc, es, "slwo")
        sx1 = Sem(nc, es, "sx1")
        scf = Sem(nc, es, "scf")
        slx1 = Sem(nc, es, "semx1ld")
        slgu = [Sem(nc, es, f"slgu{i}") for i in range(2)]
        sld = [Sem(nc, es, f"sld{i}") for i in range(2)]
        sor = Sem(nc, es, "sor")
        sgt = Sem(nc, es, "sgt")
        ssc = Sem(nc, es, "ssc")
        stm = Sem(nc, es, "stm")
        slmo = Sem(nc, es, "slmo")
        dma_sems = [sLc] + slx + stv0 + [slw] + slt + sqk + \
            spv + slk + [slx0, slwo, sx1, scf, slx1] + slgu + sld + \
            [sor, sgt, ssc, stm, slmo]
        sems = [sT, sV, sA, sZ] + dma_sems

        def sb(name, shape, dt=F32):
            return es.enter_context(nc.sbuf_tensor(name, shape, dt))

        ones_col = sb("ones_col", [128, 1])
        ones_row = sb("ones_row", [1, 128])
        ident = sb("ident", [128, 128])

        kmask_sb = sb("kmask_sb", [128, 32])
        gwT_sb = sb("gwT_sb", [128, NCH * E])
        sgw_sb = sb("sgw_sb", [128, NCH])

        z1 = sb("z1", [1, 512])
        zflag = sb("zflag", [1, 512])
        vsum_sb = sb("vsum_sb", [1, 128])

        rz = sb("rz", [1, 512])


        lg = sb("lg", [128, 32])
        gnm = sb("gnm", [128, 1])
        grs = sb("grs", [128, 1])
        gm1 = sb("gm1", [128, 8])
        gm2 = sb("gm2", [128, 8])
        gden = sb("gden", [128, 1])
        gpr = sb("gpr", [128, 32])
        gp1 = sb("gp1", [128, 8])
        gp2 = sb("gp2", [128, 8])
        gcomb = sb("gcomb", [128, 32])

        pes = contextlib.ExitStack()

        def psb(name, shape, dt=F32):
            return pes.enter_context(nc.sbuf_tensor(name, shape, dt))

        with nc.Block() as block:
            env = Env(block, sems)
            M = env.mark
            W = env.wait

            def ph_init():
                def _(g):
                    g.memset(ones_col[:], 1.0)
                    g.memset(ones_row[:], 1.0)
                    g.memset(ident[:], 1.0)
                    ins = g.memset(gm1[:], -1.0)
                    sZ.inc(ins)
                    ins = g.memset(gm2[:], -1.0)
                    sZ.inc(ins)
                    M("init_ms", sZ)
                    W(g, sZ, "init_ms")
                    g.wait_ge(sZ.h, sZ.n)
                    ins = g.affine_select(ident[:], ident[:], [[-1, 128]],
                                          ALU.is_equal, 0.0, base=0,
                                          channel_multiplier=1)
                    sZ.inc(ins)
                    M("init", sZ)
                    g.wait_ge(sZ.h, sZ.n)
                env.sec("gpsimd", _)

                def _(s):
                    ins = s.dma_start(out=kmask_sb[:], in_=kmask)
                    sLc.inc(ins, 16)
                    for c in range(NCH):
                        ins = s.dma_start(out=gwT_sb[:, c * E:(c + 1) * E],
                                          in_=gwT[c * 128:(c + 1) * 128, :])
                        sLc.inc(ins, 16)
                    ins = s.dma_start(
                        out=sgw_sb[:],
                        in_=sgw.rearrange("(c p) one -> p (c one)", p=128),
                        allow_slow_non_contiguous=True)
                    sLc.inc(ins, 16)
                    M("init_ld", sLc)
                env.sec("sync", _)
            env.run(ph_init)

            xrow = [psb(f"xrow{i}", [128, D]) for i in range(2)]
            xev = [psb(f"xev{i}", [128, 512]) for i in range(2)]

            def ph_p0():
                def _(s):
                    for tb in range(32):
                        if tb >= 2:
                            W(s, sT, ("p0tr", tb - 2, 3))
                        ins = s.dma_start(out=xrow[tb % 2][:],
                                          in_=xin[tb * 128:(tb + 1) * 128, :])
                        slx[tb % 2].inc(ins, 16)
                        M(("p0ld", tb), slx[tb % 2])
                env.sec("sync", _)

                def _(t):
                    W(t, sZ, "init")
                    for tb in range(32):
                        W(t, slx[tb % 2], ("p0ld", tb))
                        for quad in range(4):
                            i = tb * 4 + quad
                            bank = PS[i % 2]
                            if i >= 2:
                                W(t, sV, ("p0ev", i - 2))
                            for j in range(4):
                                c = quad * 4 + j
                                ins = t.transpose(
                                    bank[:, j * 128:(j + 1) * 128],
                                    xrow[tb % 2][:, c * 128:(c + 1) * 128],
                                    ident[:])
                            sT.inc(ins)
                            M(("p0tr", tb, quad), sT)
                env.sec("tensor", _)

                def _(v):
                    v.drain()
                    for tb in range(32):
                        for quad in range(4):
                            i = tb * 4 + quad
                            W(v, sT, ("p0tr", tb, quad))
                            if i >= 2:
                                W(v, stv0[i % 2], ("p0st", i - 2))
                            ins = v.tensor_copy(xev[i % 2][:], PS[i % 2][:])
                            sV.inc(ins)
                            M(("p0ev", i), sV)
                env.sec("vector", _)

                def _(g):
                    for tb in range(32):
                        for quad in range(4):
                            i = tb * 4 + quad
                            W(g, sV, ("p0ev", i))
                            for j in range(4):
                                c = quad * 4 + j
                                ins = g.dma_start(
                                    out=xfm_s[c, :, tb * 128:(tb + 1) * 128],
                                    in_=xev[i % 2][:, j * 128:(j + 1) * 128])
                                stv0[i % 2].inc(ins, 16)
                            M(("p0st", i), stv0[i % 2])
                env.sec("gpsimd", _)
            env.run(ph_p0)
            pes.close()

            if stop_after == "p0":
                _finish(block, dma_sems)
                return nc

            w_half = psb("w_half", [128, NCH * 1024])
            xt = [psb(f"xt{i}", [128, NCH * 512]) for i in range(2)]
            pev = [psb(f"pev{i}", [128, 512]) for i in range(2)]

            def ph_p1():
                first = [True]
                for wsrc, dst_s, ntt, nm in [(wq, q_s, 4, "q"), (wk, k_s, 8, "k")]:
                    for half in range(2):
                        tag = (nm, half)
                        prev_tag = {"q0": None, "q1": ("q", 0),
                                    "k0": ("q", 1), "k1": ("k", 0)}[nm + str(half)]

                        def _(s, wsrc=wsrc, ntt=ntt, tag=tag, prev_tag=prev_tag,
                              half=half, first=first):
                            if tag == ("q", 0):
                                W(s, stv0[0], ("p0st", 126))
                                W(s, stv0[1], ("p0st", 127))
                            if prev_tag is not None:
                                W(s, sT, ("p1wfree", prev_tag))
                            for c in range(NCH):
                                ins = s.dma_start(
                                    out=w_half[:, c * 1024:(c + 1) * 1024],
                                    in_=wsrc[c * 128:(c + 1) * 128,
                                             half * 1024:(half + 1) * 1024])
                                slw.inc(ins, 16)
                            M(("p1wld", tag), slw)
                            for tt in range(ntt):
                                if tt >= 2:
                                    W(s, sT, ("p1mm", tag, tt - 2, 7))
                                for c in range(NCH):
                                    ins = s.dma_start(
                                        out=xt[tt % 2][:, c * 512:(c + 1) * 512],
                                        in_=xfm_s[c, :, tt * 512:(tt + 1) * 512])
                                    slt[tt % 2].inc(ins, 16)
                                M(("p1xld", tag, tt), slt[tt % 2])
                        env.sec("sync", _)

                        prev_ev = {("q", 0): ("p0ev", 127),
                                   ("q", 1): ("p1ev", ("q", 0), 31),
                                   ("k", 0): ("p1ev", ("q", 1), 31),
                                   ("k", 1): ("p1ev", ("k", 0), 63)}[tag]

                        def _(t, ntt=ntt, tag=tag, prev_ev=prev_ev):
                            W(t, slw, ("p1wld", tag))
                            for tt in range(ntt):
                                W(t, slt[tt % 2], ("p1xld", tag, tt))
                                for f in range(8):
                                    i = tt * 8 + f
                                    bank = PS[i % 2]
                                    if i >= 2:
                                        W(t, sV, ("p1ev", tag, i - 2))
                                    elif i == 0:
                                        W(t, sV, prev_ev)
                                    for c in range(NCH):
                                        ins = t.matmul(
                                            bank[:],
                                            w_half[:, c * 1024 + f * 128:
                                                   c * 1024 + (f + 1) * 128],
                                            xt[tt % 2][:, c * 512:(c + 1) * 512],
                                            start=(c == 0), stop=(c == NCH - 1))
                                    sT.inc(ins)
                                    M(("p1mm", tag, tt, f), sT)
                            M(("p1wfree", tag), sT)
                        env.sec("tensor", _)

        
                        prev_st = {("q", 0): None,
                                   ("q", 1): (("q", 0), 31),
                                   ("k", 0): (("q", 1), 31),
                                   ("k", 1): (("k", 0), 63)}[tag]

                        def _(v, ntt=ntt, tag=tag, prev_st=prev_st):
                            v.drain()
                            for tt in range(ntt):
                                for f in range(8):
                                    i = tt * 8 + f
                                    W(v, sT, ("p1mm", tag, tt, f))
                                    if i >= 2:
                                        W(v, sqk[i % 2], ("p1st", tag, i - 2))
                                    elif prev_st is not None:
                                        pt, pl = prev_st
                                        W(v, sqk[i % 2],
                                          ("p1st", pt, pl - 1 + i))
                                    ins = v.tensor_copy(pev[i % 2][:], PS[i % 2][:])
                                    sV.inc(ins)
                                    M(("p1ev", tag, i), sV)
                        env.sec("vector", _)

                        def _(g, ntt=ntt, tag=tag, dst_s=dst_s, half=half):
                            for tt in range(ntt):
                                for f in range(8):
                                    i = tt * 8 + f
                                    W(g, sV, ("p1ev", tag, i))
                                    ins = g.dma_start(
                                        out=dst_s[half * 8 + f, :,
                                                  tt * 512:(tt + 1) * 512],
                                        in_=pev[i % 2][:])
                                    sqk[i % 2].inc(ins, 16)
                                    M(("p1st", tag, i), sqk[i % 2])
                        env.sec("gpsimd", _)

            env.run(ph_p1)
            pes.close()

            wv_sb = psb("wv_sb", [128, NCH * 1024])
            xb = [psb(f"xb{i}", [128, NCH * 512]) for i in range(2)]
            pev = [psb(f"pev{i}b", [128, 512]) for i in range(2)]

            def ph_p1v():
                def _(s):
                    W(s, sT, ("p1wfree", ("k", 1)))
                    W(s, sqk[0], ("p1st", ("k", 1), 62))
                    W(s, sqk[1], ("p1st", ("k", 1), 63))
                    for half in range(2):
                        if half >= 1:
                            W(s, sT, ("p1vmm", half - 1, 7, 7))
                        for c in range(NCH):
                            ins = s.dma_start(
                                out=wv_sb[:, c * 1024:(c + 1) * 1024],
                                in_=wv[c * 128:(c + 1) * 128,
                                       half * 1024:(half + 1) * 1024])
                            slw.inc(ins, 16)
                        M(("p1vw", half), slw)
                        for tbg in range(8):
                            i = half * 8 + tbg
                            if i >= 2:
                                W(s, sT, ("p1vmm", (i - 2) // 8, (i - 2) % 8, 7))
                            for c in range(NCH):
                                ins = s.dma_start(
                                    out=xb[i % 2][:, c * 512:(c + 1) * 512],
                                    in_=xfm_s[c, :, tbg * 512:(tbg + 1) * 512])
                                slt[i % 2].inc(ins, 16)
                            M(("p1vx", half, tbg), slt[i % 2])
                env.sec("sync", _)

                def _(t):
                    for half in range(2):
                        W(t, slw, ("p1vw", half))
                        for tbg in range(8):
                            ig = half * 8 + tbg
                            W(t, slt[ig % 2], ("p1vx", half, tbg))
                            for sub in range(8):
                                tb4, dl = divmod(sub, 2)
                                i = ig * 8 + sub
                                bank = PS[i % 2]
                                if i >= 2:
                                    W(t, sV, ("p1vev", i - 2))
                                elif i == 0:
                                    W(t, sV, ("p1ev", ("k", 1), 63))
                                for c in range(NCH):
                                    ins = t.matmul(
                                        bank[:],
                                        xb[ig % 2][:, c * 512 + tb4 * 128:
                                                   c * 512 + (tb4 + 1) * 128],
                                        wv_sb[:, c * 1024 + dl * 512:
                                              c * 1024 + (dl + 1) * 512],
                                        start=(c == 0), stop=(c == NCH - 1))
                                sT.inc(ins)
                                M(("p1vmm", half, tbg, sub), sT)
                env.sec("tensor", _)

                def _(v):
                    v.drain()
                    for half in range(2):
                        for tbg in range(8):
                            for sub in range(8):
                                i = (half * 8 + tbg) * 8 + sub
                                W(v, sT, ("p1vmm", half, tbg, sub))
                                if i >= 2:
                                    W(v, spv[i % 2], ("p1vst", i - 2))
                                ins = v.tensor_copy(pev[i % 2][:], PS[i % 2][:])
                                sV.inc(ins)
                                M(("p1vev", i), sV)
                env.sec("vector", _)

                def _(g):
                    for half in range(2):
                        for tbg in range(8):
                            for sub in range(8):
                                tb4, dl = divmod(sub, 2)
                                tb = tbg * 4 + tb4
                                i = (half * 8 + tbg) * 8 + sub
                                W(g, sV, ("p1vev", i))
                                for hh in range(4):
                                    h = (half * 2 + dl) * 4 + hh
                                    ins = g.dma_start(
                                        out=v_s[h, :, tb * 128:(tb + 1) * 128],
                                        in_=pev[i % 2][:,
                                                       hh * 128:(hh + 1) * 128])
                                    spv[i % 2].inc(ins, 16)
                                M(("p1vst", i), spv[i % 2])
                env.sec("gpsimd", _)
            env.run(ph_p1v)
            pes.close()

            if stop_after == "p1":
                _finish(block, dma_sems)
                return nc

            st_a = psb("st_a", [1, 512])
            st_b = psb("st_b", [1, 512])
            st_c = psb("st_c", [1, 512])
            st_d = psb("st_d", [1, 512])
            st_e = psb("st_e", [1, 512])
            sqt = [psb(f"sqt{i}", [128, 512]) for i in range(2)]
            mb_b = psb("mb_b", [128, 512])
            yb_b = psb("yb_b", [128, 512])
            kh = [psb(f"kh{i}", [128, 4096]) for i in range(2)]
            vh = [psb(f"vh{i}", [128, 4096]) for i in range(2)]
            qh = [psb(f"qh{i}", [128, 512]) for i in range(2)]
            ptile = [psb(f"pt{i}", [128, 512]) for i in range(3)]
            stile = psb("stile", [128, 512])
            rzb = psb("rzb", [128, 512])
            o_all = psb("o_all", [128, HEADS * 512])
            x0r = psb("x0r", [128, NCH * 512])
            g8 = psb("g8", [8, 512])
            cf = psb("cf", [8, 512])
            x1tmr = [psb("x1tmr0", [128, 2048])] * 2
            dmask = psb("dmask", [128, 4 * 512])
            x1g = o_all
            wo_q = psb("wo_q", [128, NCH * 256])

            def ph_p2():
                def _(g):
                    ins = g.memset(dmask[:], 0.0)
                    sZ.inc(ins)
                    g.wait_ge(sZ.h, sZ.n)
                    for m in range(4):
                        ins = g.affine_select(
                            dmask[:, m * 512:(m + 1) * 512],
                            dmask[:, m * 512:(m + 1) * 512],
                            [[-1, 512]], ALU.is_ge, -FMAX,
                            base=128 * m - 1, channel_multiplier=1)
                        sZ.inc(ins)
                    M("p2dm", sZ)
                    g.wait_ge(sZ.h, sZ.n)
                env.sec("gpsimd", _)

                def prev_head(g, h):
                    k = g * HEADS + h - 1
                    return None if k < 0 else divmod(k, HEADS)

                ngrp = 1 if stop_after in ("p2heads", "p2g1") else NG
                for g in range(ngrp):
                    kbl = kbs_of(g)
                    ext = len(kbl)
                    diag = set(range(4 * g, 4 * g + 4))
                    diag_idx = [0, 1, 2, 3]

                    def _(s, g=g):
                        if g == 0:
                            W(s, spv[0], ("p1vst", 126))
                            W(s, spv[1], ("p1vst", 127))
                            W(s, sqk[0], ("p1st", ("k", 1), 62))
                            W(s, sqk[1], ("p1st", ("k", 1), 63))
                        for h in range(HEADS):
                            k_glob = g * HEADS + h
                            if k_glob >= 2:
                                pg, ph_ = divmod(k_glob - 2, HEADS)
                                W(s, sT, ("p2pe", pg, ph_))
                            slot = h % 2
                            ins = s.dma_start(
                                out=kh[slot][:, 0:(32 - 4 * g) * 128],
                                in_=k_s[h, :, 4 * g * 128:4096])
                            slk[slot].inc(ins, 16)
                            ins = s.dma_start(out=vh[slot][:],
                                              in_=v_s[h, :, :])
                            slk[slot].inc(ins, 16)
                            ins = s.dma_start(
                                out=qh[slot][:],
                                in_=q_s[h, :, g * 512:(g + 1) * 512])
                            slk[slot].inc(ins, 16)
                            M(("p2ld", g, h), slk[slot])
                        if g >= 1:
                            W(s, sV, ("p2x1", g - 1))
                        for c in range(NCH):
                            ins = s.dma_start(
                                out=x0r[:, c * 512:(c + 1) * 512],
                                in_=xfm_s[c, :, g * 512:(g + 1) * 512])
                            slx0.inc(ins, 16)
                        M(("p2x0", g), slx0)
                    env.sec("sync", _)

                    for h in range(HEADS):
                        slot = h % 2
                        ph = prev_head(g, h)

                        def _(t, g=g, h=h, slot=slot, ext=ext, ph=ph,
                              kbl=kbl):
                            W(t, slk[slot], ("p2ld", g, h))
                            if g == 0 and h == 0:
                                W(t, sV, ("p1vev", 127))
                            if g == 3:
                                if h == 0:
                                    W(t, sV, ("p2tmev", 2, 3))
                                else:
                                    W(t, sV, ("p2zfx", g, h - 1))
                                for kb in range(32):
                                    ins = t.matmul(
                                        PS[4][0:1, 0:128], ones_col[:],
                                        vh[slot][:, kb * 128:(kb + 1) * 128],
                                        start=(kb == 0), stop=(kb == 31))
                                sT.inc(ins)
                                M(("p2vsm", g, h), sT)
                            if ph is not None:
                                W(t, sV, ("p2osc",) + ph)
                            def zpv(j):
                                ins = t.matmul(
                                    PS[2][:],
                                    vh[slot][:, kbl[j] * 128:
                                             (kbl[j] + 1) * 128],
                                    ptile[j % 3][:],
                                    start=(j == 0),
                                    stop=(j == ext - 1 and g != 3))
                                sT.inc(ins)
                                M(("p2pv", g, h, j), sT)
                                ins = t.matmul(PS[5][0:1, :], ones_col[:],
                                               ptile[j % 3][:],
                                               start=(j == 0),
                                               stop=(j == ext - 1))
                                sT.inc(ins)
                                M(("p2zt", g, h, j), sT)

                            for i in range(ext):
                                if i >= 2:
                                    W(t, sA, ("p2exp", g, h, i - 2))
                                ins = t.matmul(PS[i % 2][:],
                                               kh[slot][:, i * 128:(i + 1) * 128],
                                               qh[slot][:],
                                               start=True, stop=True)
                                sT.inc(ins)
                                M(("p2smm", g, h, i), sT)
                                if i >= 1:
                                    W(t, sA, ("p2exp", g, h, i - 1))
                                    zpv(i - 1)
                            W(t, sA, ("p2exp", g, h, ext - 1))
                            zpv(ext - 1)
                            if g == 3:
                                W(t, sV, ("p2zfx", g, h))
                                ins = t.matmul(PS[2][:], vsum_sb[:], zflag[:],
                                               start=False, stop=True)
                                sT.inc(ins)
                                M(("p2corr", g, h), sT)
                            M(("p2pe", g, h), sT)
                            W(t, sV, ("p2rz", g, h))
                            if ph is not None:
                                W(t, sV, ("p2rzb",) + ph)
                            elif g == 0 and h == 0:
                                pass
                            if h == 0 and g >= 1:
                                W(t, sV, ("p2mbb", g - 1))
                            ins = t.matmul(PS[3][:], ones_row[:], rz[:],
                                           start=True, stop=True)
                            sT.inc(ins)
                            M(("p2bc", g, h), sT)
                        env.sec("tensor", _)

                        def _(v, g=g, h=h, ext=ext, diag_idx=diag_idx, ph=ph):
                            v.drain()
                            if g == 0 and h == 0:
                                W(v, sZ, "p2dm")
                            prev_d = None
                            for i in diag_idx:
                                W(v, sT, ("p2smm", g, h, i))
                                if prev_d is not None:
                                    W(v, sA, ("p2exp", g, h, prev_d))
                                elif ph is not None:
                                    W(v, sA, ("p2exp", ph[0], ph[1],
                                              4 * ph[0] + 3))
                                m = kbs_of(g)[i] - 4 * g
                                ins = v.scalar_tensor_tensor(
                                    stile[:], PS[i % 2][:], SCALE,
                                    dmask[:, m * 512:(m + 1) * 512],
                                    ALU.mult, ALU.add)
                                sV.inc(ins)
                                M(("p2stt", g, h, i), sV)
                                prev_d = i
                            if ph is not None:
                                W(v, sT, ("p2bc",) + ph)
                            W(v, sT, ("p2zt", g, h, ext - 1))
                            if g == 3:
                                if h >= 1:
                                    W(v, sT, ("p2corr", g, h - 1))
                                v.drain()
                                v.tensor_single_scalar(zflag[:], PS[5][0:1, :],
                                                       0.0, ALU.is_equal)
                                v.drain()
                                v.scalar_tensor_tensor(z1[:], zflag[:], 4096.0,
                                                       PS[5][0:1, :],
                                                       ALU.mult, ALU.add)
                                W(v, sT, ("p2vsm", g, h))
                                v.drain()
                                ins = v.tensor_copy(vsum_sb[:],
                                                    PS[4][0:1, 0:128])
                                sV.inc(ins)
                                M(("p2zfx", g, h), sV)
                                v.drain()
                                ins = v.reciprocal(rz[:], z1[:])
                            else:
                                v.drain()
                                ins = v.reciprocal(rz[:], PS[5][0:1, :])
                            sV.inc(ins)
                            M(("p2rz", g, h), sV)
                            W(v, sT, ("p2bc", g, h))
                            ins = v.tensor_copy(rzb[:], PS[3][:])
                            sV.inc(ins)
                            M(("p2rzb", g, h), sV)
                            W(v, sT, ("p2corr", g, h) if g == 3
                              else ("p2pv", g, h, ext - 1))
                            if g >= 1 and h == 0:
                                W(v, sT, ("p2wolast", g - 1))
                                W(v, sT, ("p2glg", g - 1))
                                W(v, sx1, ("p2x1st", g - 1))
                                W(v, sT, ("p2tmtr", g - 1, 3))
                            v.drain()
                            ins = v.tensor_tensor(
                                o_all[:, h * 512:(h + 1) * 512],
                                PS[2][:], rzb[:], ALU.mult)
                            sV.inc(ins)
                            M(("p2osc", g, h), sV)
                        env.sec("vector", _)

                        def _(a, g=g, h=h, ext=ext, diag=diag, ph=ph):
                            if g == 0 and h == 0:
                                W(a, sLc, "init_ld")
                            if ph is not None:
                                W(a, sT, ("p2pe",) + ph)
                            for i, kb in enumerate(kbs_of(g)):
                                if i >= 3:
                                    W(a, sT, ("p2zt", g, h, i - 3))
                                if kb in diag:
                                    W(a, sV, ("p2stt", g, h, i))
                                    ins = a.activation(ptile[i % 3][:], stile[:],
                                                       AF.Exp, bias=0.0,
                                                       scale=1.0)
                                else:
                                    W(a, sT, ("p2smm", g, h, i))
                                    ins = a.activation(
                                        ptile[i % 3][:], PS[i % 2][:], AF.Exp,
                                        bias=kmask_sb[:, kb:kb + 1], scale=SCALE)
                                sA.inc(ins)
                                M(("p2exp", g, h, i), sA)
                        env.sec("scalar", _)


                    if stop_after == "p2heads":
                        def _(gp, g=g):
                            for h in range(HEADS):
                                W(gp, sV, ("p2osc", g, h))
                                ins = gp.dma_start(
                                    out=dump[:, h * 512:(h + 1) * 512],
                                    in_=o_all[:, h * 512:(h + 1) * 512])
                                sx1.inc(ins, 16)
                            M(("p2dump", g), sx1)
                        env.sec("gpsimd", _)
                        break

                    def _(s, g=g):
                        for qq in range(8):
                            if qq == 0:
                                if g >= 1:
                                    W(s, sT, ("p2wolast", g - 1))
                            else:
                                W(s, sT, ("p2womm", g, 2 * qq - 1))
                            for c in range(NCH):
                                ins = s.dma_start(
                                    out=wo_q[:, c * 256:(c + 1) * 256],
                                    in_=wo[c * 128:(c + 1) * 128,
                                           qq * 256:(qq + 1) * 256])
                                slwo.inc(ins, 16)
                            M(("p2wold", g, qq), slwo)
                    env.sec("sync", _)

                    def stats(t, g, d):
                        W(t, sV, ("p2r", g, d))
                        W(t, sA, ("p2sq", g, d))
                        t.matmul(PS[6][0:1, :], ones_col[:],
                                 x0r[:, d * 512:(d + 1) * 512],
                                 start=(d == 0), stop=(d == 15))
                        ins = t.matmul(PS[7][0:1, :], ones_col[:],
                                       sqt[d % 2][:],
                                       start=(d == 0), stop=(d == 15))
                        sT.inc(ins)
                        M(("p2stats", g, d), sT)

                    def _(t, g=g):
                        W(t, sV, ("p2osc", g, HEADS - 1))
                        if g == 0:
                            W(t, sLc, "init_ld")
                        for qq in range(8):
                            W(t, slwo, ("p2wold", g, qq))
                            for dd in range(2):
                                d = qq * 2 + dd
                                bank = PS[4 + d % 2]
                                if d >= 2:
                                    W(t, sV, ("p2r", g, d - 2))
                                for h2 in range(HEADS):
                                    ins = t.matmul(
                                        bank[:],
                                        wo_q[:, h2 * 256 + dd * 128:
                                             h2 * 256 + (dd + 1) * 128],
                                        o_all[:, h2 * 512:(h2 + 1) * 512],
                                        start=(h2 == 0), stop=(h2 == HEADS - 1))
                                sT.inc(ins)
                                M(("p2womm", g, d), sT)
                                if d >= 1:
                                    stats(t, g, d - 1)
                        stats(t, g, 15)
                        M(("p2wolast", g), sT)
                        W(t, sV, ("p2lnt", g))
                        ins = t.matmul(PS[3][:], ones_row[:], st_a[:],
                                       start=True, stop=True)
                        sT.inc(ins)
                        M(("p2bcm", g), sT)
                        ins = t.matmul(PS[4][:], ones_row[:], st_d[:],
                                       start=True, stop=True)
                        sT.inc(ins)
                        M(("p2bcy", g), sT)
                        W(t, sV, ("p2x1", g))
                        for c in range(NCH):
                            ins = t.matmul(PS[6][0:8, :],
                                           gwT_sb[:, c * E:(c + 1) * E],
                                           x1g[:, c * 512:(c + 1) * 512],
                                           start=(c == 0), stop=(c == NCH - 1))
                        sT.inc(ins)
                        M(("p2glg", g), sT)
                        W(t, sV, ("p2g8", g))
                        for tt in range(4):
                            ins = t.transpose(PS[7][:, tt * 8:(tt + 1) * 8],
                                              g8[0:8, tt * 128:(tt + 1) * 128],
                                              ident[0:8, 0:8])
                        sT.inc(ins)
                        M(("p2gtr", g), sT)
                        for tt in range(4):
                            W(t, sV, ("p2comb", g, tt))
                            ins = t.transpose(PS[6][0:8,
                                                    tt * 128:(tt + 1) * 128],
                                              gcomb[:, tt * 8:(tt + 1) * 8],
                                              ident[:])
                        sT.inc(ins)
                        M(("p2ctr", g), sT)
                        for tb in range(4):
                            if tb == 0:
                                W(t, sV, ("p2cf", g))
                                if g >= 1:
                                    W(t, sV, ("p2tmev", g - 1, 3))
                            else:
                                W(t, sV, ("p2tmev", g, tb - 1))
                            for c in range(NCH):
                                ins = t.transpose(
                                    PS[4 + c // 4][:, (c % 4) * 128:
                                                   (c % 4 + 1) * 128],
                                    x1g[:, c * 512 + tb * 128:
                                        c * 512 + (tb + 1) * 128],
                                    ident[:])
                                if c % 4 == 3:
                                    sT.inc(ins)
                            M(("p2tmtr", g, tb), sT)
                    env.sec("tensor", _)

                    def _(a, g=g):
                        W(a, slx0, ("p2x0", g))
                        if g >= 1:
                            W(a, sT, ("p2stats", g - 1, 15))
                        for d in range(16):
                            W(a, sV, ("p2r", g, d))
                            if d >= 2:
                                W(a, sT, ("p2stats", g, d - 2))
                            ins = a.activation(sqt[d % 2][:],
                                               x0r[:, d * 512:(d + 1) * 512],
                                               AF.Square, bias=0.0, scale=1.0)
                            sA.inc(ins)
                            M(("p2sq", g, d), sA)
                        W(a, sV, ("p2veps", g))
                        ins = a.activation(st_c[:], st_b[:], AF.Sqrt,
                                           bias=0.0, scale=1.0)
                        sA.inc(ins)
                        M(("p2sqrt", g), sA)
                        for tt in range(4):
                            W(a, sV, ("p2gnm", g, tt))
                            if tt >= 1:
                                W(a, sV, ("p2comb", g, tt - 1))
                            ins = a.activation(gpr[:, tt * 8:(tt + 1) * 8],
                                               lg[:, tt * 8:(tt + 1) * 8],
                                               AF.Exp, bias=gnm[:], scale=1.0,
                                               accum_out=grs[:])
                            sA.inc(ins)
                            M(("p2gexp", g, tt), sA)
                    env.sec("scalar", _)

                    def _(v, g=g):
                        v.drain()
                        W(v, slx0, ("p2x0", g))
                        for d in range(16):
                            W(v, sT, ("p2womm", g, d))
                            ins = v.tensor_tensor(
                                x0r[:, d * 512:(d + 1) * 512],
                                PS[4 + d % 2][:],
                                x0r[:, d * 512:(d + 1) * 512], ALU.add)
                            sV.inc(ins)
                            M(("p2r", g, d), sV)
                        W(v, sT, ("p2stats", g, 15))
                        v.drain()
                        v.tensor_scalar_mul(st_a[:], PS[6][0:1, :], INVD)
                        ins = v.tensor_scalar_mul(st_b[:], PS[7][0:1, :], INVD)
                        sV.inc(ins)
                        v.drain()
                        v.tensor_mul(st_c[:], st_a[:], st_a[:])
                        v.drain()
                        v.tensor_sub(st_b[:], st_b[:], st_c[:])
                        v.drain()
                        ins = v.tensor_scalar_add(st_b[:], st_b[:], EPS)
                        sV.inc(ins)
                        M(("p2veps", g), sV)
                        W(v, sA, ("p2sqrt", g))
                        ins = v.reciprocal(st_d[:], st_c[:])
                        sV.inc(ins)
                        for _it in range(2):
                            v.drain()
                            v.tensor_mul(st_e[:], st_b[:], st_d[:])
                            v.drain()
                            v.tensor_mul(st_e[:], st_e[:], st_d[:])
                            v.drain()
                            v.tensor_scalar(st_e[:], st_e[:], -0.5, 1.5,
                                            ALU.mult, ALU.add)
                            v.drain()
                            v.tensor_mul(st_d[:], st_d[:], st_e[:])
                        ins = v.engine_nop()
                        sV.inc(ins)
                        M(("p2lnt", g), sV)
                        W(v, sT, ("p2bcm", g))
                        v.drain()
                        ins = v.tensor_copy(mb_b[:], PS[3][:])
                        sV.inc(ins)
                        M(("p2mbb", g), sV)
                        W(v, sT, ("p2bcy", g))
                        ins = v.tensor_copy(yb_b[:], PS[4][:])
                        sV.inc(ins)
                        M(("p2yb2", g), sV)
                        if g >= 1:
                            W(v, sx1, ("p2x1st", g - 1))
                        for d in range(16):
                            sl = slice(d * 512, (d + 1) * 512)
                            v.drain()
                            v.tensor_sub(x1g[:, sl], x0r[:, sl], mb_b[:])
                            v.drain()
                            ins = v.tensor_mul(x1g[:, sl], x1g[:, sl], yb_b[:])
                            sV.inc(ins)
                        M(("p2x1", g), sV)
                        W(v, sT, ("p2glg", g))
                        v.drain()
                        ins = v.tensor_copy(g8[:], PS[6][0:8, :])
                        sV.inc(ins)
                        M(("p2g8", g), sV)
                        W(v, sT, ("p2gtr", g))
                        ins = v.tensor_copy(lg[:], PS[7][:, 0:32])
                        sV.inc(ins)
                        for tt in range(4):
                            sl = slice(tt * 8, (tt + 1) * 8)
                            if tt == 0 and g >= 1:
                                W(v, stm, ("p2tmst", g - 1, 3))
                            if tt >= 1:
                                W(v, sA, ("p2gexp", g, tt - 1))
                            v.drain()
                            ins = v.reduce_max(gnm[:], lg[:, sl], axis=AX.X,
                                               negate=True)
                            sV.inc(ins)
                            M(("p2gnm", g, tt), sV)
                            W(v, sA, ("p2gexp", g, tt))
                            v.drain()
                            v.reciprocal(grs[:], grs[:])
                            v.drain()
                            v.tensor_scalar_mul(gpr[:, sl], gpr[:, sl], grs[:])
                            v.drain()
                            v.reduce_max(gm1[:, 0:1], gpr[:, sl], axis=AX.X)
                            v.drain()
                            v.match_replace(gp1[:], gm1[:], gpr[:, sl], 0.0)
                            v.drain()
                            v.reduce_max(gm2[:, 0:1], gp1[:], axis=AX.X)
                            v.drain()
                            v.match_replace(gp2[:], gm2[:], gp1[:], 0.0)
                            v.tensor_add(gden[:], gm1[:, 0:1], gm2[:, 0:1])
                            v.drain()
                            v.tensor_sub(gcomb[:, sl], gpr[:, sl], gp2[:])
                            v.reciprocal(gden[:], gden[:])
                            v.drain()
                            ins = v.tensor_scalar_mul(gcomb[:, sl],
                                                      gcomb[:, sl], gden[:])
                            sV.inc(ins)
                            M(("p2comb", g, tt), sV)
                        if g >= 1:
                            W(v, scf, ("p2cfst", g - 1))
                        W(v, sT, ("p2ctr", g))
                        v.drain()
                        ins = v.tensor_copy(cf[:], PS[6][0:8, :])
                        sV.inc(ins)
                        M(("p2cf", g), sV)
                        for tb in range(4):
                            W(v, sT, ("p2tmtr", g, tb))
                            if (g, tb) != (0, 0):
                                pb = (g, tb - 1) if tb else (g - 1, 3)
                                W(v, stm, ("p2tmst",) + pb)
                            v.drain()
                            for q4 in range(4):
                                ins = v.tensor_copy(
                                    x1tmr[tb % 2][:, q4 * 512:(q4 + 1) * 512],
                                    PS[4 + q4][:])
                            sV.inc(ins)
                            M(("p2tmev", g, tb), sV)
                    env.sec("vector", _)

                    def _(gp, g=g):
                        W(gp, sV, ("p2x1", g))
                        for c in range(NCH):
                            ins = gp.dma_start(
                                out=x1_s[c, :, g * 512:(g + 1) * 512],
                                in_=x1g[:, c * 512:(c + 1) * 512])
                            sx1.inc(ins, 16)
                        M(("p2x1st", g), sx1)
                        W(gp, sV, ("p2cf", g))
                        ins = gp.dma_start(out=cfm_s[:, g * 512:(g + 1) * 512],
                                           in_=cf[:])
                        scf.inc(ins, 16)
                        M(("p2cfst", g), scf)
                        for tb in range(4):
                            W(gp, sV, ("p2tmev", g, tb))
                            r0 = g * 512 + tb * 128
                            ins = gp.dma_start(
                                out=x1c_s[r0:r0 + 128, 0:2048],
                                in_=x1tmr[tb % 2][:])
                            stm.inc(ins, 16)
                            ins = gp.dma_start(
                                out=x1c_s[r0:r0 + 128, 2048:2056],
                                in_=gcomb[:, tb * 8:(tb + 1) * 8])
                            stm.inc(ins, 16)
                            M(("p2tmst", g, tb), stm)
                    env.sec("gpsimd", _)
            env.run(ph_p2)
            pes.close()

            if stop_after in ("p2heads", "p2g1"):
                _finish(block, dma_sems)
                return nc

            if stop_after == "p2":
                _finish(block, dma_sems)
                return nc

            iotaP = psb("iotaP", [16, 128])
            iota_row = psb("iota_row", [1, CAP])
            cnt2 = psb("cnt2", [1, 8])
            cntf = psb("cntf", [1, 8])
            cntb = psb("cntb", [16, 1])
            ivf = psb("ivf", [16, 128])
            crow = psb("crow", [1, CAP])
            cf16 = psb("cf16", [16, 128])
            self_ = psb("self_", [16, 128])
            idxf = psb("idxf", [16, 128])
            cnt_u = pes.enter_context(nc.sbuf_tensor("cnt_u", [1, 1], U32))
            idx16 = pes.enter_context(nc.sbuf_tensor("idx16", [16, 128], I16))
            idxr = pes.enter_context(nc.sbuf_tensor("idxr", [128, 8 * 48], I16))
            zrow = psb("zrow", [128, 512])
            xcg = [psb(f"xcg{i}", [128, 3 * XROW]) for i in range(2)]
            otm = psb("otm", [128, 3 * 2048])
            xg_fm = psb("xg_fm", [128, NCH * CAP])
            hpa = psb("hpa", [128, NHC * CAP])
            o_fm = psb("o_fm", [128, NCH * CAP])
            wgt = [psb(f"wgta{i}", [128, NCH * 128]) for i in range(2)]
            wut = [psb(f"wuta{i}", [128, NCH * 128]) for i in range(2)]
            wdt = [psb(f"wdta{i}", [128, 512]) for i in range(2)]
            sgl = psb("sgla", [128, CAP])
            cbig = psb("cbiga", [128, CAP])

            def ph_p3a():
                def _(g):
                    W(g, stm, ("p2tmst", NG - 1, 3))
                    W(g, sx1, ("p2x1st", NG - 1))
                    W(g, scf, ("p2cfst", NG - 1))
                    ins = g.iota(iotaP[:], [[16, 128]], base=1,
                                 channel_multiplier=1,
                                 allow_small_or_imprecise_dtypes=True)
                    sZ.inc(ins)
                    ins = g.iota(iota_row[:], [[1, CAP]], base=0,
                                 channel_multiplier=0,
                                 allow_small_or_imprecise_dtypes=True)
                    sZ.inc(ins)
                    ins = g.memset(cntf[:], 0.0)
                    sZ.inc(ins)
                    ins = g.memset(zrow[:], 0.0)
                    sZ.inc(ins)
                    ins = g.memset(xcg[0][:], 0.0)
                    sZ.inc(ins)
                    ins = g.memset(xcg[1][:], 0.0)
                    sZ.inc(ins)
                    M("a_const", sZ)
                    g.wait_ge(sZ.h, sZ.n)
                env.sec("gpsimd", _)

                for e in range(E):
                    def _(s, e=e):
                        if e == 0:
                            W(s, scf, ("p2cfst", NG - 1))
                            W(s, stm, ("p2tmst", NG - 1, 3))
                            W(s, sx1, ("p2x1st", NG - 1))
                        else:
                            W(s, sV, ("a_self", e - 1))
                            W(s, slmo, ("a_idx", e - 1))
                        for half in range(2):
                            ins = s.dma_start(
                                out=cf16[:, half * 64:(half + 1) * 64],
                                in_=cfm_s[e:e + 1, half * 1024:(half + 1) * 1024]
                                .rearrange("one (f p) -> (one p) f", p=16),
                                allow_slow_non_contiguous=True)
                            slmo.inc(ins, 16)
                        M(("a_cf", e), slmo)
                    env.sec("sync", _)

                    def _(v, e=e):
                        W(v, slmo, ("a_cf", e))
                        if e == 0:
                            W(v, sZ, "a_const")
                        else:
                            W(v, sZ, ("a_sg", e - 1))
                        v.drain()
                        v.tensor_single_scalar(self_[:], cf16[:], 0.0, ALU.is_gt)
                        v.drain()
                        v.tensor_mul(self_[:], self_[:], iotaP[:])
                        v.drain()
                        ins = v.tensor_scalar_add(self_[:], self_[:], -1.0)
                        sV.inc(ins)
                        M(("a_self", e), sV)
                    env.sec("vector", _)

                    def _(g, e=e):
                        if e == 0:
                            W(g, sZ, "a_const")
                            g.load_library(library_config.sparse_gather)
                        W(g, sV, ("a_self", e))
                        if e >= 1:
                            W(g, sV, ("a_psel", e - 1))
                        g.memset(idxf[:], -1.0)
                        g.drain()
                        ins = g.sparse_gather(idxf[:], self_[:],
                                              num_found=cnt_u[:])
                        sZ.inc(ins)
                        M(("a_sg", e), sZ)
                    env.sec("gpsimd", _)

                    def _(v, e=e):
                        W(v, sZ, ("a_sg", e))
                        if e >= 1:
                            W(v, slmo, ("a_idx", e - 1))
                        v.drain()
                        v.tensor_scalar_max(idxf[:], idxf[:], 0.0)
                        v.drain()
                        v.tensor_copy(idx16[:], idxf[:])
                        v.drain()
                        v.tensor_single_scalar(idx16[:], idx16[:], 0.0, ALU.max)
                        v.drain()
                        v.tensor_single_scalar(idx16[:], idx16[:], 2047.0,
                                               ALU.min)
                        v.drain()
                        v.tensor_copy(cntf[0:1, e:e + 1], cnt_u[:])
                        v.drain()
                        ins = v.tensor_scalar_add(cnt2[0:1, e:e + 1],
                                                  cntf[0:1, e:e + 1],
                                                  -float(CAP))
                        sV.inc(ins)
                        M(("a_post", e), sV)
                    env.sec("vector", _)

                    def _(t, e=e):
                        W(t, sV, ("a_post", e))
                        ins = t.matmul(PS[0][0:16, 0:1], ones_row[0:1, 0:16],
                                       cntf[0:1, e:e + 1],
                                       start=True, stop=True)
                        sT.inc(ins)
                        M(("a_cbm", e), sT)
                    env.sec("tensor", _)

                    def _(v, e=e):
                        W(v, sT, ("a_cbm", e))
                        v.drain()
                        v.tensor_copy(cntb[:], PS[0][0:16, 0:1])
                        v.drain()
                        v.tensor_copy(self_[:], idx16[:])
                        v.tensor_scalar(idxf[:], iotaP[:], cntb[:], None,
                                        ALU.is_le)
                        v.tensor_scalar(ivf[:], iotaP[:], cntb[:], None,
                                        ALU.is_gt)
                        v.drain()
                        v.tensor_mul(self_[:], self_[:], idxf[:])
                        v.tensor_single_scalar(ivf[:], ivf[:], float(TOWN),
                                               ALU.mult)
                        v.drain()
                        v.tensor_add(self_[:], self_[:], ivf[:])
                        v.drain()
                        ins = v.tensor_copy(idx16[:], self_[:])
                        sV.inc(ins)
                        M(("a_psel", e), sV)
                    env.sec("vector", _)

                    def _(s, e=e):
                        W(s, sV, ("a_psel", e))
                        for r in range(8):
                            ins = s.dma_start(
                                out=idxr[16 * r:16 * (r + 1),
                                         e * 48:(e + 1) * 48],
                                in_=idx16[:, 0:48])
                            slmo.inc(ins, 16)
                        M(("a_idx", e), slmo)
                    env.sec("sync", _)

                def _(g):
                    g.load_library(library_config.mlp)
                    for r in range(16):
                        for cc in range(4):
                            ins = g.dma_start(
                                out=moe_s[r * 128:(r + 1) * 128,
                                          cc * 512:(cc + 1) * 512],
                                in_=zrow[:])
                            ssc.inc(ins, 16)
                    for r in range(16):
                        ins = g.dma_start(
                            out=x1c_s[r * 128:(r + 1) * 128, 2056:XROW],
                            in_=zrow[:, 0:XROW - 2056])
                        ssc.inc(ins, 16)
                    for cc in range(4):
                        ins = g.dma_start(
                            out=moe_s[TOWN:TOWN + 16, cc * 512:(cc + 1) * 512],
                            in_=zrow[0:16, :])
                        ssc.inc(ins, 16)
                        ins = g.dma_start(
                            out=x1c_s[TOWN:TOWN + 16, cc * 512:(cc + 1) * 512],
                            in_=zrow[0:16, :])
                        ssc.inc(ins, 16)
                    ins = g.dma_start(out=x1c_s[TOWN:TOWN + 16, 2048:XROW],
                                      in_=zrow[0:16, 0:XROW - 2048])
                    ssc.inc(ins, 16)
                    M("a_zero", ssc)
                env.sec("gpsimd", _)

                NK = 2 * E
                for e in range(E):
                    for ch in range(2):
                        k = e * 2 + ch
                        isl = slice(e * 48 + ch * 24, e * 48 + ch * 24 + 24)

                        def _(g, e=e, ch=ch, k=k, isl=isl):
                            if k == 0:
                                W(g, slmo, ("a_idx", E - 1))
                                W(g, ssc, "a_zero")
                                W(g, stm, ("p2tmst", NG - 1, 3))
                            elif k >= 2:
                                W(g, sT, ("a_xtr", k - 2, 47))
                            ins = g.dma_gather(
                                xcg[k % 2].rearrange("p (b r) -> p b r",
                                                     r=XROW),
                                x1c_s[:, :], idxr[:, isl], CAP, CAP, XROW)
                            sgt.inc(ins, 16)
                            M(("a_g", k), sgt)
                            g.wait_ge(sgt.h, sgt.n)
                        env.sec("gpsimd", _)

                        def _(t, e=e, k=k):
                            W(t, sgt, ("a_g", k))
                            if k >= 1:
                                W(t, sV, ("a_cbe", k - 1))
                            for b in range(3):
                                ins = t.transpose(
                                    PS[5][0:1, b * 128:(b + 1) * 128],
                                    xcg[k % 2][:, b * XROW + 2048 + e:
                                               b * XROW + 2049 + e],
                                    ident[:])
                                sT.inc(ins)
                            M(("a_crt", k), sT)
                        env.sec("tensor", _)

                        def _(t, k=k):
                            W(t, sgt, ("a_g", k))
                            for i in range(48):
                                b, c = divmod(i, 16)
                                if i >= 4:
                                    W(t, sV, ("a_xev", k, i - 4))
                                ins = t.transpose(
                                    PS[i % 4][:, ((i // 4) % 4) * 128:
                                              ((i // 4) % 4 + 1) * 128],
                                    xcg[k % 2][:, b * XROW + c * 128:
                                               b * XROW + (c + 1) * 128],
                                    ident[:])
                                sT.inc(ins)
                                M(("a_xtr", k, i), sT)
                        env.sec("tensor", _)

                        def _(v, k=k):
                            for i in range(48):
                                b, c = divmod(i, 16)
                                W(v, sT, ("a_xtr", k, i))
                                ins = v.tensor_copy(
                                    xg_fm[:, c * CAP + b * 128:
                                          c * CAP + (b + 1) * 128],
                                    PS[i % 4][:, ((i // 4) % 4) * 128:
                                              ((i // 4) % 4 + 1) * 128])
                                sV.inc(ins)
                                M(("a_xev", k, i), sV)
                            W(v, sT, ("a_crt", k))
                            v.drain()
                            csrc = cntf if ch == 0 else cnt2
                            v.tensor_scalar(crow[:], iota_row[0:1, 0:CAP],
                                            csrc[0:1, k // 2:k // 2 + 1], None,
                                            ALU.is_lt)
                            v.drain()
                            ins = v.tensor_tensor(crow[:], PS[5][0:1, 0:CAP],
                                                  crow[:], ALU.mult)
                            sV.inc(ins)
                            M(("a_crm", k), sV)
                        env.sec("vector", _)

                        def _(t, k=k):
                            W(t, sV, ("a_crm", k))
                            ins = t.matmul(PS[5][:, 0:CAP], ones_row[:],
                                           crow[:], start=True, stop=True)
                            sT.inc(ins)
                            M(("a_cbc", k), sT)
                        env.sec("tensor", _)

                        def _(v, k=k):
                            W(v, sT, ("a_cbc", k))
                            v.drain()
                            ins = v.tensor_copy(cbig[:], PS[5][:, 0:CAP])
                            sV.inc(ins)
                            M(("a_cbe", k), sV)
                        env.sec("vector", _)

                        def _(s, e=e, k=k):
                            for hc in range(NHC):
                                m = k * NHC + hc
                                if m == 0 and k == 0:
                                    W(s, stm, ("p2tmst", NG - 1, 3))
                                    W(s, sx1, ("p2x1st", NG - 1))
                                    W(s, scf, ("p2cfst", NG - 1))
                                if m >= 2:
                                    pk = divmod(m - 2, NHC)
                                    W(s, sT, ("a_gu", pk[0], pk[1]))
                                ins = s.dma_start(
                                    out=wgt[hc % 2][:],
                                    in_=wg[e][:, hc * D:(hc + 1) * D])
                                slgu[hc % 2].inc(ins, 16)
                                ins = s.dma_start(
                                    out=wut[hc % 2][:],
                                    in_=wu[e][:, hc * D:(hc + 1) * D])
                                slgu[hc % 2].inc(ins, 16)
                                M(("a_wld", k, hc), slgu[hc % 2])
                        env.sec("sync", _)

                        def _(t, k=k):
                            for hc in range(NHC):
                                W(t, slgu[hc % 2], ("a_wld", k, hc))
                                m = k * NHC + hc
                                if hc == 0:
                                    W(t, sV, ("a_xev", k, 47))
                                if m >= 2:
                                    pk = divmod(m - 2, NHC)
                                    W(t, sA, ("a_si",) + pk)
                                if m >= 1:
                                    pk = divmod(m - 1, NHC)
                                    W(t, sV, ("a_t1",) + pk)
                                for c in range(NCH):
                                    ins = t.matmul(
                                        PS[2 + hc % 2][:, 0:CAP],
                                        wgt[hc % 2][:, c * 128:(c + 1) * 128],
                                        xg_fm[:, c * CAP:(c + 1) * CAP],
                                        start=(c == 0), stop=(c == NCH - 1))
                                sT.inc(ins)
                                for c in range(NCH):
                                    ins = t.matmul(
                                        PS[4][:, 0:CAP],
                                        wut[hc % 2][:, c * 128:(c + 1) * 128],
                                        xg_fm[:, c * CAP:(c + 1) * CAP],
                                        start=(c == 0), stop=(c == NCH - 1))
                                sT.inc(ins)
                                M(("a_gu", k, hc), sT)
                        env.sec("tensor", _)

                        def _(a, k=k):
                            for hc in range(NHC):
                                W(a, sT, ("a_gu", k, hc))
                                m = k * NHC + hc
                                if m >= 1:
                                    pk = divmod(m - 1, NHC)
                                    W(a, sV, ("a_t1",) + pk)
                                ins = a.activation(sgl[:], PS[2 + hc % 2][:, 0:CAP],
                                                   AF.Silu, bias=0.0, scale=1.0)
                                sA.inc(ins)
                                M(("a_si", k, hc), sA)
                        env.sec("scalar", _)

                        def _(v, k=k):
                            for hc in range(NHC):
                                W(v, sA, ("a_si", k, hc))
                                if hc == 0 and k >= 1:
                                    W(v, sT, ("a_dn", k - 1))
                                v.drain()
                                v.tensor_tensor(sgl[:], sgl[:], PS[4][:, 0:CAP],
                                                ALU.mult)
                                v.drain()
                                ins = v.tensor_tensor(
                                    hpa[:, hc * CAP:(hc + 1) * CAP],
                                    sgl[:], cbig[:], ALU.mult)
                                sV.inc(ins)
                                M(("a_t1", k, hc), sV)
                        env.sec("vector", _)

                        def _(s, e=e, k=k):
                            for dgrp in range(4):
                                for hc in range(NHC):
                                    md = (k * 4 + dgrp) * NHC + hc
                                    if md == 0 and k == 0:
                                        W(s, stm, ("p2tmst", NG - 1, 3))
                                        W(s, sx1, ("p2x1st", NG - 1))
                                        W(s, scf, ("p2cfst", NG - 1))
                                    if md >= 2:
                                        if hc >= 2:
                                            pk = ("a_dnm", k, dgrp, hc - 2)
                                        elif dgrp >= 1:
                                            pk = ("a_dnm", k, dgrp - 1,
                                                  NHC - 1 - hc)
                                        else:
                                            pk = ("a_dnm", k - 1, 3,
                                                  NHC - 1 - hc)
                                        W(s, sT, pk)
                                    ins = s.dma_start(
                                        out=wdt[hc % 2][:],
                                        in_=wd[e][hc * 128:(hc + 1) * 128,
                                                  dgrp * 512:(dgrp + 1) * 512])
                                    sld[hc % 2].inc(ins, 16)
                                    M(("a_dld", k, dgrp, hc), sld[hc % 2])
                        env.sec("sync", _)

                        def _(t, k=k):
                            for dgrp in range(4):
                                for hc in range(NHC):
                                    W(t, sld[hc % 2], ("a_dld", k, dgrp, hc))
                                    W(t, sV, ("a_t1", k, hc))
                                    if hc == 0:
                                        if dgrp == 0:
                                            W(t, sV, ("a_t1", k, NHC - 1))
                                            if k >= 1:
                                                W(t, sV, ("a_ov", k - 1, 3))
                                        else:
                                            W(t, sV, ("a_ov", k, dgrp - 1))
                                    for dj in range(4):
                                        ins = t.matmul(
                                            PS[4 + dj][:, 0:CAP],
                                            wdt[hc % 2][:, dj * 128:
                                                        (dj + 1) * 128],
                                            hpa[:, hc * CAP:(hc + 1) * CAP],
                                            start=(hc == 0), stop=(hc == NHC - 1))
                                    sT.inc(ins)
                                    M(("a_dnm", k, dgrp, hc), sT)
                            M(("a_dn", k), sT)
                        env.sec("tensor", _)

                        def _(v, k=k):
                            for dgrp in range(4):
                                W(v, sT, ("a_dnm", k, dgrp, NHC - 1))
                                v.drain()
                                for dj in range(4):
                                    d = dgrp * 4 + dj
                                    ins = v.tensor_copy(
                                        o_fm[:, d * CAP:(d + 1) * CAP],
                                        PS[4 + dj][:, 0:CAP])
                                    sV.inc(ins)
                                M(("a_ov", k, dgrp), sV)
                        env.sec("vector", _)

                        def _(t, k=k):
                            for i in range(48):
                                c, b = divmod(i, 3)
                                W(t, sV, ("a_ov", k, c // 4))
                                if i >= 4:
                                    W(t, sV, ("a_otv", k, i - 4))
                                ins = t.transpose(
                                    PS[i % 4][:, ((i // 4) % 4) * 128:
                                              ((i // 4) % 4 + 1) * 128],
                                    o_fm[:, c * CAP + b * 128:
                                         c * CAP + (b + 1) * 128],
                                    ident[:])
                                sT.inc(ins)
                                M(("a_otr", k, i), sT)
                        env.sec("tensor", _)

                        def _(v, k=k):
                            if k >= 1:
                                W(v, ssc, ("a_sc", k - 1))
                            for i in range(48):
                                c, b = divmod(i, 3)
                                W(v, sT, ("a_otr", k, i))
                                ins = v.tensor_copy(
                                    otm[:, b * 2048 + c * 128:
                                        b * 2048 + (c + 1) * 128],
                                    PS[i % 4][:, ((i // 4) % 4) * 128:
                                              ((i // 4) % 4 + 1) * 128])
                                sV.inc(ins)
                                M(("a_otv", k, i), sV)
                            ins = v.engine_nop()
                            sV.inc(ins)
                            M(("a_otev", k), sV)
                        env.sec("vector", _)

                        def _(g, k=k, isl=isl):
                            W(g, sV, ("a_otev", k))
                            if k >= 1:
                                W(g, ssc, ("a_sc", k - 1))
                            ins = g.dma_scatter_add(
                                moe_s[:, :],
                                otm.rearrange("p (b r) -> p b r", r=2048),
                                idxr[:, isl], CAP, CAP, 2048)
                            ssc.inc(ins, 16)
                            M(("a_sc", k), ssc)
                        env.sec("gpsimd", _)
            env.run(ph_p3a)
            pes.close()

            st_a = psb("st_a3", [1, 512])
            st_b = psb("st_b3", [1, 512])
            st_c = psb("st_c3", [1, 512])
            st_d = psb("st_d3", [1, 512])
            st_e = psb("st_e3", [1, 512])
            sg1 = psb("sg1", [1, 512])
            sqt = [psb(f"sqt3{i}", [128, 512]) for i in range(2)]
            mb_b = psb("mb_b3", [128, 512])
            yb_b = psb("yb_b3", [128, 512])
            x1t = psb("x1t", [128, NCH * 512])
            cb = psb("cb", [128, 512])
            wgt = [psb(f"wgt{i}", [128, NCH * 128]) for i in range(2)]
            wut = [psb(f"wut{i}", [128, NCH * 128]) for i in range(2)]
            wdt = [psb(f"wdt{i}", [128, 512]) for i in range(2)]
            sgl = [psb("sgl0", [128, 512])] * 2
            t1b = [psb("t1b0", [128, 512])] * 2
            hp = psb("hp", [128, NHC * 512])
            acc = psb("acc", [128, NCH * 512])
            orow = psb("orow", [128, D])
            moet = psb("moet", [128, 2048])

            def ph_p3b():
                for t_i in range(NG):
                    def _(s, t_i=t_i):
                        if t_i == 0:
                            W(s, sx1, ("p2x1st", NG - 1))
                            W(s, ssc, ("a_sc", 2 * E - 1))
                        else:
                            W(s, sT, ("p3otr", t_i - 1, 3))
                        for c in range(NCH):
                            ins = s.dma_start(
                                out=x1t[:, c * 512:(c + 1) * 512],
                                in_=x1_s[c, :, t_i * 512:(t_i + 1) * 512])
                            slx1.inc(ins, 16)
                        M(("p3ld", t_i), slx1)
                    env.sec("sync", _)

                    def _(t, t_i=t_i):
                        W(t, slx1, ("p3ld", t_i))
                        if t_i == 0:
                            W(t, sLc, "init_ld")
                            W(t, sV, ("a_otev", 2 * E - 1))
                        else:
                            W(t, sV, ("p3oev", t_i - 1, 3))
                            W(t, sA, ("p3sig", t_i - 1))
                        for c in range(NCH):
                            ins = t.matmul(PS[6][0:1, :],
                                           sgw_sb[:, c:c + 1],
                                           x1t[:, c * 512:(c + 1) * 512],
                                           start=(c == 0), stop=(c == NCH - 1))
                        sT.inc(ins)
                        M(("p3sgm", t_i), sT)
                        W(t, sA, ("p3sig", t_i))
                        if t_i >= 1:
                            W(t, sV, ("p3cb", t_i - 1))
                        ins = t.matmul(PS[3][:], ones_row[:], sg1[:],
                                       start=True, stop=True)
                        sT.inc(ins)
                        M(("p3cbm", t_i), sT)
                    env.sec("tensor", _)

                    def _(a, t_i=t_i):
                        W(a, sT, ("p3sgm", t_i))
                        ins = a.activation(sg1[:], PS[6][0:1, :], AF.Sigmoid,
                                           bias=0.0, scale=1.0)
                        sA.inc(ins)
                        M(("p3sig", t_i), sA)
                    env.sec("scalar", _)

                    def _(v, t_i=t_i):
                        v.drain()
                        if t_i >= 1:
                            W(v, sV, ("p3t1", t_i - 1, 9, NHC - 1))
                        W(v, sT, ("p3cbm", t_i))
                        ins = v.tensor_copy(cb[:], PS[3][:])
                        sV.inc(ins)
                        M(("p3cb", t_i), sV)
                    env.sec("vector", _)

                    for e in (8, 9):
                        nhc = NHC
                        hoff = (e - 8) * NHC

                        def _(s, t_i=t_i, e=e, hoff=hoff):
                            for hc in range(nhc):
                                m = (t_i * 2 + (e - 8)) * NHC + hc
                                if m >= 2:
                                    pt, r = divmod(m - 2, 2 * NHC)
                                    pe, phc = divmod(r, NHC)
                                    W(s, sT, ("p3gu", pt, pe + 8, phc))
                                ch = hoff + hc
                                ins = s.dma_start(out=wgt[hc % 2][:],
                                                  in_=shg[:, ch * D:(ch + 1) * D])
                                slgu[hc % 2].inc(ins, 16)
                                ins = s.dma_start(out=wut[hc % 2][:],
                                                  in_=shu[:, ch * D:(ch + 1) * D])
                                slgu[hc % 2].inc(ins, 16)
                                M(("p3wld", t_i, e, hc), slgu[hc % 2])
                        env.sec("sync", _)

                        def _(t, t_i=t_i, e=e):
                            for hc in range(nhc):
                                W(t, slgu[hc % 2], ("p3wld", t_i, e, hc))
                                m = (t_i * 2 + (e - 8)) * NHC + hc
                                if e == 8 and hc == 0:
                                    W(t, sV, ("p3cb", t_i))
                                    if t_i == 0:
                                        W(t, sV, ("a_otev", 2 * E - 1))
                                    else:
                                        W(t, sV, ("p3lnt", t_i - 1))
                                if m >= 2:
                                    pt, r = divmod(m - 2, 2 * NHC)
                                    pe, phc = divmod(r, NHC)
                                    W(t, sA, ("p3silu", pt, pe + 8, phc))
                                    W(t, sV, ("p3t1", pt, pe + 8, phc))
                                for c in range(NCH):
                                    ins = t.matmul(
                                        PS[0 + hc % 2][:],
                                        wgt[hc % 2][:, c * 128:(c + 1) * 128],
                                        x1t[:, c * 512:(c + 1) * 512],
                                        start=(c == 0), stop=(c == NCH - 1))
                                sT.inc(ins)
                                for c in range(NCH):
                                    ins = t.matmul(
                                        PS[2 + hc % 2][:],
                                        wut[hc % 2][:, c * 128:(c + 1) * 128],
                                        x1t[:, c * 512:(c + 1) * 512],
                                        start=(c == 0), stop=(c == NCH - 1))
                                sT.inc(ins)
                                M(("p3gu", t_i, e, hc), sT)
                        env.sec("tensor", _)

                        def _(a, t_i=t_i, e=e):
                            for hc in range(nhc):
                                W(a, sT, ("p3gu", t_i, e, hc))
                                m = (t_i * 2 + (e - 8)) * NHC + hc
                                if m >= 1:
                                    pt, r = divmod(m - 1, 2 * NHC)
                                    pe, phc = divmod(r, NHC)
                                    W(a, sV, ("p3t1", pt, pe + 8, phc))
                                ins = a.activation(sgl[hc % 2][:],
                                                   PS[0 + hc % 2][:], AF.Silu,
                                                   bias=0.0, scale=1.0)
                                sA.inc(ins)
                                M(("p3silu", t_i, e, hc), sA)
                        env.sec("scalar", _)

                        def _(v, t_i=t_i, e=e):
                            v.drain()
                            for hc in range(nhc):
                                W(v, sA, ("p3silu", t_i, e, hc))
                                m = (t_i * 2 + (e - 8)) * NHC + hc
                                if m >= 1:
                                    pt, r = divmod(m - 1, 2 * NHC)
                                    pe, phc = divmod(r, NHC)
                                    W(v, sV, ("p3hp", pt, pe + 8, phc))
                                v.drain()
                                v.tensor_tensor(t1b[hc % 2][:],
                                                sgl[hc % 2][:],
                                                PS[2 + hc % 2][:], ALU.mult)
                                v.drain()
                                ins = v.tensor_tensor(
                                    hp[:, hc * 512:(hc + 1) * 512],
                                    t1b[hc % 2][:], cb[:], ALU.mult)
                                sV.inc(ins)
                                M(("p3t1", t_i, e, hc), sV)
                                env.marks[("p3hp", t_i, e, hc)] =                                     env.marks[("p3t1", t_i, e, hc)]
                        env.sec("vector", _)

                        def _(s, t_i=t_i, e=e, hoff=hoff):
                            for dgrp in range(4):
                                for hc in range(nhc):
                                    md = ((t_i * 2 + (e - 8)) * 4 + dgrp) * NHC + hc
                                    pk = None
                                    if hc >= 2:
                                        pk = ("p3dnm", t_i, e, dgrp, hc - 2)
                                    elif dgrp >= 1:
                                        pk = ("p3dnm", t_i, e, dgrp - 1,
                                              nhc - 1 - hc)
                                    elif md >= 2:
                                        pt, pe = (t_i, 8) if e == 9 else                                             (t_i - 1, 9)
                                        pk = ("p3dnm", pt, pe, 3, nhc - 1 - hc)
                                    if pk is not None:
                                        W(s, sT, pk)
                                    ch = hoff + hc
                                    ins = s.dma_start(
                                        out=wdt[hc % 2][:],
                                        in_=shd[ch * 128:(ch + 1) * 128,
                                                dgrp * 512:(dgrp + 1) * 512])
                                    sld[hc % 2].inc(ins, 16)
                                    M(("p3dld", t_i, e, dgrp, hc), sld[hc % 2])
                        env.sec("sync", _)

                        def _(t, t_i=t_i, e=e):
                            for dgrp in range(4):
                                for hc in range(nhc):
                                    W(t, sld[hc % 2], ("p3dld", t_i, e, dgrp, hc))
                                    W(t, sV, ("p3hp", t_i, e, hc))
                                    if hc == 0:
                                        if dgrp > 0:
                                            W(t, sV, ("p3acc", t_i, e, dgrp - 1))
                                        elif e == 9:
                                            W(t, sV, ("p3acc", t_i, 8, 3))
                                        elif t_i > 0:
                                            W(t, sV, ("p3acc", t_i - 1, 9, 3))
                                            W(t, sV, ("p3oev", t_i - 1, 3))
                                    for dj in range(4):
                                        ins = t.matmul(
                                            PS[4 + dj][:],
                                            wdt[hc % 2][:, dj * 128:
                                                        (dj + 1) * 128],
                                            hp[:, hc * 512:(hc + 1) * 512],
                                            start=(hc == 0),
                                            stop=(hc == nhc - 1))
                                    sT.inc(ins)
                                    M(("p3dnm", t_i, e, dgrp, hc), sT)
                            M(("p3dn", t_i, e), sT)
                        env.sec("tensor", _)

                        def _(v, t_i=t_i, e=e):
                            for dgrp in range(4):
                                W(v, sT, ("p3dnm", t_i, e, dgrp, nhc - 1))
                                v.drain()
                                for dj in range(4):
                                    d = dgrp * 4 + dj
                                    sl = slice(d * 512, (d + 1) * 512)
                                    if e == 8:
                                        ins = v.tensor_copy(acc[:, sl],
                                                            PS[4 + dj][:])
                                    else:
                                        ins = v.tensor_tensor(acc[:, sl],
                                                              acc[:, sl],
                                                              PS[4 + dj][:],
                                                              ALU.add)
                                    sV.inc(ins)
                                M(("p3acc", t_i, e, dgrp), sV)
                        env.sec("vector", _)

                    def _(s, t_i=t_i):
                        for tb in range(4):
                            if t_i == 0 and tb == 0:
                                W(s, ssc, ("a_sc", 2 * E - 1))
                            if (t_i, tb) != (0, 0):
                                pb = (t_i, tb - 1) if tb else (t_i - 1, 3)
                                W(s, sT, ("p3mtr",) + pb)
                            ins = s.dma_start(
                                out=moet[:],
                                in_=moe_s[t_i * 512 + tb * 128:
                                          t_i * 512 + (tb + 1) * 128, :])
                            slmo.inc(ins, 16)
                            M(("p3mld", t_i, tb), slmo)
                    env.sec("sync", _)

                    def _(t, t_i=t_i):
                        for tb in range(4):
                            W(t, slmo, ("p3mld", t_i, tb))
                            if tb == 0:
                                W(t, sV, ("p3acc", t_i, 9, 3))
                                if t_i >= 1:
                                    W(t, sV, ("p3ybb", t_i - 1))
                            else:
                                W(t, sV, ("p3mev", t_i, tb - 1))
                            for c in range(NCH):
                                if c >= 2:
                                    W(t, sV, ("p3mevc", t_i, tb, c - 2))
                                ins = t.transpose(
                                    PS[2 + c % 2][:, ((c // 2) % 4) * 128:
                                                  ((c // 2) % 4 + 1) * 128],
                                    moet[:, c * 128:(c + 1) * 128],
                                    ident[:])
                                sT.inc(ins)
                                M(("p3mtrc", t_i, tb, c), sT)
                            M(("p3mtr", t_i, tb), sT)
                    env.sec("tensor", _)

                    def _(v, t_i=t_i):
                        for tb in range(4):
                            for c in range(NCH):
                                W(v, sT, ("p3mtrc", t_i, tb, c))
                                if c == 0:
                                    v.drain()
                                ins = v.tensor_tensor(
                                    acc[:, c * 512 + tb * 128:
                                        c * 512 + (tb + 1) * 128],
                                    acc[:, c * 512 + tb * 128:
                                        c * 512 + (tb + 1) * 128],
                                    PS[2 + c % 2][:, ((c // 2) % 4) * 128:
                                                  ((c // 2) % 4 + 1) * 128],
                                    ALU.add)
                                sV.inc(ins)
                                M(("p3mevc", t_i, tb, c), sV)
                            M(("p3mev", t_i, tb), sV)
                    env.sec("vector", _)

                    def _(v, t_i=t_i):
                        W(v, sV, ("p3mev", t_i, 3))
                        for d in range(16):
                            sl = slice(d * 512, (d + 1) * 512)
                            v.drain()
                            ins = v.tensor_tensor(acc[:, sl], acc[:, sl],
                                                  x1t[:, sl], ALU.add)
                            sV.inc(ins)
                            M(("p3r2", t_i, d), sV)
                    env.sec("vector", _)

                    def _(a, t_i=t_i):
                        if t_i == 0:
                            W(a, sT, ("p2stats", NG - 1, 15))
                        else:
                            W(a, sT, ("p3stats", t_i - 1, 15))
                        for d in range(16):
                            W(a, sV, ("p3r2", t_i, d))
                            if d >= 2:
                                W(a, sT, ("p3stats", t_i, d - 2))
                            ins = a.activation(sqt[d % 2][:],
                                               acc[:, d * 512:(d + 1) * 512],
                                               AF.Square, bias=0.0, scale=1.0)
                            sA.inc(ins)
                            M(("p3sq", t_i, d), sA)
                        W(a, sV, ("p3veps", t_i))
                        ins = a.activation(st_c[:], st_b[:], AF.Sqrt,
                                           bias=0.0, scale=1.0)
                        sA.inc(ins)
                        M(("p3sqrt", t_i), sA)
                    env.sec("scalar", _)

                    def _(t, t_i=t_i):
                        for d in range(16):
                            W(t, sV, ("p3r2", t_i, d))
                            W(t, sA, ("p3sq", t_i, d))
                            t.matmul(PS[0][0:1, :], ones_col[:],
                                     acc[:, d * 512:(d + 1) * 512],
                                     start=(d == 0), stop=(d == 15))
                            ins = t.matmul(PS[1][0:1, :], ones_col[:],
                                           sqt[d % 2][:],
                                           start=(d == 0), stop=(d == 15))
                            sT.inc(ins)
                            M(("p3stats", t_i, d), sT)
                        W(t, sV, ("p3lnt", t_i))
                        ins = t.matmul(PS[2][:], ones_row[:], st_a[:],
                                       start=True, stop=True)
                        sT.inc(ins)
                        M(("p3bcm", t_i), sT)
                        ins = t.matmul(PS[3][:], ones_row[:], st_d[:],
                                       start=True, stop=True)
                        sT.inc(ins)
                        M(("p3bcy", t_i), sT)
                        W(t, sV, ("p3xn", t_i))
                        for tb in range(4):
                            if tb >= 1:
                                W(t, sV, ("p3oev", t_i, tb - 1))
                            for c in range(NCH):
                                ins = t.transpose(
                                    PS[4 + c // 4][:, (c % 4) * 128:
                                                   (c % 4 + 1) * 128],
                                    x1t[:, c * 512 + tb * 128:
                                        c * 512 + (tb + 1) * 128],
                                    ident[:])
                                if c % 4 == 3:
                                    sT.inc(ins)
                            M(("p3otr", t_i, tb), sT)
                    env.sec("tensor", _)

                    def _(v, t_i=t_i):
                        W(v, sT, ("p3stats", t_i, 15))
                        v.drain()
                        v.tensor_scalar_mul(st_a[:], PS[0][0:1, :], INVD)
                        ins = v.tensor_scalar_mul(st_b[:], PS[1][0:1, :], INVD)
                        sV.inc(ins)
                        v.drain()
                        v.tensor_mul(st_c[:], st_a[:], st_a[:])
                        v.drain()
                        v.tensor_sub(st_b[:], st_b[:], st_c[:])
                        v.drain()
                        ins = v.tensor_scalar_add(st_b[:], st_b[:], EPS)
                        sV.inc(ins)
                        M(("p3veps", t_i), sV)
                        W(v, sA, ("p3sqrt", t_i))
                        ins = v.reciprocal(st_d[:], st_c[:])
                        sV.inc(ins)
                        for _it in range(2):
                            v.drain()
                            v.tensor_mul(st_e[:], st_b[:], st_d[:])
                            v.drain()
                            v.tensor_mul(st_e[:], st_e[:], st_d[:])
                            v.drain()
                            v.tensor_scalar(st_e[:], st_e[:], -0.5, 1.5,
                                            ALU.mult, ALU.add)
                            v.drain()
                            v.tensor_mul(st_d[:], st_d[:], st_e[:])
                        ins = v.engine_nop()
                        sV.inc(ins)
                        M(("p3lnt", t_i), sV)
                        W(v, sT, ("p3bcm", t_i))
                        v.drain()
                        ins = v.tensor_copy(mb_b[:], PS[2][:])
                        sV.inc(ins)
                        W(v, sT, ("p3bcy", t_i))
                        ins = v.tensor_copy(yb_b[:], PS[3][:])
                        sV.inc(ins)
                        M(("p3ybb", t_i), sV)
                        W(v, sT, ("p3gu", t_i, 9, NHC - 1))
                        for d in range(16):
                            sl = slice(d * 512, (d + 1) * 512)
                            v.drain()
                            v.tensor_sub(x1t[:, sl], acc[:, sl], mb_b[:])
                            v.drain()
                            ins = v.tensor_mul(x1t[:, sl], x1t[:, sl], yb_b[:])
                            sV.inc(ins)
                        M(("p3xn", t_i), sV)
                        for tb in range(4):
                            W(v, sT, ("p3otr", t_i, tb))
                            if (t_i, tb) != (0, 0):
                                pb = (t_i, tb - 1) if tb >= 1 else (t_i - 1, 3)
                                W(v, sor, ("p3ost",) + pb)
                            v.drain()
                            for q4 in range(4):
                                ins = v.tensor_copy(
                                    orow[:, q4 * 512:(q4 + 1) * 512],
                                    PS[4 + q4][:])
                            sV.inc(ins)
                            M(("p3oev", t_i, tb), sV)
                    env.sec("vector", _)

                    def _(gp, t_i=t_i):
                        for tb in range(4):
                            W(gp, sV, ("p3oev", t_i, tb))
                            ins = gp.dma_start(
                                out=out[t_i * 512 + tb * 128:
                                        t_i * 512 + (tb + 1) * 128, :],
                                in_=orow[:])
                            sor.inc(ins, 16)
                            M(("p3ost", t_i, tb), sor)
                    env.sec("gpsimd", _)
            env.run(ph_p3b)
            pes.close()

            _finish(block, dma_sems)
    lower_extended_insts(nc)
    return nc


def _finish(block, store_sems):
    finals = [(s, s.n) for s in store_sems]

    def _(g):
        for s, n in finals:
            g.wait_ge(s.h, n)
    block.gpsimd(_)


def _in_maps(inputs):
    x = np.asarray(inputs["L0_attn_x"], dtype=np.float32)
    gwT = np.ascontiguousarray(np.asarray(inputs["L0_ffn_gate_w"],
                                          dtype=np.float32).T)
    sgw = np.ascontiguousarray(np.asarray(inputs["L0_ffn_shgate_w"],
                                          dtype=np.float32).reshape(D, 1))
    shared = {
        "wq": np.ascontiguousarray(np.asarray(inputs["L0_attn_Wq"], np.float32)),
        "wk": np.ascontiguousarray(np.asarray(inputs["L0_attn_Wk"], np.float32)),
        "wv": np.ascontiguousarray(np.asarray(inputs["L0_attn_Wv"], np.float32)),
        "wo": np.ascontiguousarray(np.asarray(inputs["L0_attn_Wo"], np.float32)),
        "gwT": gwT,
        "wg": np.ascontiguousarray(
            np.asarray(inputs["L0_ffn_w_gate"], np.float32)
            .reshape(E, NCH, 128, NHC, 128).transpose(0, 2, 3, 1, 4)
            .reshape(E, 128, NHC * D)),
        "wu": np.ascontiguousarray(
            np.asarray(inputs["L0_ffn_w_up"], np.float32)
            .reshape(E, NCH, 128, NHC, 128).transpose(0, 2, 3, 1, 4)
            .reshape(E, 128, NHC * D)),
        "wd": np.ascontiguousarray(np.asarray(inputs["L0_ffn_w_down"], np.float32)),
        "sgw": sgw,
        "shg": np.ascontiguousarray(
            np.asarray(inputs["L0_ffn_shw_gate"], np.float32)
            .reshape(NCH, 128, NSC, 128).transpose(1, 2, 0, 3)
            .reshape(128, NSC * D)),
        "shu": np.ascontiguousarray(
            np.asarray(inputs["L0_ffn_shw_up"], np.float32)
            .reshape(NCH, 128, NSC, 128).transpose(1, 2, 0, 3)
            .reshape(128, NSC * D)),
        "shd": np.ascontiguousarray(np.asarray(inputs["L0_ffn_shw_down"], np.float32)),
    }
    maps = []
    for c in range(8):
        b, hf = divmod(c, 2)
        own = x[b, hf * TOWN:(hf + 1) * TOWN, :]
        other = x[b, (1 - hf) * TOWN:(2 - hf) * TOWN, :]
        xin_np = np.ascontiguousarray(np.concatenate([own, other], axis=0))
        km = np.zeros((128, 32), dtype=np.float32)
        if hf == 1:
            km[:, 16:] = -FMAX
        m = dict(shared)
        m["xin"] = xin_np
        m["kmask"] = km
        maps.append(m)
    return maps


def kernel(**inputs):
    maps = _in_maps(inputs)
    nc = build_nc()
    res = run_bass_kernel_spmd(nc, maps, core_ids=list(range(8)))
    outp = np.empty((4, 2 * TOWN, D), dtype=np.float32)
    for c in range(8):
        b, hf = divmod(c, 2)
        outp[b, hf * TOWN:(hf + 1) * TOWN, :] = res.results[c]["out"]
    return outp


# version 12 (final)
# speedup vs baseline: 1.1330x; 1.1330x over previous
import contextlib
import numpy as np

import concourse.bass as bass
from concourse.library_overlay import lower_extended_insts
import concourse.mybir as mybir
from concourse.bass_utils import run_bass_kernel_spmd
from concourse import library_config

F32 = mybir.dt.float32
I16 = mybir.dt.int16
U32 = mybir.dt.uint32
AF = mybir.ActivationFunctionType
ALU = mybir.AluOpType
AX = mybir.AxisListType
FMAX = float(np.finfo(np.float32).max)

D = 2048
NCH = 16
TOWN = 2048
HEADS = 16
E = 8
HME = 1408
NHC = 11
HSH = 2816
NSC = 22
NG = 4
CAP = 384
XROW = 2112
SCALE = 1.0 / float(np.sqrt(np.float32(128)))
EPS = 1e-5
INVD = 1.0 / float(np.float32(D))


class Sem:
    def __init__(self, nc, es, name):
        self.h = es.enter_context(nc.semaphore(name))
        self.name = name
        self.n = 0

    def inc(self, instr, k=1):
        instr.then_inc(self.h, k)
        self.n += k
        return self.n


class _DummyInstr:
    def then_inc(self, h, k):
        return self


class _CountEngine:
    def __getattr__(self, name):
        def _f(*a, **k):
            return _DummyInstr()
        return _f


class Env:
    def __init__(self, block, sems):
        self.block = block
        self.sems = sems
        self.marks = {}
        self.counting = False

    def sec(self, engine_name, fn):
        if self.counting:
            fn(_CountEngine())
        else:
            getattr(self.block, engine_name)(fn)

    def wait(self, eng, sem, key):
        if not self.counting:
            nm, val = self.marks[key]
            assert nm == sem.name, ("wait sem mismatch", key, nm, sem.name)
            eng.wait_ge(sem.h, val)

    def mark(self, key, sem):
        if self.counting:
            assert key not in self.marks, ("duplicate mark", key)
            self.marks[key] = (sem.name, sem.n)
        else:
            assert self.marks.get(key) == (sem.name, sem.n), \
                (key, self.marks.get(key), sem.n)

    def run(self, fn):
        snap = [(s, s.n) for s in self.sems]
        self.counting = True
        fn()
        counted = [(s, s.n) for s in self.sems]
        for s, n in snap:
            s.n = n
        self.counting = False
        fn()
        for (s, n) in counted:
            assert s.n == n, (s, s.n, n)


def kbs_of(g):
    return list(range(4 * g, 32))


def _prev_hc(t_i, e, hc, back):
    gi = (t_i * 10 + e) * NHC + hc - back
    if gi < 0:
        return None
    q, hc = divmod(gi, NHC)
    t_i, e = divmod(q, 10)
    return (t_i, e, hc)


def build_nc(stop_after="p3", debug_taps=False):
    nc = bass.Bass("TRN2", target_bir_lowering=False)
    scratch_kind = "ExternalOutput" if debug_taps else "Internal"

    xin = nc.dram_tensor("xin", [2 * TOWN, D], F32, kind="ExternalInput").ap()
    kmask = nc.dram_tensor("kmask", [128, 32], F32, kind="ExternalInput").ap()
    wq = nc.dram_tensor("wq", [D, D], F32, kind="ExternalInput").ap()
    wk = nc.dram_tensor("wk", [D, D], F32, kind="ExternalInput").ap()
    wv = nc.dram_tensor("wv", [D, D], F32, kind="ExternalInput").ap()
    wo = nc.dram_tensor("wo", [D, D], F32, kind="ExternalInput").ap()
    gwT = nc.dram_tensor("gwT", [D, E], F32, kind="ExternalInput").ap()
    wg = nc.dram_tensor("wg", [E, 128, NHC * D], F32, kind="ExternalInput").ap()
    wu = nc.dram_tensor("wu", [E, 128, NHC * D], F32, kind="ExternalInput").ap()
    wd = nc.dram_tensor("wd", [E, HME, D], F32, kind="ExternalInput").ap()
    sgw = nc.dram_tensor("sgw", [D, 1], F32, kind="ExternalInput").ap()
    shg = nc.dram_tensor("shg", [128, NSC * D], F32, kind="ExternalInput").ap()
    shu = nc.dram_tensor("shu", [128, NSC * D], F32, kind="ExternalInput").ap()
    shd = nc.dram_tensor("shd", [HSH, D], F32, kind="ExternalInput").ap()
    out = nc.dram_tensor("out", [TOWN, D], F32, kind="ExternalOutput").ap()

    dump = nc.dram_tensor("dump", [128, HEADS * 512], F32,
                          kind="ExternalOutput" if debug_taps else "Internal").ap()
    xfm_s = nc.dram_tensor("xfm_s", [NCH, 128, 2 * TOWN], F32, kind=scratch_kind).ap()
    q_s = nc.dram_tensor("q_s", [HEADS, 128, TOWN], F32, kind=scratch_kind).ap()
    k_s = nc.dram_tensor("k_s", [HEADS, 128, 2 * TOWN], F32, kind=scratch_kind).ap()
    v_s = nc.dram_tensor("v_s", [HEADS, 128, 4096], F32, kind=scratch_kind).ap()
    x1_s = nc.dram_tensor("x1_s", [NCH, 128, TOWN], F32, kind=scratch_kind).ap()
    x1c_s = nc.dram_tensor("x1c_s", [TOWN + 16, XROW], F32,
                           kind=scratch_kind).ap()
    moe_s = nc.dram_tensor("moe_s", [TOWN + 16, 2048], F32,
                           kind=scratch_kind).ap()
    cfm_s = nc.dram_tensor("cfm_s", [E, TOWN], F32, kind=scratch_kind).ap()

    es = contextlib.ExitStack()
    with es:
        PS = [es.enter_context(nc.psum_tensor(f"ps{i}", [128, 512], F32))
              for i in range(8)]

        sT = Sem(nc, es, "sT")
        sV = Sem(nc, es, "sV")
        sA = Sem(nc, es, "sA")
        sZ = Sem(nc, es, "sZ")
        sLc = Sem(nc, es, "sLc")
        slx = [Sem(nc, es, f"slx{i}") for i in range(2)]
        stv0 = [Sem(nc, es, f"stv0{i}") for i in range(2)]
        slw = Sem(nc, es, "slw")
        slt = [Sem(nc, es, f"slt{i}") for i in range(2)]
        sqk = [Sem(nc, es, f"sqk{i}") for i in range(2)]
        spv = [Sem(nc, es, f"spv{i}") for i in range(2)]
        slk = [Sem(nc, es, f"slk{i}") for i in range(2)]
        slx0 = Sem(nc, es, "semx0")
        slwo = Sem(nc, es, "slwo")
        sx1 = Sem(nc, es, "sx1")
        scf = Sem(nc, es, "scf")
        slx1 = Sem(nc, es, "semx1ld")
        slgu = [Sem(nc, es, f"slgu{i}") for i in range(2)]
        sld = [Sem(nc, es, f"sld{i}") for i in range(2)]
        sor = Sem(nc, es, "sor")
        sgt = [Sem(nc, es, "sgt0"), Sem(nc, es, "sgt1")]
        ssc = Sem(nc, es, "ssc")
        stm = Sem(nc, es, "stm")
        slmo = Sem(nc, es, "slmo")
        dma_sems = [sLc] + slx + stv0 + [slw] + slt + sqk + \
            spv + slk + [slx0, slwo, sx1, scf, slx1] + slgu + sld + \
            [sor, sgt[0], sgt[1], ssc, stm, slmo]
        sems = [sT, sV, sA, sZ] + dma_sems

        def sb(name, shape, dt=F32):
            return es.enter_context(nc.sbuf_tensor(name, shape, dt))

        ones_col = sb("ones_col", [128, 1])
        ones_row = sb("ones_row", [1, 128])
        ident = sb("ident", [128, 128])

        kmask_sb = sb("kmask_sb", [128, 32])
        gwT_sb = sb("gwT_sb", [128, NCH * E])
        sgw_sb = sb("sgw_sb", [128, NCH])

        z1 = sb("z1", [1, 512])
        zflag = sb("zflag", [1, 512])
        vsum_sb = sb("vsum_sb", [1, 128])

        rz = sb("rz", [1, 512])


        lg = sb("lg", [128, 32])
        gnm = sb("gnm", [128, 1])
        grs = sb("grs", [128, 1])
        gm1 = sb("gm1", [128, 8])
        gm2 = sb("gm2", [128, 8])
        gden = sb("gden", [128, 1])
        gpr = sb("gpr", [128, 32])
        gp1 = sb("gp1", [128, 8])
        gp2 = sb("gp2", [128, 8])
        gcomb = sb("gcomb", [128, 32])

        pes = contextlib.ExitStack()

        def psb(name, shape, dt=F32):
            return pes.enter_context(nc.sbuf_tensor(name, shape, dt))

        with nc.Block() as block:
            env = Env(block, sems)
            M = env.mark
            W = env.wait

            def ph_init():
                def _(g):
                    g.memset(ones_col[:], 1.0)
                    g.memset(ones_row[:], 1.0)
                    g.memset(ident[:], 1.0)
                    ins = g.memset(gm1[:], -1.0)
                    sZ.inc(ins)
                    ins = g.memset(gm2[:], -1.0)
                    sZ.inc(ins)
                    M("init_ms", sZ)
                    W(g, sZ, "init_ms")
                    g.wait_ge(sZ.h, sZ.n)
                    ins = g.affine_select(ident[:], ident[:], [[-1, 128]],
                                          ALU.is_equal, 0.0, base=0,
                                          channel_multiplier=1)
                    sZ.inc(ins)
                    M("init", sZ)
                    g.wait_ge(sZ.h, sZ.n)
                env.sec("gpsimd", _)

                def _(s):
                    ins = s.dma_start(out=kmask_sb[:], in_=kmask)
                    sLc.inc(ins, 16)
                    for c in range(NCH):
                        ins = s.dma_start(out=gwT_sb[:, c * E:(c + 1) * E],
                                          in_=gwT[c * 128:(c + 1) * 128, :])
                        sLc.inc(ins, 16)
                    ins = s.dma_start(
                        out=sgw_sb[:],
                        in_=sgw.rearrange("(c p) one -> p (c one)", p=128),
                        allow_slow_non_contiguous=True)
                    sLc.inc(ins, 16)
                    M("init_ld", sLc)
                env.sec("sync", _)
            env.run(ph_init)

            xrow = [psb(f"xrow{i}", [128, D]) for i in range(2)]
            xev = [psb(f"xev{i}", [128, 512]) for i in range(2)]

            def ph_p0():
                def _(s):
                    for tb in range(32):
                        if tb >= 2:
                            W(s, sT, ("p0tr", tb - 2, 3))
                        ins = s.dma_start(out=xrow[tb % 2][:],
                                          in_=xin[tb * 128:(tb + 1) * 128, :])
                        slx[tb % 2].inc(ins, 16)
                        M(("p0ld", tb), slx[tb % 2])
                env.sec("sync", _)

                def _(t):
                    W(t, sZ, "init")
                    for tb in range(32):
                        W(t, slx[tb % 2], ("p0ld", tb))
                        for quad in range(4):
                            i = tb * 4 + quad
                            bank = PS[i % 2]
                            if i >= 2:
                                W(t, sV, ("p0ev", i - 2))
                            for j in range(4):
                                c = quad * 4 + j
                                ins = t.transpose(
                                    bank[:, j * 128:(j + 1) * 128],
                                    xrow[tb % 2][:, c * 128:(c + 1) * 128],
                                    ident[:])
                            sT.inc(ins)
                            M(("p0tr", tb, quad), sT)
                env.sec("tensor", _)

                def _(v):
                    v.drain()
                    for tb in range(32):
                        for quad in range(4):
                            i = tb * 4 + quad
                            W(v, sT, ("p0tr", tb, quad))
                            if i >= 2:
                                W(v, stv0[i % 2], ("p0st", i - 2))
                            ins = v.tensor_copy(xev[i % 2][:], PS[i % 2][:])
                            sV.inc(ins)
                            M(("p0ev", i), sV)
                env.sec("vector", _)

                def _(g):
                    for tb in range(32):
                        for quad in range(4):
                            i = tb * 4 + quad
                            W(g, sV, ("p0ev", i))
                            for j in range(4):
                                c = quad * 4 + j
                                ins = g.dma_start(
                                    out=xfm_s[c, :, tb * 128:(tb + 1) * 128],
                                    in_=xev[i % 2][:, j * 128:(j + 1) * 128])
                                stv0[i % 2].inc(ins, 16)
                            M(("p0st", i), stv0[i % 2])
                env.sec("gpsimd", _)
            env.run(ph_p0)
            pes.close()

            if stop_after == "p0":
                _finish(block, dma_sems)
                return nc

            w_half = psb("w_half", [128, NCH * 1024])
            xt = [psb(f"xt{i}", [128, NCH * 512]) for i in range(2)]
            pev = [psb(f"pev{i}", [128, 512]) for i in range(2)]

            def ph_p1():
                first = [True]
                for wsrc, dst_s, ntt, nm in [(wq, q_s, 4, "q"), (wk, k_s, 8, "k")]:
                    for half in range(2):
                        tag = (nm, half)
                        prev_tag = {"q0": None, "q1": ("q", 0),
                                    "k0": ("q", 1), "k1": ("k", 0)}[nm + str(half)]

                        def _(s, wsrc=wsrc, ntt=ntt, tag=tag, prev_tag=prev_tag,
                              half=half, first=first):
                            if tag == ("q", 0):
                                W(s, stv0[0], ("p0st", 126))
                                W(s, stv0[1], ("p0st", 127))
                            if prev_tag is not None:
                                W(s, sT, ("p1wfree", prev_tag))
                            for c in range(NCH):
                                ins = s.dma_start(
                                    out=w_half[:, c * 1024:(c + 1) * 1024],
                                    in_=wsrc[c * 128:(c + 1) * 128,
                                             half * 1024:(half + 1) * 1024])
                                slw.inc(ins, 16)
                            M(("p1wld", tag), slw)
                            for tt in range(ntt):
                                if tt >= 2:
                                    W(s, sT, ("p1mm", tag, tt - 2, 7))
                                for c in range(NCH):
                                    ins = s.dma_start(
                                        out=xt[tt % 2][:, c * 512:(c + 1) * 512],
                                        in_=xfm_s[c, :, tt * 512:(tt + 1) * 512])
                                    slt[tt % 2].inc(ins, 16)
                                M(("p1xld", tag, tt), slt[tt % 2])
                        env.sec("sync", _)

                        prev_ev = {("q", 0): ("p0ev", 127),
                                   ("q", 1): ("p1ev", ("q", 0), 31),
                                   ("k", 0): ("p1ev", ("q", 1), 31),
                                   ("k", 1): ("p1ev", ("k", 0), 63)}[tag]

                        def _(t, ntt=ntt, tag=tag, prev_ev=prev_ev):
                            W(t, slw, ("p1wld", tag))
                            for tt in range(ntt):
                                W(t, slt[tt % 2], ("p1xld", tag, tt))
                                for f in range(8):
                                    i = tt * 8 + f
                                    bank = PS[i % 2]
                                    if i >= 2:
                                        W(t, sV, ("p1ev", tag, i - 2))
                                    elif i == 0:
                                        W(t, sV, prev_ev)
                                    for c in range(NCH):
                                        ins = t.matmul(
                                            bank[:],
                                            w_half[:, c * 1024 + f * 128:
                                                   c * 1024 + (f + 1) * 128],
                                            xt[tt % 2][:, c * 512:(c + 1) * 512],
                                            start=(c == 0), stop=(c == NCH - 1))
                                    sT.inc(ins)
                                    M(("p1mm", tag, tt, f), sT)
                            M(("p1wfree", tag), sT)
                        env.sec("tensor", _)

        
                        prev_st = {("q", 0): None,
                                   ("q", 1): (("q", 0), 31),
                                   ("k", 0): (("q", 1), 31),
                                   ("k", 1): (("k", 0), 63)}[tag]

                        def _(v, ntt=ntt, tag=tag, prev_st=prev_st):
                            v.drain()
                            for tt in range(ntt):
                                for f in range(8):
                                    i = tt * 8 + f
                                    W(v, sT, ("p1mm", tag, tt, f))
                                    if i >= 2:
                                        W(v, sqk[i % 2], ("p1st", tag, i - 2))
                                    elif prev_st is not None:
                                        pt, pl = prev_st
                                        W(v, sqk[i % 2],
                                          ("p1st", pt, pl - 1 + i))
                                    ins = v.tensor_copy(pev[i % 2][:], PS[i % 2][:])
                                    sV.inc(ins)
                                    M(("p1ev", tag, i), sV)
                        env.sec("vector", _)

                        def _(g, ntt=ntt, tag=tag, dst_s=dst_s, half=half):
                            for tt in range(ntt):
                                for f in range(8):
                                    i = tt * 8 + f
                                    W(g, sV, ("p1ev", tag, i))
                                    ins = g.dma_start(
                                        out=dst_s[half * 8 + f, :,
                                                  tt * 512:(tt + 1) * 512],
                                        in_=pev[i % 2][:])
                                    sqk[i % 2].inc(ins, 16)
                                    M(("p1st", tag, i), sqk[i % 2])
                        env.sec("gpsimd", _)

            env.run(ph_p1)
            pes.close()

            wv_sb = psb("wv_sb", [128, NCH * 1024])
            xb = [psb(f"xb{i}", [128, NCH * 512]) for i in range(2)]
            pev = [psb(f"pev{i}b", [128, 512]) for i in range(2)]

            def ph_p1v():
                def _(s):
                    W(s, sT, ("p1wfree", ("k", 1)))
                    W(s, sqk[0], ("p1st", ("k", 1), 62))
                    W(s, sqk[1], ("p1st", ("k", 1), 63))
                    for half in range(2):
                        if half >= 1:
                            W(s, sT, ("p1vmm", half - 1, 7, 7))
                        for c in range(NCH):
                            ins = s.dma_start(
                                out=wv_sb[:, c * 1024:(c + 1) * 1024],
                                in_=wv[c * 128:(c + 1) * 128,
                                       half * 1024:(half + 1) * 1024])
                            slw.inc(ins, 16)
                        M(("p1vw", half), slw)
                        for tbg in range(8):
                            i = half * 8 + tbg
                            if i >= 2:
                                W(s, sT, ("p1vmm", (i - 2) // 8, (i - 2) % 8, 7))
                            for c in range(NCH):
                                ins = s.dma_start(
                                    out=xb[i % 2][:, c * 512:(c + 1) * 512],
                                    in_=xfm_s[c, :, tbg * 512:(tbg + 1) * 512])
                                slt[i % 2].inc(ins, 16)
                            M(("p1vx", half, tbg), slt[i % 2])
                env.sec("sync", _)

                def _(t):
                    for half in range(2):
                        W(t, slw, ("p1vw", half))
                        for tbg in range(8):
                            ig = half * 8 + tbg
                            W(t, slt[ig % 2], ("p1vx", half, tbg))
                            for sub in range(8):
                                tb4, dl = divmod(sub, 2)
                                i = ig * 8 + sub
                                bank = PS[i % 2]
                                if i >= 2:
                                    W(t, sV, ("p1vev", i - 2))
                                elif i == 0:
                                    W(t, sV, ("p1ev", ("k", 1), 63))
                                for c in range(NCH):
                                    ins = t.matmul(
                                        bank[:],
                                        xb[ig % 2][:, c * 512 + tb4 * 128:
                                                   c * 512 + (tb4 + 1) * 128],
                                        wv_sb[:, c * 1024 + dl * 512:
                                              c * 1024 + (dl + 1) * 512],
                                        start=(c == 0), stop=(c == NCH - 1))
                                sT.inc(ins)
                                M(("p1vmm", half, tbg, sub), sT)
                env.sec("tensor", _)

                def _(v):
                    v.drain()
                    for half in range(2):
                        for tbg in range(8):
                            for sub in range(8):
                                i = (half * 8 + tbg) * 8 + sub
                                W(v, sT, ("p1vmm", half, tbg, sub))
                                if i >= 2:
                                    W(v, spv[i % 2], ("p1vst", i - 2))
                                ins = v.tensor_copy(pev[i % 2][:], PS[i % 2][:])
                                sV.inc(ins)
                                M(("p1vev", i), sV)
                env.sec("vector", _)

                def _(g):
                    for half in range(2):
                        for tbg in range(8):
                            for sub in range(8):
                                tb4, dl = divmod(sub, 2)
                                tb = tbg * 4 + tb4
                                i = (half * 8 + tbg) * 8 + sub
                                W(g, sV, ("p1vev", i))
                                for hh in range(4):
                                    h = (half * 2 + dl) * 4 + hh
                                    ins = g.dma_start(
                                        out=v_s[h, :, tb * 128:(tb + 1) * 128],
                                        in_=pev[i % 2][:,
                                                       hh * 128:(hh + 1) * 128])
                                    spv[i % 2].inc(ins, 16)
                                M(("p1vst", i), spv[i % 2])
                env.sec("gpsimd", _)
            env.run(ph_p1v)
            pes.close()

            if stop_after == "p1":
                _finish(block, dma_sems)
                return nc

            st_a = psb("st_a", [1, 512])
            st_b = psb("st_b", [1, 512])
            st_c = psb("st_c", [1, 512])
            st_d = psb("st_d", [1, 512])
            st_e = psb("st_e", [1, 512])
            sqt = [psb(f"sqt{i}", [128, 512]) for i in range(2)]
            mb_b = psb("mb_b", [128, 512])
            yb_b = psb("yb_b", [128, 512])
            kh = [psb(f"kh{i}", [128, 4096]) for i in range(2)]
            vh = [psb(f"vh{i}", [128, 4096]) for i in range(2)]
            qh = [psb(f"qh{i}", [128, 512]) for i in range(2)]
            ptile = [psb(f"pt{i}", [128, 512]) for i in range(3)]
            stile = psb("stile", [128, 512])
            rzb = psb("rzb", [128, 512])
            o_all = psb("o_all", [128, HEADS * 512])
            x0r = psb("x0r", [128, NCH * 512])
            g8 = psb("g8", [8, 512])
            cf = psb("cf", [8, 512])
            x1tmr = [psb("x1tmr0", [128, 2048])] * 2
            dmask = psb("dmask", [128, 4 * 512])
            x1g = o_all
            wo_q = psb("wo_q", [128, NCH * 256])

            def ph_p2():
                def _(g):
                    ins = g.memset(dmask[:], 0.0)
                    sZ.inc(ins)
                    g.wait_ge(sZ.h, sZ.n)
                    for m in range(4):
                        ins = g.affine_select(
                            dmask[:, m * 512:(m + 1) * 512],
                            dmask[:, m * 512:(m + 1) * 512],
                            [[-1, 512]], ALU.is_ge, -FMAX,
                            base=128 * m - 1, channel_multiplier=1)
                        sZ.inc(ins)
                    M("p2dm", sZ)
                    g.wait_ge(sZ.h, sZ.n)
                env.sec("gpsimd", _)

                def prev_head(g, h):
                    k = g * HEADS + h - 1
                    return None if k < 0 else divmod(k, HEADS)

                ngrp = 1 if stop_after in ("p2heads", "p2g1") else NG
                for g in range(ngrp):
                    kbl = kbs_of(g)
                    ext = len(kbl)
                    diag = set(range(4 * g, 4 * g + 4))
                    diag_idx = [0, 1, 2, 3]

                    def _(s, g=g):
                        if g == 0:
                            W(s, spv[0], ("p1vst", 126))
                            W(s, spv[1], ("p1vst", 127))
                            W(s, sqk[0], ("p1st", ("k", 1), 62))
                            W(s, sqk[1], ("p1st", ("k", 1), 63))
                        for h in range(HEADS):
                            k_glob = g * HEADS + h
                            if k_glob >= 2:
                                pg, ph_ = divmod(k_glob - 2, HEADS)
                                W(s, sT, ("p2pe", pg, ph_))
                            slot = h % 2
                            ins = s.dma_start(
                                out=kh[slot][:, 0:(32 - 4 * g) * 128],
                                in_=k_s[h, :, 4 * g * 128:4096])
                            slk[slot].inc(ins, 16)
                            ins = s.dma_start(out=vh[slot][:],
                                              in_=v_s[h, :, :])
                            slk[slot].inc(ins, 16)
                            ins = s.dma_start(
                                out=qh[slot][:],
                                in_=q_s[h, :, g * 512:(g + 1) * 512])
                            slk[slot].inc(ins, 16)
                            M(("p2ld", g, h), slk[slot])
                        if g >= 1:
                            W(s, sV, ("p2x1", g - 1))
                        for c in range(NCH):
                            ins = s.dma_start(
                                out=x0r[:, c * 512:(c + 1) * 512],
                                in_=xfm_s[c, :, g * 512:(g + 1) * 512])
                            slx0.inc(ins, 16)
                        M(("p2x0", g), slx0)
                    env.sec("sync", _)

                    for h in range(HEADS):
                        slot = h % 2
                        ph = prev_head(g, h)

                        def _(t, g=g, h=h, slot=slot, ext=ext, ph=ph,
                              kbl=kbl):
                            W(t, slk[slot], ("p2ld", g, h))
                            if g == 0 and h == 0:
                                W(t, sV, ("p1vev", 127))
                            if g == 3:
                                if h == 0:
                                    W(t, sV, ("p2tmev", 2, 3))
                                else:
                                    W(t, sV, ("p2zfx", g, h - 1))
                                for kb in range(32):
                                    ins = t.matmul(
                                        PS[4][0:1, 0:128], ones_col[:],
                                        vh[slot][:, kb * 128:(kb + 1) * 128],
                                        start=(kb == 0), stop=(kb == 31))
                                sT.inc(ins)
                                M(("p2vsm", g, h), sT)
                            if ph is not None:
                                W(t, sV, ("p2osc",) + ph)
                            def zpv(j):
                                ins = t.matmul(
                                    PS[2][:],
                                    vh[slot][:, kbl[j] * 128:
                                             (kbl[j] + 1) * 128],
                                    ptile[j % 3][:],
                                    start=(j == 0),
                                    stop=(j == ext - 1 and g != 3))
                                sT.inc(ins)
                                M(("p2pv", g, h, j), sT)
                                ins = t.matmul(PS[5][0:1, :], ones_col[:],
                                               ptile[j % 3][:],
                                               start=(j == 0),
                                               stop=(j == ext - 1))
                                sT.inc(ins)
                                M(("p2zt", g, h, j), sT)

                            for i in range(ext):
                                if i >= 2:
                                    W(t, sA, ("p2exp", g, h, i - 2))
                                ins = t.matmul(PS[i % 2][:],
                                               kh[slot][:, i * 128:(i + 1) * 128],
                                               qh[slot][:],
                                               start=True, stop=True)
                                sT.inc(ins)
                                M(("p2smm", g, h, i), sT)
                                if i >= 1:
                                    W(t, sA, ("p2exp", g, h, i - 1))
                                    zpv(i - 1)
                            W(t, sA, ("p2exp", g, h, ext - 1))
                            zpv(ext - 1)
                            if g == 3:
                                W(t, sV, ("p2zfx", g, h))
                                ins = t.matmul(PS[2][:], vsum_sb[:], zflag[:],
                                               start=False, stop=True)
                                sT.inc(ins)
                                M(("p2corr", g, h), sT)
                            M(("p2pe", g, h), sT)
                            W(t, sV, ("p2rz", g, h))
                            if ph is not None:
                                W(t, sV, ("p2rzb",) + ph)
                            elif g == 0 and h == 0:
                                pass
                            if h == 0 and g >= 1:
                                W(t, sV, ("p2mbb", g - 1))
                            ins = t.matmul(PS[3][:], ones_row[:], rz[:],
                                           start=True, stop=True)
                            sT.inc(ins)
                            M(("p2bc", g, h), sT)
                        env.sec("tensor", _)

                        def _(v, g=g, h=h, ext=ext, diag_idx=diag_idx, ph=ph):
                            v.drain()
                            if g == 0 and h == 0:
                                W(v, sZ, "p2dm")
                            prev_d = None
                            for i in diag_idx:
                                W(v, sT, ("p2smm", g, h, i))
                                if prev_d is not None:
                                    W(v, sA, ("p2exp", g, h, prev_d))
                                elif ph is not None:
                                    W(v, sA, ("p2exp", ph[0], ph[1],
                                              4 * ph[0] + 3))
                                m = kbs_of(g)[i] - 4 * g
                                ins = v.scalar_tensor_tensor(
                                    stile[:], PS[i % 2][:], SCALE,
                                    dmask[:, m * 512:(m + 1) * 512],
                                    ALU.mult, ALU.add)
                                sV.inc(ins)
                                M(("p2stt", g, h, i), sV)
                                prev_d = i
                            if ph is not None:
                                W(v, sT, ("p2bc",) + ph)
                            W(v, sT, ("p2zt", g, h, ext - 1))
                            if g == 3:
                                if h >= 1:
                                    W(v, sT, ("p2corr", g, h - 1))
                                v.drain()
                                v.tensor_single_scalar(zflag[:], PS[5][0:1, :],
                                                       0.0, ALU.is_equal)
                                v.drain()
                                v.scalar_tensor_tensor(z1[:], zflag[:], 4096.0,
                                                       PS[5][0:1, :],
                                                       ALU.mult, ALU.add)
                                W(v, sT, ("p2vsm", g, h))
                                v.drain()
                                ins = v.tensor_copy(vsum_sb[:],
                                                    PS[4][0:1, 0:128])
                                sV.inc(ins)
                                M(("p2zfx", g, h), sV)
                                v.drain()
                                ins = v.reciprocal(rz[:], z1[:])
                            else:
                                v.drain()
                                ins = v.reciprocal(rz[:], PS[5][0:1, :])
                            sV.inc(ins)
                            M(("p2rz", g, h), sV)
                            W(v, sT, ("p2bc", g, h))
                            ins = v.tensor_copy(rzb[:], PS[3][:])
                            sV.inc(ins)
                            M(("p2rzb", g, h), sV)
                            W(v, sT, ("p2corr", g, h) if g == 3
                              else ("p2pv", g, h, ext - 1))
                            if g >= 1 and h == 0:
                                W(v, sT, ("p2wolast", g - 1))
                                W(v, sT, ("p2glg", g - 1))
                                W(v, sx1, ("p2x1st", g - 1))
                                W(v, sT, ("p2tmtr", g - 1, 3))
                            v.drain()
                            ins = v.tensor_tensor(
                                o_all[:, h * 512:(h + 1) * 512],
                                PS[2][:], rzb[:], ALU.mult)
                            sV.inc(ins)
                            M(("p2osc", g, h), sV)
                        env.sec("vector", _)

                        def _(a, g=g, h=h, ext=ext, diag=diag, ph=ph):
                            if g == 0 and h == 0:
                                W(a, sLc, "init_ld")
                            if ph is not None:
                                W(a, sT, ("p2pe",) + ph)
                            for i, kb in enumerate(kbs_of(g)):
                                if i >= 3:
                                    W(a, sT, ("p2zt", g, h, i - 3))
                                if kb in diag:
                                    W(a, sV, ("p2stt", g, h, i))
                                    ins = a.activation(ptile[i % 3][:], stile[:],
                                                       AF.Exp, bias=0.0,
                                                       scale=1.0)
                                else:
                                    W(a, sT, ("p2smm", g, h, i))
                                    ins = a.activation(
                                        ptile[i % 3][:], PS[i % 2][:], AF.Exp,
                                        bias=kmask_sb[:, kb:kb + 1], scale=SCALE)
                                sA.inc(ins)
                                M(("p2exp", g, h, i), sA)
                        env.sec("scalar", _)


                    if stop_after == "p2heads":
                        def _(gp, g=g):
                            for h in range(HEADS):
                                W(gp, sV, ("p2osc", g, h))
                                ins = gp.dma_start(
                                    out=dump[:, h * 512:(h + 1) * 512],
                                    in_=o_all[:, h * 512:(h + 1) * 512])
                                sx1.inc(ins, 16)
                            M(("p2dump", g), sx1)
                        env.sec("gpsimd", _)
                        break

                    def _(s, g=g):
                        for qq in range(8):
                            if qq == 0:
                                if g >= 1:
                                    W(s, sT, ("p2wolast", g - 1))
                            else:
                                W(s, sT, ("p2womm", g, 2 * qq - 1))
                            for c in range(NCH):
                                ins = s.dma_start(
                                    out=wo_q[:, c * 256:(c + 1) * 256],
                                    in_=wo[c * 128:(c + 1) * 128,
                                           qq * 256:(qq + 1) * 256])
                                slwo.inc(ins, 16)
                            M(("p2wold", g, qq), slwo)
                    env.sec("sync", _)

                    def stats(t, g, d):
                        W(t, sV, ("p2r", g, d))
                        W(t, sA, ("p2sq", g, d))
                        t.matmul(PS[6][0:1, :], ones_col[:],
                                 x0r[:, d * 512:(d + 1) * 512],
                                 start=(d == 0), stop=(d == 15))
                        ins = t.matmul(PS[7][0:1, :], ones_col[:],
                                       sqt[d % 2][:],
                                       start=(d == 0), stop=(d == 15))
                        sT.inc(ins)
                        M(("p2stats", g, d), sT)

                    def _(t, g=g):
                        W(t, sV, ("p2osc", g, HEADS - 1))
                        if g == 0:
                            W(t, sLc, "init_ld")
                        for qq in range(8):
                            W(t, slwo, ("p2wold", g, qq))
                            for dd in range(2):
                                d = qq * 2 + dd
                                bank = PS[4 + d % 2]
                                if d >= 2:
                                    W(t, sV, ("p2r", g, d - 2))
                                for h2 in range(HEADS):
                                    ins = t.matmul(
                                        bank[:],
                                        wo_q[:, h2 * 256 + dd * 128:
                                             h2 * 256 + (dd + 1) * 128],
                                        o_all[:, h2 * 512:(h2 + 1) * 512],
                                        start=(h2 == 0), stop=(h2 == HEADS - 1))
                                sT.inc(ins)
                                M(("p2womm", g, d), sT)
                                if d >= 1:
                                    stats(t, g, d - 1)
                        stats(t, g, 15)
                        M(("p2wolast", g), sT)
                        W(t, sV, ("p2lnt", g))
                        ins = t.matmul(PS[3][:], ones_row[:], st_a[:],
                                       start=True, stop=True)
                        sT.inc(ins)
                        M(("p2bcm", g), sT)
                        ins = t.matmul(PS[4][:], ones_row[:], st_d[:],
                                       start=True, stop=True)
                        sT.inc(ins)
                        M(("p2bcy", g), sT)
                        W(t, sV, ("p2x1", g))
                        for c in range(NCH):
                            ins = t.matmul(PS[6][0:8, :],
                                           gwT_sb[:, c * E:(c + 1) * E],
                                           x1g[:, c * 512:(c + 1) * 512],
                                           start=(c == 0), stop=(c == NCH - 1))
                        sT.inc(ins)
                        M(("p2glg", g), sT)
                        W(t, sV, ("p2g8", g))
                        for tt in range(4):
                            ins = t.transpose(PS[7][:, tt * 8:(tt + 1) * 8],
                                              g8[0:8, tt * 128:(tt + 1) * 128],
                                              ident[0:8, 0:8])
                        sT.inc(ins)
                        M(("p2gtr", g), sT)
                        for tt in range(4):
                            W(t, sV, ("p2comb", g, tt))
                            ins = t.transpose(PS[6][0:8,
                                                    tt * 128:(tt + 1) * 128],
                                              gcomb[:, tt * 8:(tt + 1) * 8],
                                              ident[:])
                        sT.inc(ins)
                        M(("p2ctr", g), sT)
                        for tb in range(4):
                            if tb == 0:
                                W(t, sV, ("p2cf", g))
                                if g >= 1:
                                    W(t, sV, ("p2tmev", g - 1, 3))
                            else:
                                W(t, sV, ("p2tmev", g, tb - 1))
                            for c in range(NCH):
                                ins = t.transpose(
                                    PS[4 + c // 4][:, (c % 4) * 128:
                                                   (c % 4 + 1) * 128],
                                    x1g[:, c * 512 + tb * 128:
                                        c * 512 + (tb + 1) * 128],
                                    ident[:])
                                if c % 4 == 3:
                                    sT.inc(ins)
                            M(("p2tmtr", g, tb), sT)
                    env.sec("tensor", _)

                    def _(a, g=g):
                        W(a, slx0, ("p2x0", g))
                        if g >= 1:
                            W(a, sT, ("p2stats", g - 1, 15))
                        for d in range(16):
                            W(a, sV, ("p2r", g, d))
                            if d >= 2:
                                W(a, sT, ("p2stats", g, d - 2))
                            ins = a.activation(sqt[d % 2][:],
                                               x0r[:, d * 512:(d + 1) * 512],
                                               AF.Square, bias=0.0, scale=1.0)
                            sA.inc(ins)
                            M(("p2sq", g, d), sA)
                        W(a, sV, ("p2veps", g))
                        ins = a.activation(st_c[:], st_b[:], AF.Sqrt,
                                           bias=0.0, scale=1.0)
                        sA.inc(ins)
                        M(("p2sqrt", g), sA)
                        for tt in range(4):
                            W(a, sV, ("p2gnm", g, tt))
                            if tt >= 1:
                                W(a, sV, ("p2comb", g, tt - 1))
                            ins = a.activation(gpr[:, tt * 8:(tt + 1) * 8],
                                               lg[:, tt * 8:(tt + 1) * 8],
                                               AF.Exp, bias=gnm[:], scale=1.0,
                                               accum_out=grs[:])
                            sA.inc(ins)
                            M(("p2gexp", g, tt), sA)
                    env.sec("scalar", _)

                    def _(v, g=g):
                        v.drain()
                        W(v, slx0, ("p2x0", g))
                        for d in range(16):
                            W(v, sT, ("p2womm", g, d))
                            ins = v.tensor_tensor(
                                x0r[:, d * 512:(d + 1) * 512],
                                PS[4 + d % 2][:],
                                x0r[:, d * 512:(d + 1) * 512], ALU.add)
                            sV.inc(ins)
                            M(("p2r", g, d), sV)
                        W(v, sT, ("p2stats", g, 15))
                        v.drain()
                        v.tensor_scalar_mul(st_a[:], PS[6][0:1, :], INVD)
                        ins = v.tensor_scalar_mul(st_b[:], PS[7][0:1, :], INVD)
                        sV.inc(ins)
                        v.drain()
                        v.tensor_mul(st_c[:], st_a[:], st_a[:])
                        v.drain()
                        v.tensor_sub(st_b[:], st_b[:], st_c[:])
                        v.drain()
                        ins = v.tensor_scalar_add(st_b[:], st_b[:], EPS)
                        sV.inc(ins)
                        M(("p2veps", g), sV)
                        W(v, sA, ("p2sqrt", g))
                        ins = v.reciprocal(st_d[:], st_c[:])
                        sV.inc(ins)
                        for _it in range(2):
                            v.drain()
                            v.tensor_mul(st_e[:], st_b[:], st_d[:])
                            v.drain()
                            v.tensor_mul(st_e[:], st_e[:], st_d[:])
                            v.drain()
                            v.tensor_scalar(st_e[:], st_e[:], -0.5, 1.5,
                                            ALU.mult, ALU.add)
                            v.drain()
                            v.tensor_mul(st_d[:], st_d[:], st_e[:])
                        ins = v.engine_nop()
                        sV.inc(ins)
                        M(("p2lnt", g), sV)
                        W(v, sT, ("p2bcm", g))
                        v.drain()
                        ins = v.tensor_copy(mb_b[:], PS[3][:])
                        sV.inc(ins)
                        M(("p2mbb", g), sV)
                        W(v, sT, ("p2bcy", g))
                        ins = v.tensor_copy(yb_b[:], PS[4][:])
                        sV.inc(ins)
                        M(("p2yb2", g), sV)
                        if g >= 1:
                            W(v, sx1, ("p2x1st", g - 1))
                        for d in range(16):
                            sl = slice(d * 512, (d + 1) * 512)
                            v.drain()
                            v.tensor_sub(x1g[:, sl], x0r[:, sl], mb_b[:])
                            v.drain()
                            ins = v.tensor_mul(x1g[:, sl], x1g[:, sl], yb_b[:])
                            sV.inc(ins)
                        M(("p2x1", g), sV)
                        W(v, sT, ("p2glg", g))
                        v.drain()
                        ins = v.tensor_copy(g8[:], PS[6][0:8, :])
                        sV.inc(ins)
                        M(("p2g8", g), sV)
                        W(v, sT, ("p2gtr", g))
                        ins = v.tensor_copy(lg[:], PS[7][:, 0:32])
                        sV.inc(ins)
                        for tt in range(4):
                            sl = slice(tt * 8, (tt + 1) * 8)
                            if tt == 0 and g >= 1:
                                W(v, stm, ("p2tmst", g - 1, 3))
                            if tt >= 1:
                                W(v, sA, ("p2gexp", g, tt - 1))
                            v.drain()
                            ins = v.reduce_max(gnm[:], lg[:, sl], axis=AX.X,
                                               negate=True)
                            sV.inc(ins)
                            M(("p2gnm", g, tt), sV)
                            W(v, sA, ("p2gexp", g, tt))
                            v.drain()
                            v.reciprocal(grs[:], grs[:])
                            v.drain()
                            v.tensor_scalar_mul(gpr[:, sl], gpr[:, sl], grs[:])
                            v.drain()
                            v.reduce_max(gm1[:, 0:1], gpr[:, sl], axis=AX.X)
                            v.drain()
                            v.match_replace(gp1[:], gm1[:], gpr[:, sl], 0.0)
                            v.drain()
                            v.reduce_max(gm2[:, 0:1], gp1[:], axis=AX.X)
                            v.drain()
                            v.match_replace(gp2[:], gm2[:], gp1[:], 0.0)
                            v.tensor_add(gden[:], gm1[:, 0:1], gm2[:, 0:1])
                            v.drain()
                            v.tensor_sub(gcomb[:, sl], gpr[:, sl], gp2[:])
                            v.reciprocal(gden[:], gden[:])
                            v.drain()
                            ins = v.tensor_scalar_mul(gcomb[:, sl],
                                                      gcomb[:, sl], gden[:])
                            sV.inc(ins)
                            M(("p2comb", g, tt), sV)
                        if g >= 1:
                            W(v, scf, ("p2cfst", g - 1))
                        W(v, sT, ("p2ctr", g))
                        v.drain()
                        ins = v.tensor_copy(cf[:], PS[6][0:8, :])
                        sV.inc(ins)
                        M(("p2cf", g), sV)
                        for tb in range(4):
                            W(v, sT, ("p2tmtr", g, tb))
                            if (g, tb) != (0, 0):
                                pb = (g, tb - 1) if tb else (g - 1, 3)
                                W(v, stm, ("p2tmst",) + pb)
                            v.drain()
                            for q4 in range(4):
                                ins = v.tensor_copy(
                                    x1tmr[tb % 2][:, q4 * 512:(q4 + 1) * 512],
                                    PS[4 + q4][:])
                            sV.inc(ins)
                            M(("p2tmev", g, tb), sV)
                    env.sec("vector", _)

                    def _(gp, g=g):
                        W(gp, sV, ("p2x1", g))
                        for c in range(NCH):
                            ins = gp.dma_start(
                                out=x1_s[c, :, g * 512:(g + 1) * 512],
                                in_=x1g[:, c * 512:(c + 1) * 512])
                            sx1.inc(ins, 16)
                        M(("p2x1st", g), sx1)
                        W(gp, sV, ("p2cf", g))
                        ins = gp.dma_start(out=cfm_s[:, g * 512:(g + 1) * 512],
                                           in_=cf[:])
                        scf.inc(ins, 16)
                        M(("p2cfst", g), scf)
                        for tb in range(4):
                            W(gp, sV, ("p2tmev", g, tb))
                            r0 = g * 512 + tb * 128
                            ins = gp.dma_start(
                                out=x1c_s[r0:r0 + 128, 0:2048],
                                in_=x1tmr[tb % 2][:])
                            stm.inc(ins, 16)
                            ins = gp.dma_start(
                                out=x1c_s[r0:r0 + 128, 2048:2056],
                                in_=gcomb[:, tb * 8:(tb + 1) * 8])
                            stm.inc(ins, 16)
                            M(("p2tmst", g, tb), stm)
                    env.sec("gpsimd", _)
            env.run(ph_p2)
            pes.close()

            if stop_after in ("p2heads", "p2g1"):
                _finish(block, dma_sems)
                return nc

            if stop_after == "p2":
                _finish(block, dma_sems)
                return nc

            iotaP = psb("iotaP", [16, 128])
            iota_row = psb("iota_row", [1, CAP])
            cnt2 = psb("cnt2", [1, 8])
            cntf = psb("cntf", [1, 8])
            cntb = psb("cntb", [16, 1])
            ivf = psb("ivf", [16, 128])
            crow = psb("crow", [1, CAP])
            cf16 = psb("cf16", [16, 128])
            self_ = psb("self_", [16, 128])
            idxf = psb("idxf", [16, 128])
            cnt_u = pes.enter_context(nc.sbuf_tensor("cnt_u", [1, 1], U32))
            idx16 = pes.enter_context(nc.sbuf_tensor("idx16", [16, 128], I16))
            idxr = pes.enter_context(nc.sbuf_tensor("idxr", [128, 8 * 48], I16))
            zrow = psb("zrow", [128, 512])
            xcg = [psb(f"xcg{i}", [128, 3 * XROW]) for i in range(2)]
            otm = psb("otm", [128, 3 * 2048])
            xg_fm = psb("xg_fm", [128, NCH * CAP])
            hpa = psb("hpa", [128, NHC * CAP])
            o_fm = psb("o_fm", [128, NCH * CAP])
            wgt = [psb(f"wgta{i}", [128, NCH * 128]) for i in range(2)]
            wut = [psb(f"wuta{i}", [128, NCH * 128]) for i in range(2)]
            wdt = [psb(f"wdta{i}", [128, 512]) for i in range(2)]
            sgl = psb("sgla", [128, CAP])
            cbig = psb("cbiga", [128, CAP])

            def ph_p3a():
                def _(g):
                    W(g, stm, ("p2tmst", NG - 1, 3))
                    W(g, sx1, ("p2x1st", NG - 1))
                    W(g, scf, ("p2cfst", NG - 1))
                    ins = g.iota(iotaP[:], [[16, 128]], base=1,
                                 channel_multiplier=1,
                                 allow_small_or_imprecise_dtypes=True)
                    sZ.inc(ins)
                    ins = g.iota(iota_row[:], [[1, CAP]], base=0,
                                 channel_multiplier=0,
                                 allow_small_or_imprecise_dtypes=True)
                    sZ.inc(ins)
                    ins = g.memset(cntf[:], 0.0)
                    sZ.inc(ins)
                    ins = g.memset(zrow[:], 0.0)
                    sZ.inc(ins)
                    ins = g.memset(xcg[0][:], 0.0)
                    sZ.inc(ins)
                    ins = g.memset(xcg[1][:], 0.0)
                    sZ.inc(ins)
                    M("a_const", sZ)
                    g.wait_ge(sZ.h, sZ.n)
                env.sec("gpsimd", _)

                for e in range(E):
                    def _(s, e=e):
                        if e == 0:
                            W(s, scf, ("p2cfst", NG - 1))
                            W(s, stm, ("p2tmst", NG - 1, 3))
                            W(s, sx1, ("p2x1st", NG - 1))
                        else:
                            W(s, sV, ("a_self", e - 1))
                            W(s, slmo, ("a_idx", e - 1))
                        for half in range(2):
                            ins = s.dma_start(
                                out=cf16[:, half * 64:(half + 1) * 64],
                                in_=cfm_s[e:e + 1, half * 1024:(half + 1) * 1024]
                                .rearrange("one (f p) -> (one p) f", p=16),
                                allow_slow_non_contiguous=True)
                            slmo.inc(ins, 16)
                        M(("a_cf", e), slmo)
                    env.sec("sync", _)

                    def _(v, e=e):
                        W(v, slmo, ("a_cf", e))
                        if e == 0:
                            W(v, sZ, "a_const")
                        else:
                            W(v, sZ, ("a_sg", e - 1))
                        v.drain()
                        v.tensor_single_scalar(self_[:], cf16[:], 0.0, ALU.is_gt)
                        v.drain()
                        v.tensor_mul(self_[:], self_[:], iotaP[:])
                        v.drain()
                        ins = v.tensor_scalar_add(self_[:], self_[:], -1.0)
                        sV.inc(ins)
                        M(("a_self", e), sV)
                    env.sec("vector", _)

                    def _(g, e=e):
                        if e == 0:
                            W(g, sZ, "a_const")
                            g.load_library(library_config.sparse_gather)
                        W(g, sV, ("a_self", e))
                        if e >= 1:
                            W(g, sV, ("a_psel", e - 1))
                        g.memset(idxf[:], -1.0)
                        g.drain()
                        ins = g.sparse_gather(idxf[:], self_[:],
                                              num_found=cnt_u[:])
                        sZ.inc(ins)
                        M(("a_sg", e), sZ)
                    env.sec("gpsimd", _)

                    def _(v, e=e):
                        W(v, sZ, ("a_sg", e))
                        if e >= 1:
                            W(v, slmo, ("a_idx", e - 1))
                        v.drain()
                        v.tensor_scalar_max(idxf[:], idxf[:], 0.0)
                        v.drain()
                        v.tensor_copy(idx16[:], idxf[:])
                        v.drain()
                        v.tensor_single_scalar(idx16[:], idx16[:], 0.0, ALU.max)
                        v.drain()
                        v.tensor_single_scalar(idx16[:], idx16[:], 2047.0,
                                               ALU.min)
                        v.drain()
                        v.tensor_copy(cntf[0:1, e:e + 1], cnt_u[:])
                        v.drain()
                        ins = v.tensor_scalar_add(cnt2[0:1, e:e + 1],
                                                  cntf[0:1, e:e + 1],
                                                  -float(CAP))
                        sV.inc(ins)
                        M(("a_post", e), sV)
                    env.sec("vector", _)

                    def _(t, e=e):
                        W(t, sV, ("a_post", e))
                        ins = t.matmul(PS[0][0:16, 0:1], ones_row[0:1, 0:16],
                                       cntf[0:1, e:e + 1],
                                       start=True, stop=True)
                        sT.inc(ins)
                        M(("a_cbm", e), sT)
                    env.sec("tensor", _)

                    def _(v, e=e):
                        W(v, sT, ("a_cbm", e))
                        v.drain()
                        v.tensor_copy(cntb[:], PS[0][0:16, 0:1])
                        v.drain()
                        v.tensor_copy(self_[:], idx16[:])
                        v.tensor_scalar(idxf[:], iotaP[:], cntb[:], None,
                                        ALU.is_le)
                        v.tensor_scalar(ivf[:], iotaP[:], cntb[:], None,
                                        ALU.is_gt)
                        v.drain()
                        v.tensor_mul(self_[:], self_[:], idxf[:])
                        v.tensor_single_scalar(ivf[:], ivf[:], float(TOWN),
                                               ALU.mult)
                        v.drain()
                        v.tensor_add(self_[:], self_[:], ivf[:])
                        v.drain()
                        ins = v.tensor_copy(idx16[:], self_[:])
                        sV.inc(ins)
                        M(("a_psel", e), sV)
                    env.sec("vector", _)

                    def _(s, e=e):
                        W(s, sV, ("a_psel", e))
                        for r in range(8):
                            ins = s.dma_start(
                                out=idxr[16 * r:16 * (r + 1),
                                         e * 48:(e + 1) * 48],
                                in_=idx16[:, 0:48])
                            slmo.inc(ins, 16)
                        M(("a_idx", e), slmo)
                    env.sec("sync", _)

                def _(g):
                    g.load_library(library_config.mlp)
                    for r in range(16):
                        for cc in range(4):
                            ins = g.dma_start(
                                out=moe_s[r * 128:(r + 1) * 128,
                                          cc * 512:(cc + 1) * 512],
                                in_=zrow[:])
                            ssc.inc(ins, 16)
                    for r in range(16):
                        ins = g.dma_start(
                            out=x1c_s[r * 128:(r + 1) * 128, 2056:XROW],
                            in_=zrow[:, 0:XROW - 2056])
                        ssc.inc(ins, 16)
                    for cc in range(4):
                        ins = g.dma_start(
                            out=moe_s[TOWN:TOWN + 16, cc * 512:(cc + 1) * 512],
                            in_=zrow[0:16, :])
                        ssc.inc(ins, 16)
                        ins = g.dma_start(
                            out=x1c_s[TOWN:TOWN + 16, cc * 512:(cc + 1) * 512],
                            in_=zrow[0:16, :])
                        ssc.inc(ins, 16)
                    ins = g.dma_start(out=x1c_s[TOWN:TOWN + 16, 2048:XROW],
                                      in_=zrow[0:16, 0:XROW - 2048])
                    ssc.inc(ins, 16)
                    M("a_zero", ssc)
                env.sec("gpsimd", _)

                NK = 2 * E
                for e in range(E):
                    for ch in range(2):
                        k = e * 2 + ch
                        isl = slice(e * 48 + ch * 24, e * 48 + ch * 24 + 24)

                        def _(g, e=e, ch=ch, k=k, isl=isl):
                            if k == 0:
                                W(g, slmo, ("a_idx", E - 1))
                                W(g, ssc, "a_zero")
                                W(g, stm, ("p2tmst", NG - 1, 3))
                            elif k >= 2:
                                W(g, sT, ("a_xtr", k - 2, 47))
                            ins = g.dma_gather(
                                xcg[k % 2].rearrange("p (b r) -> p b r",
                                                     r=XROW),
                                x1c_s[:, :], idxr[:, isl], CAP, CAP, XROW)
                            sgt[k % 2].inc(ins, 16)
                            M(("a_g", k), sgt[k % 2])
                        env.sec("gpsimd", _)

                        if k >= 1:
                            pisl = slice((k - 1) // 2 * 48
                                         + ((k - 1) % 2) * 24,
                                         (k - 1) // 2 * 48
                                         + ((k - 1) % 2) * 24 + 24)

                            def _(g, k=k - 1, isl=pisl):
                                W(g, sV, ("a_otev", k))
                                if k >= 1:
                                    W(g, ssc, ("a_sc", k - 1))
                                ins = g.dma_scatter_add(
                                    moe_s[:, :],
                                    otm.rearrange("p (b r) -> p b r", r=2048),
                                    idxr[:, isl], CAP, CAP, 2048)
                                ssc.inc(ins, 16)
                                M(("a_sc", k), ssc)
                            env.sec("gpsimd", _)

                        def _(t, e=e, k=k):
                            W(t, sgt[k % 2], ("a_g", k))
                            if k >= 1:
                                W(t, sV, ("a_cbe", k - 1))
                            for b in range(3):
                                ins = t.transpose(
                                    PS[5][0:1, b * 128:(b + 1) * 128],
                                    xcg[k % 2][:, b * XROW + 2048 + e:
                                               b * XROW + 2049 + e],
                                    ident[:])
                                sT.inc(ins)
                            M(("a_crt", k), sT)
                        env.sec("tensor", _)

                        def _(t, k=k):
                            W(t, sgt[k % 2], ("a_g", k))
                            for i in range(48):
                                b, c = divmod(i, 16)
                                if i >= 4:
                                    W(t, sV, ("a_xev", k, i - 4))
                                ins = t.transpose(
                                    PS[i % 4][:, ((i // 4) % 4) * 128:
                                              ((i // 4) % 4 + 1) * 128],
                                    xcg[k % 2][:, b * XROW + c * 128:
                                               b * XROW + (c + 1) * 128],
                                    ident[:])
                                sT.inc(ins)
                                M(("a_xtr", k, i), sT)
                        env.sec("tensor", _)

                        def _(v, k=k):
                            for i in range(48):
                                b, c = divmod(i, 16)
                                W(v, sT, ("a_xtr", k, i))
                                ins = v.tensor_copy(
                                    xg_fm[:, c * CAP + b * 128:
                                          c * CAP + (b + 1) * 128],
                                    PS[i % 4][:, ((i // 4) % 4) * 128:
                                              ((i // 4) % 4 + 1) * 128])
                                sV.inc(ins)
                                M(("a_xev", k, i), sV)
                            W(v, sT, ("a_crt", k))
                            v.drain()
                            csrc = cntf if ch == 0 else cnt2
                            v.tensor_scalar(crow[:], iota_row[0:1, 0:CAP],
                                            csrc[0:1, k // 2:k // 2 + 1], None,
                                            ALU.is_lt)
                            v.drain()
                            ins = v.tensor_tensor(crow[:], PS[5][0:1, 0:CAP],
                                                  crow[:], ALU.mult)
                            sV.inc(ins)
                            M(("a_crm", k), sV)
                        env.sec("vector", _)

                        def _(t, k=k):
                            W(t, sV, ("a_crm", k))
                            ins = t.matmul(PS[5][:, 0:CAP], ones_row[:],
                                           crow[:], start=True, stop=True)
                            sT.inc(ins)
                            M(("a_cbc", k), sT)
                        env.sec("tensor", _)

                        def _(v, k=k):
                            W(v, sT, ("a_cbc", k))
                            v.drain()
                            ins = v.tensor_copy(cbig[:], PS[5][:, 0:CAP])
                            sV.inc(ins)
                            M(("a_cbe", k), sV)
                        env.sec("vector", _)

                        def _(s, e=e, k=k):
                            for hc in range(NHC):
                                m = k * NHC + hc
                                if m == 0 and k == 0:
                                    W(s, stm, ("p2tmst", NG - 1, 3))
                                    W(s, sx1, ("p2x1st", NG - 1))
                                    W(s, scf, ("p2cfst", NG - 1))
                                if m >= 2:
                                    pk = divmod(m - 2, NHC)
                                    W(s, sT, ("a_gu", pk[0], pk[1]))
                                ins = s.dma_start(
                                    out=wgt[hc % 2][:],
                                    in_=wg[e][:, hc * D:(hc + 1) * D])
                                slgu[hc % 2].inc(ins, 16)
                                ins = s.dma_start(
                                    out=wut[hc % 2][:],
                                    in_=wu[e][:, hc * D:(hc + 1) * D])
                                slgu[hc % 2].inc(ins, 16)
                                M(("a_wld", k, hc), slgu[hc % 2])
                        env.sec("sync", _)

                        def _(t, k=k):
                            for hc in range(NHC):
                                W(t, slgu[hc % 2], ("a_wld", k, hc))
                                m = k * NHC + hc
                                if hc == 0:
                                    W(t, sV, ("a_xev", k, 47))
                                if m >= 2:
                                    pk = divmod(m - 2, NHC)
                                    W(t, sA, ("a_si",) + pk)
                                if m >= 1:
                                    pk = divmod(m - 1, NHC)
                                    W(t, sV, ("a_t1",) + pk)
                                for c in range(NCH):
                                    ins = t.matmul(
                                        PS[2 + hc % 2][:, 0:CAP],
                                        wgt[hc % 2][:, c * 128:(c + 1) * 128],
                                        xg_fm[:, c * CAP:(c + 1) * CAP],
                                        start=(c == 0), stop=(c == NCH - 1))
                                sT.inc(ins)
                                for c in range(NCH):
                                    ins = t.matmul(
                                        PS[4][:, 0:CAP],
                                        wut[hc % 2][:, c * 128:(c + 1) * 128],
                                        xg_fm[:, c * CAP:(c + 1) * CAP],
                                        start=(c == 0), stop=(c == NCH - 1))
                                sT.inc(ins)
                                M(("a_gu", k, hc), sT)
                        env.sec("tensor", _)

                        def _(a, k=k):
                            for hc in range(NHC):
                                W(a, sT, ("a_gu", k, hc))
                                m = k * NHC + hc
                                if m >= 1:
                                    pk = divmod(m - 1, NHC)
                                    W(a, sV, ("a_t1",) + pk)
                                ins = a.activation(sgl[:], PS[2 + hc % 2][:, 0:CAP],
                                                   AF.Silu, bias=0.0, scale=1.0)
                                sA.inc(ins)
                                M(("a_si", k, hc), sA)
                        env.sec("scalar", _)

                        def _(v, k=k):
                            for hc in range(NHC):
                                W(v, sA, ("a_si", k, hc))
                                if hc == 0 and k >= 1:
                                    W(v, sT, ("a_dn", k - 1))
                                v.drain()
                                v.tensor_tensor(sgl[:], sgl[:], PS[4][:, 0:CAP],
                                                ALU.mult)
                                v.drain()
                                ins = v.tensor_tensor(
                                    hpa[:, hc * CAP:(hc + 1) * CAP],
                                    sgl[:], cbig[:], ALU.mult)
                                sV.inc(ins)
                                M(("a_t1", k, hc), sV)
                        env.sec("vector", _)

                        def _(s, e=e, k=k):
                            for dgrp in range(4):
                                for hc in range(NHC):
                                    md = (k * 4 + dgrp) * NHC + hc
                                    if md == 0 and k == 0:
                                        W(s, stm, ("p2tmst", NG - 1, 3))
                                        W(s, sx1, ("p2x1st", NG - 1))
                                        W(s, scf, ("p2cfst", NG - 1))
                                    if md >= 2:
                                        if hc >= 2:
                                            pk = ("a_dnm", k, dgrp, hc - 2)
                                        elif dgrp >= 1:
                                            pk = ("a_dnm", k, dgrp - 1,
                                                  NHC - 1 - hc)
                                        else:
                                            pk = ("a_dnm", k - 1, 3,
                                                  NHC - 1 - hc)
                                        W(s, sT, pk)
                                    ins = s.dma_start(
                                        out=wdt[hc % 2][:],
                                        in_=wd[e][hc * 128:(hc + 1) * 128,
                                                  dgrp * 512:(dgrp + 1) * 512])
                                    sld[hc % 2].inc(ins, 16)
                                    M(("a_dld", k, dgrp, hc), sld[hc % 2])
                        env.sec("sync", _)

                        def _(t, k=k):
                            for dgrp in range(4):
                                for hc in range(NHC):
                                    W(t, sld[hc % 2], ("a_dld", k, dgrp, hc))
                                    W(t, sV, ("a_t1", k, hc))
                                    if hc == 0:
                                        if dgrp == 0:
                                            W(t, sV, ("a_t1", k, NHC - 1))
                                            if k >= 1:
                                                W(t, sV, ("a_ov", k - 1, 3))
                                        else:
                                            W(t, sV, ("a_ov", k, dgrp - 1))
                                    for dj in range(4):
                                        ins = t.matmul(
                                            PS[4 + dj][:, 0:CAP],
                                            wdt[hc % 2][:, dj * 128:
                                                        (dj + 1) * 128],
                                            hpa[:, hc * CAP:(hc + 1) * CAP],
                                            start=(hc == 0), stop=(hc == NHC - 1))
                                    sT.inc(ins)
                                    M(("a_dnm", k, dgrp, hc), sT)
                            M(("a_dn", k), sT)
                        env.sec("tensor", _)

                        def _(v, k=k):
                            for dgrp in range(4):
                                W(v, sT, ("a_dnm", k, dgrp, NHC - 1))
                                v.drain()
                                for dj in range(4):
                                    d = dgrp * 4 + dj
                                    ins = v.tensor_copy(
                                        o_fm[:, d * CAP:(d + 1) * CAP],
                                        PS[4 + dj][:, 0:CAP])
                                    sV.inc(ins)
                                M(("a_ov", k, dgrp), sV)
                        env.sec("vector", _)

                        def _(t, k=k):
                            for i in range(48):
                                c, b = divmod(i, 3)
                                W(t, sV, ("a_ov", k, c // 4))
                                if i >= 4:
                                    W(t, sV, ("a_otv", k, i - 4))
                                ins = t.transpose(
                                    PS[i % 4][:, ((i // 4) % 4) * 128:
                                              ((i // 4) % 4 + 1) * 128],
                                    o_fm[:, c * CAP + b * 128:
                                         c * CAP + (b + 1) * 128],
                                    ident[:])
                                sT.inc(ins)
                                M(("a_otr", k, i), sT)
                        env.sec("tensor", _)

                        def _(v, k=k):
                            if k >= 1:
                                W(v, ssc, ("a_sc", k - 1))
                            for i in range(48):
                                c, b = divmod(i, 3)
                                W(v, sT, ("a_otr", k, i))
                                ins = v.tensor_copy(
                                    otm[:, b * 2048 + c * 128:
                                        b * 2048 + (c + 1) * 128],
                                    PS[i % 4][:, ((i // 4) % 4) * 128:
                                              ((i // 4) % 4 + 1) * 128])
                                sV.inc(ins)
                                M(("a_otv", k, i), sV)
                            ins = v.engine_nop()
                            sV.inc(ins)
                            M(("a_otev", k), sV)
                        env.sec("vector", _)

                        if k == NK - 1:
                            def _(g, k=k, isl=isl):
                                W(g, sV, ("a_otev", k))
                                W(g, ssc, ("a_sc", k - 1))
                                ins = g.dma_scatter_add(
                                    moe_s[:, :],
                                    otm.rearrange("p (b r) -> p b r", r=2048),
                                    idxr[:, isl], CAP, CAP, 2048)
                                ssc.inc(ins, 16)
                                M(("a_sc", k), ssc)
                            env.sec("gpsimd", _)
            env.run(ph_p3a)
            pes.close()

            st_a = psb("st_a3", [1, 512])
            st_b = psb("st_b3", [1, 512])
            st_c = psb("st_c3", [1, 512])
            st_d = psb("st_d3", [1, 512])
            st_e = psb("st_e3", [1, 512])
            sg1 = psb("sg1", [1, 512])
            sqt = [psb(f"sqt3{i}", [128, 512]) for i in range(2)]
            mb_b = psb("mb_b3", [128, 512])
            yb_b = psb("yb_b3", [128, 512])
            x1t = psb("x1t", [128, NCH * 512])
            cb = psb("cb", [128, 512])
            wgt = [psb(f"wgt{i}", [128, NCH * 128]) for i in range(2)]
            wut = [psb(f"wut{i}", [128, NCH * 128]) for i in range(2)]
            wdt = [psb(f"wdt{i}", [128, 512]) for i in range(2)]
            sgl = [psb("sgl0", [128, 512])] * 2
            t1b = [psb("t1b0", [128, 512])] * 2
            hp = psb("hp", [128, NHC * 512])
            acc = psb("acc", [128, NCH * 512])
            orow = psb("orow", [128, D])
            moet = psb("moet", [128, 2048])

            def ph_p3b():
                for t_i in range(NG):
                    def _(s, t_i=t_i):
                        if t_i == 0:
                            W(s, sx1, ("p2x1st", NG - 1))
                            W(s, ssc, ("a_sc", 2 * E - 1))
                        else:
                            W(s, sT, ("p3otr", t_i - 1, 3))
                        for c in range(NCH):
                            ins = s.dma_start(
                                out=x1t[:, c * 512:(c + 1) * 512],
                                in_=x1_s[c, :, t_i * 512:(t_i + 1) * 512])
                            slx1.inc(ins, 16)
                        M(("p3ld", t_i), slx1)
                    env.sec("sync", _)

                    def _(t, t_i=t_i):
                        W(t, slx1, ("p3ld", t_i))
                        if t_i == 0:
                            W(t, sLc, "init_ld")
                            W(t, sV, ("a_otev", 2 * E - 1))
                        else:
                            W(t, sV, ("p3oev", t_i - 1, 3))
                            W(t, sA, ("p3sig", t_i - 1))
                        for c in range(NCH):
                            ins = t.matmul(PS[6][0:1, :],
                                           sgw_sb[:, c:c + 1],
                                           x1t[:, c * 512:(c + 1) * 512],
                                           start=(c == 0), stop=(c == NCH - 1))
                        sT.inc(ins)
                        M(("p3sgm", t_i), sT)
                        W(t, sA, ("p3sig", t_i))
                        if t_i >= 1:
                            W(t, sV, ("p3cb", t_i - 1))
                        ins = t.matmul(PS[3][:], ones_row[:], sg1[:],
                                       start=True, stop=True)
                        sT.inc(ins)
                        M(("p3cbm", t_i), sT)
                    env.sec("tensor", _)

                    def _(a, t_i=t_i):
                        W(a, sT, ("p3sgm", t_i))
                        ins = a.activation(sg1[:], PS[6][0:1, :], AF.Sigmoid,
                                           bias=0.0, scale=1.0)
                        sA.inc(ins)
                        M(("p3sig", t_i), sA)
                    env.sec("scalar", _)

                    def _(v, t_i=t_i):
                        v.drain()
                        if t_i >= 1:
                            W(v, sV, ("p3t1", t_i - 1, 9, NHC - 1))
                        W(v, sT, ("p3cbm", t_i))
                        ins = v.tensor_copy(cb[:], PS[3][:])
                        sV.inc(ins)
                        M(("p3cb", t_i), sV)
                    env.sec("vector", _)

                    for e in (8, 9):
                        nhc = NHC
                        hoff = (e - 8) * NHC

                        def _(s, t_i=t_i, e=e, hoff=hoff):
                            for hc in range(nhc):
                                m = (t_i * 2 + (e - 8)) * NHC + hc
                                if m >= 2:
                                    pt, r = divmod(m - 2, 2 * NHC)
                                    pe, phc = divmod(r, NHC)
                                    W(s, sT, ("p3gu", pt, pe + 8, phc))
                                ch = hoff + hc
                                ins = s.dma_start(out=wgt[hc % 2][:],
                                                  in_=shg[:, ch * D:(ch + 1) * D])
                                slgu[hc % 2].inc(ins, 16)
                                ins = s.dma_start(out=wut[hc % 2][:],
                                                  in_=shu[:, ch * D:(ch + 1) * D])
                                slgu[hc % 2].inc(ins, 16)
                                M(("p3wld", t_i, e, hc), slgu[hc % 2])
                        env.sec("sync", _)

                        def _(t, t_i=t_i, e=e):
                            for hc in range(nhc):
                                W(t, slgu[hc % 2], ("p3wld", t_i, e, hc))
                                m = (t_i * 2 + (e - 8)) * NHC + hc
                                if e == 8 and hc == 0:
                                    W(t, sV, ("p3cb", t_i))
                                    if t_i == 0:
                                        W(t, sV, ("a_otev", 2 * E - 1))
                                    else:
                                        W(t, sV, ("p3lnt", t_i - 1))
                                if m >= 2:
                                    pt, r = divmod(m - 2, 2 * NHC)
                                    pe, phc = divmod(r, NHC)
                                    W(t, sA, ("p3silu", pt, pe + 8, phc))
                                    W(t, sV, ("p3t1", pt, pe + 8, phc))
                                for c in range(NCH):
                                    ins = t.matmul(
                                        PS[0 + hc % 2][:],
                                        wgt[hc % 2][:, c * 128:(c + 1) * 128],
                                        x1t[:, c * 512:(c + 1) * 512],
                                        start=(c == 0), stop=(c == NCH - 1))
                                sT.inc(ins)
                                for c in range(NCH):
                                    ins = t.matmul(
                                        PS[2 + hc % 2][:],
                                        wut[hc % 2][:, c * 128:(c + 1) * 128],
                                        x1t[:, c * 512:(c + 1) * 512],
                                        start=(c == 0), stop=(c == NCH - 1))
                                sT.inc(ins)
                                M(("p3gu", t_i, e, hc), sT)
                        env.sec("tensor", _)

                        def _(a, t_i=t_i, e=e):
                            for hc in range(nhc):
                                W(a, sT, ("p3gu", t_i, e, hc))
                                m = (t_i * 2 + (e - 8)) * NHC + hc
                                if m >= 1:
                                    pt, r = divmod(m - 1, 2 * NHC)
                                    pe, phc = divmod(r, NHC)
                                    W(a, sV, ("p3t1", pt, pe + 8, phc))
                                ins = a.activation(sgl[hc % 2][:],
                                                   PS[0 + hc % 2][:], AF.Silu,
                                                   bias=0.0, scale=1.0)
                                sA.inc(ins)
                                M(("p3silu", t_i, e, hc), sA)
                        env.sec("scalar", _)

                        def _(v, t_i=t_i, e=e):
                            v.drain()
                            for hc in range(nhc):
                                W(v, sA, ("p3silu", t_i, e, hc))
                                m = (t_i * 2 + (e - 8)) * NHC + hc
                                if m >= 1:
                                    pt, r = divmod(m - 1, 2 * NHC)
                                    pe, phc = divmod(r, NHC)
                                    W(v, sV, ("p3hp", pt, pe + 8, phc))
                                v.drain()
                                v.tensor_tensor(t1b[hc % 2][:],
                                                sgl[hc % 2][:],
                                                PS[2 + hc % 2][:], ALU.mult)
                                v.drain()
                                ins = v.tensor_tensor(
                                    hp[:, hc * 512:(hc + 1) * 512],
                                    t1b[hc % 2][:], cb[:], ALU.mult)
                                sV.inc(ins)
                                M(("p3t1", t_i, e, hc), sV)
                                env.marks[("p3hp", t_i, e, hc)] =                                     env.marks[("p3t1", t_i, e, hc)]
                        env.sec("vector", _)

                        def _(s, t_i=t_i, e=e, hoff=hoff):
                            for dgrp in range(4):
                                for hc in range(nhc):
                                    md = ((t_i * 2 + (e - 8)) * 4 + dgrp) * NHC + hc
                                    pk = None
                                    if hc >= 2:
                                        pk = ("p3dnm", t_i, e, dgrp, hc - 2)
                                    elif dgrp >= 1:
                                        pk = ("p3dnm", t_i, e, dgrp - 1,
                                              nhc - 1 - hc)
                                    elif md >= 2:
                                        pt, pe = (t_i, 8) if e == 9 else                                             (t_i - 1, 9)
                                        pk = ("p3dnm", pt, pe, 3, nhc - 1 - hc)
                                    if pk is not None:
                                        W(s, sT, pk)
                                    ch = hoff + hc
                                    ins = s.dma_start(
                                        out=wdt[hc % 2][:],
                                        in_=shd[ch * 128:(ch + 1) * 128,
                                                dgrp * 512:(dgrp + 1) * 512])
                                    sld[hc % 2].inc(ins, 16)
                                    M(("p3dld", t_i, e, dgrp, hc), sld[hc % 2])
                        env.sec("sync", _)

                        def _(t, t_i=t_i, e=e):
                            for dgrp in range(4):
                                for hc in range(nhc):
                                    W(t, sld[hc % 2], ("p3dld", t_i, e, dgrp, hc))
                                    W(t, sV, ("p3hp", t_i, e, hc))
                                    if hc == 0:
                                        if dgrp > 0:
                                            W(t, sV, ("p3acc", t_i, e, dgrp - 1))
                                        elif e == 9:
                                            W(t, sV, ("p3acc", t_i, 8, 3))
                                        elif t_i > 0:
                                            W(t, sV, ("p3acc", t_i - 1, 9, 3))
                                            W(t, sV, ("p3oev", t_i - 1, 3))
                                    for dj in range(4):
                                        ins = t.matmul(
                                            PS[4 + dj][:],
                                            wdt[hc % 2][:, dj * 128:
                                                        (dj + 1) * 128],
                                            hp[:, hc * 512:(hc + 1) * 512],
                                            start=(hc == 0),
                                            stop=(hc == nhc - 1))
                                    sT.inc(ins)
                                    M(("p3dnm", t_i, e, dgrp, hc), sT)
                            M(("p3dn", t_i, e), sT)
                        env.sec("tensor", _)

                        def _(v, t_i=t_i, e=e):
                            for dgrp in range(4):
                                W(v, sT, ("p3dnm", t_i, e, dgrp, nhc - 1))
                                v.drain()
                                for dj in range(4):
                                    d = dgrp * 4 + dj
                                    sl = slice(d * 512, (d + 1) * 512)
                                    if e == 8:
                                        ins = v.tensor_copy(acc[:, sl],
                                                            PS[4 + dj][:])
                                    else:
                                        ins = v.tensor_tensor(acc[:, sl],
                                                              acc[:, sl],
                                                              PS[4 + dj][:],
                                                              ALU.add)
                                    sV.inc(ins)
                                M(("p3acc", t_i, e, dgrp), sV)
                        env.sec("vector", _)

                    def _(s, t_i=t_i):
                        for tb in range(4):
                            if t_i == 0 and tb == 0:
                                W(s, ssc, ("a_sc", 2 * E - 1))
                            if (t_i, tb) != (0, 0):
                                pb = (t_i, tb - 1) if tb else (t_i - 1, 3)
                                W(s, sT, ("p3mtr",) + pb)
                            ins = s.dma_start(
                                out=moet[:],
                                in_=moe_s[t_i * 512 + tb * 128:
                                          t_i * 512 + (tb + 1) * 128, :])
                            slmo.inc(ins, 16)
                            M(("p3mld", t_i, tb), slmo)
                    env.sec("sync", _)

                    def _(t, t_i=t_i):
                        for tb in range(4):
                            W(t, slmo, ("p3mld", t_i, tb))
                            if tb == 0:
                                W(t, sV, ("p3acc", t_i, 9, 3))
                                if t_i >= 1:
                                    W(t, sV, ("p3ybb", t_i - 1))
                            else:
                                W(t, sV, ("p3mev", t_i, tb - 1))
                            for c in range(NCH):
                                if c >= 2:
                                    W(t, sV, ("p3mevc", t_i, tb, c - 2))
                                ins = t.transpose(
                                    PS[2 + c % 2][:, ((c // 2) % 4) * 128:
                                                  ((c // 2) % 4 + 1) * 128],
                                    moet[:, c * 128:(c + 1) * 128],
                                    ident[:])
                                sT.inc(ins)
                                M(("p3mtrc", t_i, tb, c), sT)
                            M(("p3mtr", t_i, tb), sT)
                    env.sec("tensor", _)

                    def _(v, t_i=t_i):
                        for tb in range(4):
                            for c in range(NCH):
                                W(v, sT, ("p3mtrc", t_i, tb, c))
                                if c == 0:
                                    v.drain()
                                ins = v.tensor_tensor(
                                    acc[:, c * 512 + tb * 128:
                                        c * 512 + (tb + 1) * 128],
                                    acc[:, c * 512 + tb * 128:
                                        c * 512 + (tb + 1) * 128],
                                    PS[2 + c % 2][:, ((c // 2) % 4) * 128:
                                                  ((c // 2) % 4 + 1) * 128],
                                    ALU.add)
                                sV.inc(ins)
                                M(("p3mevc", t_i, tb, c), sV)
                            M(("p3mev", t_i, tb), sV)
                    env.sec("vector", _)

                    def _(v, t_i=t_i):
                        W(v, sV, ("p3mev", t_i, 3))
                        for d in range(16):
                            sl = slice(d * 512, (d + 1) * 512)
                            v.drain()
                            ins = v.tensor_tensor(acc[:, sl], acc[:, sl],
                                                  x1t[:, sl], ALU.add)
                            sV.inc(ins)
                            M(("p3r2", t_i, d), sV)
                    env.sec("vector", _)

                    def _(a, t_i=t_i):
                        if t_i == 0:
                            W(a, sT, ("p2stats", NG - 1, 15))
                        else:
                            W(a, sT, ("p3stats", t_i - 1, 15))
                        for d in range(16):
                            W(a, sV, ("p3r2", t_i, d))
                            if d >= 2:
                                W(a, sT, ("p3stats", t_i, d - 2))
                            ins = a.activation(sqt[d % 2][:],
                                               acc[:, d * 512:(d + 1) * 512],
                                               AF.Square, bias=0.0, scale=1.0)
                            sA.inc(ins)
                            M(("p3sq", t_i, d), sA)
                        W(a, sV, ("p3veps", t_i))
                        ins = a.activation(st_c[:], st_b[:], AF.Sqrt,
                                           bias=0.0, scale=1.0)
                        sA.inc(ins)
                        M(("p3sqrt", t_i), sA)
                    env.sec("scalar", _)

                    def _(t, t_i=t_i):
                        for d in range(16):
                            W(t, sV, ("p3r2", t_i, d))
                            W(t, sA, ("p3sq", t_i, d))
                            t.matmul(PS[0][0:1, :], ones_col[:],
                                     acc[:, d * 512:(d + 1) * 512],
                                     start=(d == 0), stop=(d == 15))
                            ins = t.matmul(PS[1][0:1, :], ones_col[:],
                                           sqt[d % 2][:],
                                           start=(d == 0), stop=(d == 15))
                            sT.inc(ins)
                            M(("p3stats", t_i, d), sT)
                        W(t, sV, ("p3lnt", t_i))
                        ins = t.matmul(PS[2][:], ones_row[:], st_a[:],
                                       start=True, stop=True)
                        sT.inc(ins)
                        M(("p3bcm", t_i), sT)
                        ins = t.matmul(PS[3][:], ones_row[:], st_d[:],
                                       start=True, stop=True)
                        sT.inc(ins)
                        M(("p3bcy", t_i), sT)
                        W(t, sV, ("p3xn", t_i))
                        for tb in range(4):
                            if tb >= 1:
                                W(t, sV, ("p3oev", t_i, tb - 1))
                            for c in range(NCH):
                                ins = t.transpose(
                                    PS[4 + c // 4][:, (c % 4) * 128:
                                                   (c % 4 + 1) * 128],
                                    x1t[:, c * 512 + tb * 128:
                                        c * 512 + (tb + 1) * 128],
                                    ident[:])
                                if c % 4 == 3:
                                    sT.inc(ins)
                            M(("p3otr", t_i, tb), sT)
                    env.sec("tensor", _)

                    def _(v, t_i=t_i):
                        W(v, sT, ("p3stats", t_i, 15))
                        v.drain()
                        v.tensor_scalar_mul(st_a[:], PS[0][0:1, :], INVD)
                        ins = v.tensor_scalar_mul(st_b[:], PS[1][0:1, :], INVD)
                        sV.inc(ins)
                        v.drain()
                        v.tensor_mul(st_c[:], st_a[:], st_a[:])
                        v.drain()
                        v.tensor_sub(st_b[:], st_b[:], st_c[:])
                        v.drain()
                        ins = v.tensor_scalar_add(st_b[:], st_b[:], EPS)
                        sV.inc(ins)
                        M(("p3veps", t_i), sV)
                        W(v, sA, ("p3sqrt", t_i))
                        ins = v.reciprocal(st_d[:], st_c[:])
                        sV.inc(ins)
                        for _it in range(2):
                            v.drain()
                            v.tensor_mul(st_e[:], st_b[:], st_d[:])
                            v.drain()
                            v.tensor_mul(st_e[:], st_e[:], st_d[:])
                            v.drain()
                            v.tensor_scalar(st_e[:], st_e[:], -0.5, 1.5,
                                            ALU.mult, ALU.add)
                            v.drain()
                            v.tensor_mul(st_d[:], st_d[:], st_e[:])
                        ins = v.engine_nop()
                        sV.inc(ins)
                        M(("p3lnt", t_i), sV)
                        W(v, sT, ("p3bcm", t_i))
                        v.drain()
                        ins = v.tensor_copy(mb_b[:], PS[2][:])
                        sV.inc(ins)
                        W(v, sT, ("p3bcy", t_i))
                        ins = v.tensor_copy(yb_b[:], PS[3][:])
                        sV.inc(ins)
                        M(("p3ybb", t_i), sV)
                        W(v, sT, ("p3gu", t_i, 9, NHC - 1))
                        for d in range(16):
                            sl = slice(d * 512, (d + 1) * 512)
                            v.drain()
                            v.tensor_sub(x1t[:, sl], acc[:, sl], mb_b[:])
                            v.drain()
                            ins = v.tensor_mul(x1t[:, sl], x1t[:, sl], yb_b[:])
                            sV.inc(ins)
                        M(("p3xn", t_i), sV)
                        for tb in range(4):
                            W(v, sT, ("p3otr", t_i, tb))
                            if (t_i, tb) != (0, 0):
                                pb = (t_i, tb - 1) if tb >= 1 else (t_i - 1, 3)
                                W(v, sor, ("p3ost",) + pb)
                            v.drain()
                            for q4 in range(4):
                                ins = v.tensor_copy(
                                    orow[:, q4 * 512:(q4 + 1) * 512],
                                    PS[4 + q4][:])
                            sV.inc(ins)
                            M(("p3oev", t_i, tb), sV)
                    env.sec("vector", _)

                    def _(gp, t_i=t_i):
                        for tb in range(4):
                            W(gp, sV, ("p3oev", t_i, tb))
                            ins = gp.dma_start(
                                out=out[t_i * 512 + tb * 128:
                                        t_i * 512 + (tb + 1) * 128, :],
                                in_=orow[:])
                            sor.inc(ins, 16)
                            M(("p3ost", t_i, tb), sor)
                    env.sec("gpsimd", _)
            env.run(ph_p3b)
            pes.close()

            _finish(block, dma_sems)
    lower_extended_insts(nc)
    return nc


def _finish(block, store_sems):
    finals = [(s, s.n) for s in store_sems]

    def _(g):
        for s, n in finals:
            g.wait_ge(s.h, n)
    block.gpsimd(_)


def _in_maps(inputs):
    x = np.asarray(inputs["L0_attn_x"], dtype=np.float32)
    gwT = np.ascontiguousarray(np.asarray(inputs["L0_ffn_gate_w"],
                                          dtype=np.float32).T)
    sgw = np.ascontiguousarray(np.asarray(inputs["L0_ffn_shgate_w"],
                                          dtype=np.float32).reshape(D, 1))
    shared = {
        "wq": np.ascontiguousarray(np.asarray(inputs["L0_attn_Wq"], np.float32)),
        "wk": np.ascontiguousarray(np.asarray(inputs["L0_attn_Wk"], np.float32)),
        "wv": np.ascontiguousarray(np.asarray(inputs["L0_attn_Wv"], np.float32)),
        "wo": np.ascontiguousarray(np.asarray(inputs["L0_attn_Wo"], np.float32)),
        "gwT": gwT,
        "wg": np.ascontiguousarray(
            np.asarray(inputs["L0_ffn_w_gate"], np.float32)
            .reshape(E, NCH, 128, NHC, 128).transpose(0, 2, 3, 1, 4)
            .reshape(E, 128, NHC * D)),
        "wu": np.ascontiguousarray(
            np.asarray(inputs["L0_ffn_w_up"], np.float32)
            .reshape(E, NCH, 128, NHC, 128).transpose(0, 2, 3, 1, 4)
            .reshape(E, 128, NHC * D)),
        "wd": np.ascontiguousarray(np.asarray(inputs["L0_ffn_w_down"], np.float32)),
        "sgw": sgw,
        "shg": np.ascontiguousarray(
            np.asarray(inputs["L0_ffn_shw_gate"], np.float32)
            .reshape(NCH, 128, NSC, 128).transpose(1, 2, 0, 3)
            .reshape(128, NSC * D)),
        "shu": np.ascontiguousarray(
            np.asarray(inputs["L0_ffn_shw_up"], np.float32)
            .reshape(NCH, 128, NSC, 128).transpose(1, 2, 0, 3)
            .reshape(128, NSC * D)),
        "shd": np.ascontiguousarray(np.asarray(inputs["L0_ffn_shw_down"], np.float32)),
    }
    maps = []
    for c in range(8):
        b, hf = divmod(c, 2)
        own = x[b, hf * TOWN:(hf + 1) * TOWN, :]
        other = x[b, (1 - hf) * TOWN:(2 - hf) * TOWN, :]
        xin_np = np.ascontiguousarray(np.concatenate([own, other], axis=0))
        km = np.zeros((128, 32), dtype=np.float32)
        if hf == 1:
            km[:, 16:] = -FMAX
        m = dict(shared)
        m["xin"] = xin_np
        m["kmask"] = km
        maps.append(m)
    return maps


def kernel(**inputs):
    maps = _in_maps(inputs)
    nc = build_nc()
    res = run_bass_kernel_spmd(nc, maps, core_ids=list(range(8)))
    outp = np.empty((4, 2 * TOWN, D), dtype=np.float32)
    for c in range(8):
        b, hf = divmod(c, 2)
        outp[b, hf * TOWN:(hf + 1) * TOWN, :] = res.results[c]["out"]
    return outp


# version 13
# speedup vs baseline: 1.1431x; 1.0089x over previous
import contextlib
import numpy as np

import concourse.bass as bass
from concourse.library_overlay import lower_extended_insts
import concourse.mybir as mybir
from concourse.bass_utils import run_bass_kernel_spmd
from concourse import library_config

F32 = mybir.dt.float32
I16 = mybir.dt.int16
U32 = mybir.dt.uint32
AF = mybir.ActivationFunctionType
ALU = mybir.AluOpType
AX = mybir.AxisListType
FMAX = float(np.finfo(np.float32).max)

D = 2048
NCH = 16
TOWN = 2048
HEADS = 16
E = 8
HME = 1408
NHC = 11
HSH = 2816
NSC = 22
NG = 4
CAP = 384
XROW = 2112
SCALE = 1.0 / float(np.sqrt(np.float32(128)))
EPS = 1e-5
INVD = 1.0 / float(np.float32(D))


class Sem:
    def __init__(self, nc, es, name):
        self.h = es.enter_context(nc.semaphore(name))
        self.name = name
        self.n = 0

    def inc(self, instr, k=1):
        instr.then_inc(self.h, k)
        self.n += k
        return self.n


class _DummyInstr:
    def then_inc(self, h, k):
        return self


class _CountEngine:
    def __getattr__(self, name):
        def _f(*a, **k):
            return _DummyInstr()
        return _f


class Env:
    def __init__(self, block, sems):
        self.block = block
        self.sems = sems
        self.marks = {}
        self.counting = False

    def sec(self, engine_name, fn):
        if self.counting:
            fn(_CountEngine())
        else:
            getattr(self.block, engine_name)(fn)

    def wait(self, eng, sem, key):
        if not self.counting:
            nm, val = self.marks[key]
            assert nm == sem.name, ("wait sem mismatch", key, nm, sem.name)
            eng.wait_ge(sem.h, val)

    def mark(self, key, sem):
        if self.counting:
            assert key not in self.marks, ("duplicate mark", key)
            self.marks[key] = (sem.name, sem.n)
        else:
            assert self.marks.get(key) == (sem.name, sem.n), \
                (key, self.marks.get(key), sem.n)

    def run(self, fn):
        snap = [(s, s.n) for s in self.sems]
        self.counting = True
        fn()
        counted = [(s, s.n) for s in self.sems]
        for s, n in snap:
            s.n = n
        self.counting = False
        fn()
        for (s, n) in counted:
            assert s.n == n, (s, s.n, n)


def kbs_of(g):
    return list(range(4 * g, 32))


def _prev_hc(t_i, e, hc, back):
    gi = (t_i * 10 + e) * NHC + hc - back
    if gi < 0:
        return None
    q, hc = divmod(gi, NHC)
    t_i, e = divmod(q, 10)
    return (t_i, e, hc)


def build_nc(stop_after="p3", debug_taps=False):
    nc = bass.Bass("TRN2", target_bir_lowering=False)
    scratch_kind = "ExternalOutput" if debug_taps else "Internal"

    xin = nc.dram_tensor("xin", [2 * TOWN, D], F32, kind="ExternalInput").ap()
    kmask = nc.dram_tensor("kmask", [128, 32], F32, kind="ExternalInput").ap()
    wq = nc.dram_tensor("wq", [D, D], F32, kind="ExternalInput").ap()
    wk = nc.dram_tensor("wk", [D, D], F32, kind="ExternalInput").ap()
    wv = nc.dram_tensor("wv", [D, D], F32, kind="ExternalInput").ap()
    wo = nc.dram_tensor("wo", [D, D], F32, kind="ExternalInput").ap()
    gwT = nc.dram_tensor("gwT", [D, E], F32, kind="ExternalInput").ap()
    wg = nc.dram_tensor("wg", [E, 128, NHC * D], F32, kind="ExternalInput").ap()
    wu = nc.dram_tensor("wu", [E, 128, NHC * D], F32, kind="ExternalInput").ap()
    wd = nc.dram_tensor("wd", [E, HME, D], F32, kind="ExternalInput").ap()
    sgw = nc.dram_tensor("sgw", [D, 1], F32, kind="ExternalInput").ap()
    shg = nc.dram_tensor("shg", [128, NSC * D], F32, kind="ExternalInput").ap()
    shu = nc.dram_tensor("shu", [128, NSC * D], F32, kind="ExternalInput").ap()
    shd = nc.dram_tensor("shd", [HSH, D], F32, kind="ExternalInput").ap()
    out = nc.dram_tensor("out", [TOWN, D], F32, kind="ExternalOutput").ap()

    dump = nc.dram_tensor("dump", [128, HEADS * 512], F32,
                          kind="ExternalOutput" if debug_taps else "Internal").ap()
    xfm_s = nc.dram_tensor("xfm_s", [NCH, 128, 2 * TOWN], F32, kind=scratch_kind).ap()
    q_s = nc.dram_tensor("q_s", [HEADS, 128, TOWN], F32, kind=scratch_kind).ap()
    k_s = nc.dram_tensor("k_s", [HEADS, 128, 2 * TOWN], F32, kind=scratch_kind).ap()
    v_s = nc.dram_tensor("v_s", [HEADS, 128, 4096], F32, kind=scratch_kind).ap()
    x1_s = nc.dram_tensor("x1_s", [NCH, 128, TOWN], F32, kind=scratch_kind).ap()
    x1c_s = nc.dram_tensor("x1c_s", [TOWN + 16, XROW], F32,
                           kind=scratch_kind).ap()
    moe_s = nc.dram_tensor("moe_s", [TOWN + 16, 2048], F32,
                           kind=scratch_kind).ap()
    cfm_s = nc.dram_tensor("cfm_s", [E, TOWN], F32, kind=scratch_kind).ap()

    es = contextlib.ExitStack()
    with es:
        PS = [es.enter_context(nc.psum_tensor(f"ps{i}", [128, 512], F32))
              for i in range(8)]

        sT = Sem(nc, es, "sT")
        sV = Sem(nc, es, "sV")
        sA = Sem(nc, es, "sA")
        sZ = Sem(nc, es, "sZ")
        sLc = Sem(nc, es, "sLc")
        slx = [Sem(nc, es, f"slx{i}") for i in range(2)]
        stv0 = [Sem(nc, es, f"stv0{i}") for i in range(2)]
        slw = Sem(nc, es, "slw")
        slt = [Sem(nc, es, f"slt{i}") for i in range(2)]
        sqk = [Sem(nc, es, f"sqk{i}") for i in range(2)]
        spv = [Sem(nc, es, f"spv{i}") for i in range(2)]
        slk = [Sem(nc, es, f"slk{i}") for i in range(2)]
        slx0 = Sem(nc, es, "semx0")
        slwo = Sem(nc, es, "slwo")
        sx1 = Sem(nc, es, "sx1")
        scf = Sem(nc, es, "scf")
        slx1 = Sem(nc, es, "semx1ld")
        slgu = [Sem(nc, es, f"slgu{i}") for i in range(2)]
        sld = [Sem(nc, es, f"sld{i}") for i in range(2)]
        sor = Sem(nc, es, "sor")
        sgt = [Sem(nc, es, "sgt0"), Sem(nc, es, "sgt1")]
        ssc = Sem(nc, es, "ssc")
        stm = Sem(nc, es, "stm")
        slmo = Sem(nc, es, "slmo")
        dma_sems = [sLc] + slx + stv0 + [slw] + slt + sqk + \
            spv + slk + [slx0, slwo, sx1, scf, slx1] + slgu + sld + \
            [sor, sgt[0], sgt[1], ssc, stm, slmo]
        sems = [sT, sV, sA, sZ] + dma_sems

        def sb(name, shape, dt=F32):
            return es.enter_context(nc.sbuf_tensor(name, shape, dt))

        ones_col = sb("ones_col", [128, 1])
        ones_row = sb("ones_row", [1, 128])
        ident = sb("ident", [128, 128])

        kmask_sb = sb("kmask_sb", [128, 32])
        gwT_sb = sb("gwT_sb", [128, NCH * E])
        sgw_sb = sb("sgw_sb", [128, NCH])

        z1 = sb("z1", [1, 512])
        zflag = sb("zflag", [1, 512])
        vsum_sb = sb("vsum_sb", [1, 128])

        rz = sb("rz", [1, 512])


        lg = sb("lg", [128, 32])
        gnm = sb("gnm", [128, 1])
        grs = sb("grs", [128, 1])
        gm1 = sb("gm1", [128, 8])
        gm2 = sb("gm2", [128, 8])
        gden = sb("gden", [128, 1])
        gpr = sb("gpr", [128, 32])
        gp1 = sb("gp1", [128, 8])
        gp2 = sb("gp2", [128, 8])
        gcomb = sb("gcomb", [128, 32])

        pes = contextlib.ExitStack()

        def psb(name, shape, dt=F32):
            return pes.enter_context(nc.sbuf_tensor(name, shape, dt))

        with nc.Block() as block:
            env = Env(block, sems)
            M = env.mark
            W = env.wait

            def ph_init():
                def _(g):
                    g.memset(ones_col[:], 1.0)
                    g.memset(ones_row[:], 1.0)
                    g.memset(ident[:], 1.0)
                    ins = g.memset(gm1[:], -1.0)
                    sZ.inc(ins)
                    ins = g.memset(gm2[:], -1.0)
                    sZ.inc(ins)
                    M("init_ms", sZ)
                    W(g, sZ, "init_ms")
                    g.wait_ge(sZ.h, sZ.n)
                    ins = g.affine_select(ident[:], ident[:], [[-1, 128]],
                                          ALU.is_equal, 0.0, base=0,
                                          channel_multiplier=1)
                    sZ.inc(ins)
                    M("init", sZ)
                    g.wait_ge(sZ.h, sZ.n)
                env.sec("gpsimd", _)

                def _(s):
                    ins = s.dma_start(out=kmask_sb[:], in_=kmask)
                    sLc.inc(ins, 16)
                    for c in range(NCH):
                        ins = s.dma_start(out=gwT_sb[:, c * E:(c + 1) * E],
                                          in_=gwT[c * 128:(c + 1) * 128, :])
                        sLc.inc(ins, 16)
                    ins = s.dma_start(
                        out=sgw_sb[:],
                        in_=sgw.rearrange("(c p) one -> p (c one)", p=128),
                        allow_slow_non_contiguous=True)
                    sLc.inc(ins, 16)
                    M("init_ld", sLc)
                env.sec("sync", _)
            env.run(ph_init)

            xrow = [psb(f"xrow{i}", [128, D]) for i in range(2)]
            xev = [psb(f"xev{i}", [128, 512]) for i in range(2)]

            def ph_p0():
                def _(s):
                    for tb in range(32):
                        if tb >= 2:
                            W(s, sT, ("p0tr", tb - 2, 3))
                        ins = s.dma_start(out=xrow[tb % 2][:],
                                          in_=xin[tb * 128:(tb + 1) * 128, :])
                        slx[tb % 2].inc(ins, 16)
                        M(("p0ld", tb), slx[tb % 2])
                env.sec("sync", _)

                def _(t):
                    W(t, sZ, "init")
                    for tb in range(32):
                        W(t, slx[tb % 2], ("p0ld", tb))
                        for quad in range(4):
                            i = tb * 4 + quad
                            bank = PS[i % 2]
                            if i >= 2:
                                W(t, sV, ("p0ev", i - 2))
                            for j in range(4):
                                c = quad * 4 + j
                                ins = t.transpose(
                                    bank[:, j * 128:(j + 1) * 128],
                                    xrow[tb % 2][:, c * 128:(c + 1) * 128],
                                    ident[:])
                            sT.inc(ins)
                            M(("p0tr", tb, quad), sT)
                env.sec("tensor", _)

                def _(v):
                    v.drain()
                    for tb in range(32):
                        for quad in range(4):
                            i = tb * 4 + quad
                            W(v, sT, ("p0tr", tb, quad))
                            if i >= 2:
                                W(v, stv0[i % 2], ("p0st", i - 2))
                            ins = v.tensor_copy(xev[i % 2][:], PS[i % 2][:])
                            sV.inc(ins)
                            M(("p0ev", i), sV)
                env.sec("vector", _)

                def _(g):
                    for tb in range(32):
                        for quad in range(4):
                            i = tb * 4 + quad
                            W(g, sV, ("p0ev", i))
                            for j in range(4):
                                c = quad * 4 + j
                                ins = g.dma_start(
                                    out=xfm_s[c, :, tb * 128:(tb + 1) * 128],
                                    in_=xev[i % 2][:, j * 128:(j + 1) * 128])
                                stv0[i % 2].inc(ins, 16)
                            M(("p0st", i), stv0[i % 2])
                env.sec("gpsimd", _)
            env.run(ph_p0)
            pes.close()

            if stop_after == "p0":
                _finish(block, dma_sems)
                return nc

            w_half = psb("w_half", [128, NCH * 1024])
            xt = [psb(f"xt{i}", [128, NCH * 512]) for i in range(2)]
            pev = [psb(f"pev{i}", [128, 512]) for i in range(2)]

            def ph_p1():
                first = [True]
                for wsrc, dst_s, ntt, nm in [(wq, q_s, 4, "q"), (wk, k_s, 8, "k")]:
                    for half in range(2):
                        tag = (nm, half)
                        prev_tag = {"q0": None, "q1": ("q", 0),
                                    "k0": ("q", 1), "k1": ("k", 0)}[nm + str(half)]

                        def _(s, wsrc=wsrc, ntt=ntt, tag=tag, prev_tag=prev_tag,
                              half=half, first=first):
                            if tag == ("q", 0):
                                W(s, stv0[0], ("p0st", 126))
                                W(s, stv0[1], ("p0st", 127))
                            if prev_tag is not None:
                                W(s, sT, ("p1wfree", prev_tag))
                            for c in range(NCH):
                                ins = s.dma_start(
                                    out=w_half[:, c * 1024:(c + 1) * 1024],
                                    in_=wsrc[c * 128:(c + 1) * 128,
                                             half * 1024:(half + 1) * 1024])
                                slw.inc(ins, 16)
                            M(("p1wld", tag), slw)
                            for tt in range(ntt):
                                if tt >= 2:
                                    W(s, sT, ("p1mm", tag, tt - 2, 7))
                                for c in range(NCH):
                                    ins = s.dma_start(
                                        out=xt[tt % 2][:, c * 512:(c + 1) * 512],
                                        in_=xfm_s[c, :, tt * 512:(tt + 1) * 512])
                                    slt[tt % 2].inc(ins, 16)
                                M(("p1xld", tag, tt), slt[tt % 2])
                        env.sec("sync", _)

                        prev_ev = {("q", 0): ("p0ev", 127),
                                   ("q", 1): ("p1ev", ("q", 0), 31),
                                   ("k", 0): ("p1ev", ("q", 1), 31),
                                   ("k", 1): ("p1ev", ("k", 0), 63)}[tag]

                        def _(t, ntt=ntt, tag=tag, prev_ev=prev_ev):
                            W(t, slw, ("p1wld", tag))
                            for tt in range(ntt):
                                W(t, slt[tt % 2], ("p1xld", tag, tt))
                                for f in range(8):
                                    i = tt * 8 + f
                                    bank = PS[i % 2]
                                    if i >= 2:
                                        W(t, sV, ("p1ev", tag, i - 2))
                                    elif i == 0:
                                        W(t, sV, prev_ev)
                                    for c in range(NCH):
                                        ins = t.matmul(
                                            bank[:],
                                            w_half[:, c * 1024 + f * 128:
                                                   c * 1024 + (f + 1) * 128],
                                            xt[tt % 2][:, c * 512:(c + 1) * 512],
                                            start=(c == 0), stop=(c == NCH - 1))
                                    sT.inc(ins)
                                    M(("p1mm", tag, tt, f), sT)
                            M(("p1wfree", tag), sT)
                        env.sec("tensor", _)

        
                        prev_st = {("q", 0): None,
                                   ("q", 1): (("q", 0), 31),
                                   ("k", 0): (("q", 1), 31),
                                   ("k", 1): (("k", 0), 63)}[tag]

                        def _(v, ntt=ntt, tag=tag, prev_st=prev_st):
                            v.drain()
                            for tt in range(ntt):
                                for f in range(8):
                                    i = tt * 8 + f
                                    W(v, sT, ("p1mm", tag, tt, f))
                                    if i >= 2:
                                        W(v, sqk[i % 2], ("p1st", tag, i - 2))
                                    elif prev_st is not None:
                                        pt, pl = prev_st
                                        W(v, sqk[i % 2],
                                          ("p1st", pt, pl - 1 + i))
                                    ins = v.tensor_copy(pev[i % 2][:], PS[i % 2][:])
                                    sV.inc(ins)
                                    M(("p1ev", tag, i), sV)
                        env.sec("vector", _)

                        def _(g, ntt=ntt, tag=tag, dst_s=dst_s, half=half):
                            for tt in range(ntt):
                                for f in range(8):
                                    i = tt * 8 + f
                                    W(g, sV, ("p1ev", tag, i))
                                    ins = g.dma_start(
                                        out=dst_s[half * 8 + f, :,
                                                  tt * 512:(tt + 1) * 512],
                                        in_=pev[i % 2][:])
                                    sqk[i % 2].inc(ins, 16)
                                    M(("p1st", tag, i), sqk[i % 2])
                        env.sec("gpsimd", _)

            env.run(ph_p1)
            pes.close()

            wv_sb = psb("wv_sb", [128, NCH * 1024])
            xb = [psb(f"xb{i}", [128, NCH * 512]) for i in range(2)]
            pev = [psb(f"pev{i}b", [128, 512]) for i in range(2)]

            def ph_p1v():
                def _(s):
                    W(s, sT, ("p1wfree", ("k", 1)))
                    W(s, sqk[0], ("p1st", ("k", 1), 62))
                    W(s, sqk[1], ("p1st", ("k", 1), 63))
                    for half in range(2):
                        if half >= 1:
                            W(s, sT, ("p1vmm", half - 1, 7, 7))
                        for c in range(NCH):
                            ins = s.dma_start(
                                out=wv_sb[:, c * 1024:(c + 1) * 1024],
                                in_=wv[c * 128:(c + 1) * 128,
                                       half * 1024:(half + 1) * 1024])
                            slw.inc(ins, 16)
                        M(("p1vw", half), slw)
                        for tbg in range(8):
                            i = half * 8 + tbg
                            if i >= 2:
                                W(s, sT, ("p1vmm", (i - 2) // 8, (i - 2) % 8, 7))
                            for c in range(NCH):
                                ins = s.dma_start(
                                    out=xb[i % 2][:, c * 512:(c + 1) * 512],
                                    in_=xfm_s[c, :, tbg * 512:(tbg + 1) * 512])
                                slt[i % 2].inc(ins, 16)
                            M(("p1vx", half, tbg), slt[i % 2])
                env.sec("sync", _)

                def _(t):
                    for half in range(2):
                        W(t, slw, ("p1vw", half))
                        for tbg in range(8):
                            ig = half * 8 + tbg
                            W(t, slt[ig % 2], ("p1vx", half, tbg))
                            for sub in range(8):
                                tb4, dl = divmod(sub, 2)
                                i = ig * 8 + sub
                                bank = PS[i % 2]
                                if i >= 2:
                                    W(t, sV, ("p1vev", i - 2))
                                elif i == 0:
                                    W(t, sV, ("p1ev", ("k", 1), 63))
                                for c in range(NCH):
                                    ins = t.matmul(
                                        bank[:],
                                        xb[ig % 2][:, c * 512 + tb4 * 128:
                                                   c * 512 + (tb4 + 1) * 128],
                                        wv_sb[:, c * 1024 + dl * 512:
                                              c * 1024 + (dl + 1) * 512],
                                        start=(c == 0), stop=(c == NCH - 1))
                                sT.inc(ins)
                                M(("p1vmm", half, tbg, sub), sT)
                env.sec("tensor", _)

                def _(v):
                    v.drain()
                    for half in range(2):
                        for tbg in range(8):
                            for sub in range(8):
                                i = (half * 8 + tbg) * 8 + sub
                                W(v, sT, ("p1vmm", half, tbg, sub))
                                if i >= 2:
                                    W(v, spv[i % 2], ("p1vst", i - 2))
                                ins = v.tensor_copy(pev[i % 2][:], PS[i % 2][:])
                                sV.inc(ins)
                                M(("p1vev", i), sV)
                env.sec("vector", _)

                def _(g):
                    for half in range(2):
                        for tbg in range(8):
                            for sub in range(8):
                                tb4, dl = divmod(sub, 2)
                                tb = tbg * 4 + tb4
                                i = (half * 8 + tbg) * 8 + sub
                                W(g, sV, ("p1vev", i))
                                for hh in range(4):
                                    h = (half * 2 + dl) * 4 + hh
                                    ins = g.dma_start(
                                        out=v_s[h, :, tb * 128:(tb + 1) * 128],
                                        in_=pev[i % 2][:,
                                                       hh * 128:(hh + 1) * 128])
                                    spv[i % 2].inc(ins, 16)
                                M(("p1vst", i), spv[i % 2])
                env.sec("gpsimd", _)
            env.run(ph_p1v)
            pes.close()

            if stop_after == "p1":
                _finish(block, dma_sems)
                return nc

            st_a = psb("st_a", [1, 512])
            st_b = psb("st_b", [1, 512])
            st_c = psb("st_c", [1, 512])
            st_d = psb("st_d", [1, 512])
            st_e = psb("st_e", [1, 512])
            sqt = [psb(f"sqt{i}", [128, 512]) for i in range(2)]
            mb_b = psb("mb_b", [128, 512])
            yb_b = psb("yb_b", [128, 512])
            kh = [psb(f"kh{i}", [128, 4096]) for i in range(2)]
            vh = [psb(f"vh{i}", [128, 4096]) for i in range(2)]
            qh = [psb(f"qh{i}", [128, 512]) for i in range(2)]
            ptile = [psb(f"pt{i}", [128, 512]) for i in range(3)]
            stile = psb("stile", [128, 512])
            rzb = psb("rzb", [128, 512])
            o_all = psb("o_all", [128, HEADS * 512])
            x0r = psb("x0r", [128, NCH * 512])
            g8 = psb("g8", [8, 512])
            cf = psb("cf", [8, 512])
            x1tmr = [psb("x1tmr0", [128, 2048])] * 2
            dmask = psb("dmask", [128, 4 * 512])
            x1g = o_all
            wo_q = psb("wo_q", [128, NCH * 256])

            def ph_p2():
                def _(g):
                    ins = g.memset(dmask[:], 0.0)
                    sZ.inc(ins)
                    g.wait_ge(sZ.h, sZ.n)
                    for m in range(4):
                        ins = g.affine_select(
                            dmask[:, m * 512:(m + 1) * 512],
                            dmask[:, m * 512:(m + 1) * 512],
                            [[-1, 512]], ALU.is_ge, -FMAX,
                            base=128 * m - 1, channel_multiplier=1)
                        sZ.inc(ins)
                    M("p2dm", sZ)
                    g.wait_ge(sZ.h, sZ.n)
                env.sec("gpsimd", _)

                def prev_head(g, h):
                    k = g * HEADS + h - 1
                    return None if k < 0 else divmod(k, HEADS)

                ngrp = 1 if stop_after in ("p2heads", "p2g1") else NG
                for g in range(ngrp):
                    kbl = kbs_of(g)
                    ext = len(kbl)
                    diag = set(range(4 * g, 4 * g + 4))
                    diag_idx = [0, 1, 2, 3]

                    def _(s, g=g):
                        if g == 0:
                            W(s, spv[0], ("p1vst", 126))
                            W(s, spv[1], ("p1vst", 127))
                            W(s, sqk[0], ("p1st", ("k", 1), 62))
                            W(s, sqk[1], ("p1st", ("k", 1), 63))
                        for h in range(HEADS):
                            k_glob = g * HEADS + h
                            if k_glob >= 2:
                                pg, ph_ = divmod(k_glob - 2, HEADS)
                                W(s, sT, ("p2pe", pg, ph_))
                            slot = h % 2
                            ins = s.dma_start(
                                out=kh[slot][:, 0:(32 - 4 * g) * 128],
                                in_=k_s[h, :, 4 * g * 128:4096])
                            slk[slot].inc(ins, 16)
                            ins = s.dma_start(out=vh[slot][:],
                                              in_=v_s[h, :, :])
                            slk[slot].inc(ins, 16)
                            ins = s.dma_start(
                                out=qh[slot][:],
                                in_=q_s[h, :, g * 512:(g + 1) * 512])
                            slk[slot].inc(ins, 16)
                            M(("p2ld", g, h), slk[slot])
                        if g >= 1:
                            W(s, sV, ("p2x1", g - 1))
                        for c in range(NCH):
                            ins = s.dma_start(
                                out=x0r[:, c * 512:(c + 1) * 512],
                                in_=xfm_s[c, :, g * 512:(g + 1) * 512])
                            slx0.inc(ins, 16)
                        M(("p2x0", g), slx0)
                    env.sec("sync", _)

                    for h in range(HEADS):
                        slot = h % 2
                        ph = prev_head(g, h)

                        def _(t, g=g, h=h, slot=slot, ext=ext, ph=ph,
                              kbl=kbl):
                            W(t, slk[slot], ("p2ld", g, h))
                            if g == 0 and h == 0:
                                W(t, sV, ("p1vev", 127))
                            if g == 3:
                                if h == 0:
                                    W(t, sV, ("p2tmev", 2, 3))
                                else:
                                    W(t, sV, ("p2zfx", g, h - 1))
                                for kb in range(32):
                                    ins = t.matmul(
                                        PS[4][0:1, 0:128], ones_col[:],
                                        vh[slot][:, kb * 128:(kb + 1) * 128],
                                        start=(kb == 0), stop=(kb == 31))
                                sT.inc(ins)
                                M(("p2vsm", g, h), sT)
                            if ph is not None:
                                W(t, sV, ("p2osc",) + ph)
                            def zpv(j):
                                ins = t.matmul(
                                    PS[2][:],
                                    vh[slot][:, kbl[j] * 128:
                                             (kbl[j] + 1) * 128],
                                    ptile[j % 3][:],
                                    start=(j == 0),
                                    stop=(j == ext - 1 and g != 3))
                                sT.inc(ins)
                                M(("p2pv", g, h, j), sT)
                                ins = t.matmul(PS[5][0:1, :], ones_col[:],
                                               ptile[j % 3][:],
                                               start=(j == 0),
                                               stop=(j == ext - 1))
                                sT.inc(ins)
                                M(("p2zt", g, h, j), sT)

                            for i in range(ext):
                                if i >= 2:
                                    W(t, sA, ("p2exp", g, h, i - 2))
                                ins = t.matmul(PS[i % 2][:],
                                               kh[slot][:, i * 128:(i + 1) * 128],
                                               qh[slot][:],
                                               start=True, stop=True)
                                sT.inc(ins)
                                M(("p2smm", g, h, i), sT)
                                if i >= 1:
                                    W(t, sA, ("p2exp", g, h, i - 1))
                                    zpv(i - 1)
                            W(t, sA, ("p2exp", g, h, ext - 1))
                            zpv(ext - 1)
                            if g == 3:
                                W(t, sV, ("p2zfx", g, h))
                                ins = t.matmul(PS[2][:], vsum_sb[:], zflag[:],
                                               start=False, stop=True)
                                sT.inc(ins)
                                M(("p2corr", g, h), sT)
                            M(("p2pe", g, h), sT)
                            W(t, sV, ("p2rz", g, h))
                            if ph is not None:
                                W(t, sV, ("p2rzb",) + ph)
                            elif g == 0 and h == 0:
                                pass
                            if h == 0 and g >= 1:
                                W(t, sV, ("p2mbb", g - 1))
                            ins = t.matmul(PS[3][:], ones_row[:], rz[:],
                                           start=True, stop=True)
                            sT.inc(ins)
                            M(("p2bc", g, h), sT)
                        env.sec("tensor", _)

                        def _(v, g=g, h=h, ext=ext, diag_idx=diag_idx, ph=ph):
                            v.drain()
                            if g == 0 and h == 0:
                                W(v, sZ, "p2dm")
                            prev_d = None
                            for i in diag_idx:
                                W(v, sT, ("p2smm", g, h, i))
                                if prev_d is not None:
                                    W(v, sA, ("p2exp", g, h, prev_d))
                                elif ph is not None:
                                    W(v, sA, ("p2exp", ph[0], ph[1],
                                              4 * ph[0] + 3))
                                m = kbs_of(g)[i] - 4 * g
                                ins = v.scalar_tensor_tensor(
                                    stile[:], PS[i % 2][:], SCALE,
                                    dmask[:, m * 512:(m + 1) * 512],
                                    ALU.mult, ALU.add)
                                sV.inc(ins)
                                M(("p2stt", g, h, i), sV)
                                prev_d = i
                            if ph is not None:
                                W(v, sT, ("p2bc",) + ph)
                            W(v, sT, ("p2zt", g, h, ext - 1))
                            if g == 3:
                                if h >= 1:
                                    W(v, sT, ("p2corr", g, h - 1))
                                v.drain()
                                v.tensor_single_scalar(zflag[:], PS[5][0:1, :],
                                                       0.0, ALU.is_equal)
                                v.drain()
                                v.scalar_tensor_tensor(z1[:], zflag[:], 4096.0,
                                                       PS[5][0:1, :],
                                                       ALU.mult, ALU.add)
                                W(v, sT, ("p2vsm", g, h))
                                v.drain()
                                ins = v.tensor_copy(vsum_sb[:],
                                                    PS[4][0:1, 0:128])
                                sV.inc(ins)
                                M(("p2zfx", g, h), sV)
                                v.drain()
                                ins = v.reciprocal(rz[:], z1[:])
                            else:
                                v.drain()
                                ins = v.reciprocal(rz[:], PS[5][0:1, :])
                            sV.inc(ins)
                            M(("p2rz", g, h), sV)
                            W(v, sT, ("p2bc", g, h))
                            ins = v.tensor_copy(rzb[:], PS[3][:])
                            sV.inc(ins)
                            M(("p2rzb", g, h), sV)
                            W(v, sT, ("p2corr", g, h) if g == 3
                              else ("p2pv", g, h, ext - 1))
                            if g >= 1 and h == 0:
                                W(v, sT, ("p2wolast", g - 1))
                                W(v, sT, ("p2glg", g - 1))
                                W(v, sx1, ("p2x1st", g - 1))
                                W(v, sT, ("p2tmtr", g - 1, 3))
                            v.drain()
                            ins = v.tensor_tensor(
                                o_all[:, h * 512:(h + 1) * 512],
                                PS[2][:], rzb[:], ALU.mult)
                            sV.inc(ins)
                            M(("p2osc", g, h), sV)
                        env.sec("vector", _)

                        def _(a, g=g, h=h, ext=ext, diag=diag, ph=ph):
                            if g == 0 and h == 0:
                                W(a, sLc, "init_ld")
                            if ph is not None:
                                W(a, sT, ("p2pe",) + ph)
                            for i, kb in enumerate(kbs_of(g)):
                                if i >= 3:
                                    W(a, sT, ("p2zt", g, h, i - 3))
                                if kb in diag:
                                    W(a, sV, ("p2stt", g, h, i))
                                    ins = a.activation(ptile[i % 3][:], stile[:],
                                                       AF.Exp, bias=0.0,
                                                       scale=1.0)
                                else:
                                    W(a, sT, ("p2smm", g, h, i))
                                    ins = a.activation(
                                        ptile[i % 3][:], PS[i % 2][:], AF.Exp,
                                        bias=kmask_sb[:, kb:kb + 1], scale=SCALE)
                                sA.inc(ins)
                                M(("p2exp", g, h, i), sA)
                        env.sec("scalar", _)


                    if stop_after == "p2heads":
                        def _(gp, g=g):
                            for h in range(HEADS):
                                W(gp, sV, ("p2osc", g, h))
                                ins = gp.dma_start(
                                    out=dump[:, h * 512:(h + 1) * 512],
                                    in_=o_all[:, h * 512:(h + 1) * 512])
                                sx1.inc(ins, 16)
                            M(("p2dump", g), sx1)
                        env.sec("gpsimd", _)
                        break

                    def _(s, g=g):
                        for qq in range(8):
                            if qq == 0:
                                if g >= 1:
                                    W(s, sT, ("p2wolast", g - 1))
                            else:
                                W(s, sT, ("p2womm", g, 2 * qq - 1))
                            for c in range(NCH):
                                ins = s.dma_start(
                                    out=wo_q[:, c * 256:(c + 1) * 256],
                                    in_=wo[c * 128:(c + 1) * 128,
                                           qq * 256:(qq + 1) * 256])
                                slwo.inc(ins, 16)
                            M(("p2wold", g, qq), slwo)
                    env.sec("sync", _)

                    def stats(t, g, d):
                        W(t, sV, ("p2r", g, d))
                        W(t, sA, ("p2sq", g, d))
                        t.matmul(PS[6][0:1, :], ones_col[:],
                                 x0r[:, d * 512:(d + 1) * 512],
                                 start=(d == 0), stop=(d == 15))
                        ins = t.matmul(PS[7][0:1, :], ones_col[:],
                                       sqt[d % 2][:],
                                       start=(d == 0), stop=(d == 15))
                        sT.inc(ins)
                        M(("p2stats", g, d), sT)

                    def _(t, g=g):
                        W(t, sV, ("p2osc", g, HEADS - 1))
                        if g == 0:
                            W(t, sLc, "init_ld")
                        for qq in range(8):
                            W(t, slwo, ("p2wold", g, qq))
                            for dd in range(2):
                                d = qq * 2 + dd
                                bank = PS[d % 4]
                                if d >= 4:
                                    W(t, sV, ("p2r", g, d - 4))
                                for h2 in range(HEADS):
                                    ins = t.matmul(
                                        bank[:],
                                        wo_q[:, h2 * 256 + dd * 128:
                                             h2 * 256 + (dd + 1) * 128],
                                        o_all[:, h2 * 512:(h2 + 1) * 512],
                                        start=(h2 == 0), stop=(h2 == HEADS - 1))
                                sT.inc(ins)
                                M(("p2womm", g, d), sT)
                                if d >= 1:
                                    stats(t, g, d - 1)
                        stats(t, g, 15)
                        M(("p2wolast", g), sT)
                        W(t, sV, ("p2lnt", g))
                        ins = t.matmul(PS[3][:], ones_row[:], st_a[:],
                                       start=True, stop=True)
                        sT.inc(ins)
                        M(("p2bcm", g), sT)
                        ins = t.matmul(PS[4][:], ones_row[:], st_d[:],
                                       start=True, stop=True)
                        sT.inc(ins)
                        M(("p2bcy", g), sT)
                        W(t, sV, ("p2x1", g))
                        for c in range(NCH):
                            ins = t.matmul(PS[6][0:8, :],
                                           gwT_sb[:, c * E:(c + 1) * E],
                                           x1g[:, c * 512:(c + 1) * 512],
                                           start=(c == 0), stop=(c == NCH - 1))
                        sT.inc(ins)
                        M(("p2glg", g), sT)
                        W(t, sV, ("p2g8", g))
                        for tt in range(4):
                            ins = t.transpose(PS[7][:, tt * 8:(tt + 1) * 8],
                                              g8[0:8, tt * 128:(tt + 1) * 128],
                                              ident[0:8, 0:8])
                        sT.inc(ins)
                        M(("p2gtr", g), sT)
                        for tt in range(4):
                            W(t, sV, ("p2comb", g, tt))
                            ins = t.transpose(PS[6][0:8,
                                                    tt * 128:(tt + 1) * 128],
                                              gcomb[:, tt * 8:(tt + 1) * 8],
                                              ident[:])
                        sT.inc(ins)
                        M(("p2ctr", g), sT)
                        for tb in range(4):
                            if tb == 0:
                                W(t, sV, ("p2cf", g))
                                if g >= 1:
                                    W(t, sV, ("p2tmev", g - 1, 3))
                            else:
                                W(t, sV, ("p2tmev", g, tb - 1))
                            for c in range(NCH):
                                ins = t.transpose(
                                    PS[4 + c // 4][:, (c % 4) * 128:
                                                   (c % 4 + 1) * 128],
                                    x1g[:, c * 512 + tb * 128:
                                        c * 512 + (tb + 1) * 128],
                                    ident[:])
                                if c % 4 == 3:
                                    sT.inc(ins)
                            M(("p2tmtr", g, tb), sT)
                    env.sec("tensor", _)

                    def _(a, g=g):
                        W(a, slx0, ("p2x0", g))
                        if g >= 1:
                            W(a, sT, ("p2stats", g - 1, 15))
                        for d in range(16):
                            W(a, sV, ("p2r", g, d))
                            if d >= 2:
                                W(a, sT, ("p2stats", g, d - 2))
                            ins = a.activation(sqt[d % 2][:],
                                               x0r[:, d * 512:(d + 1) * 512],
                                               AF.Square, bias=0.0, scale=1.0)
                            sA.inc(ins)
                            M(("p2sq", g, d), sA)
                        W(a, sV, ("p2veps", g))
                        ins = a.activation(st_c[:], st_b[:], AF.Sqrt,
                                           bias=0.0, scale=1.0)
                        sA.inc(ins)
                        M(("p2sqrt", g), sA)
                        for tt in range(4):
                            W(a, sV, ("p2gnm", g, tt))
                            if tt >= 1:
                                W(a, sV, ("p2comb", g, tt - 1))
                            ins = a.activation(gpr[:, tt * 8:(tt + 1) * 8],
                                               lg[:, tt * 8:(tt + 1) * 8],
                                               AF.Exp, bias=gnm[:], scale=1.0,
                                               accum_out=grs[:])
                            sA.inc(ins)
                            M(("p2gexp", g, tt), sA)
                    env.sec("scalar", _)

                    def _(v, g=g):
                        v.drain()
                        W(v, slx0, ("p2x0", g))
                        for d in range(16):
                            W(v, sT, ("p2womm", g, d))
                            ins = v.tensor_tensor(
                                x0r[:, d * 512:(d + 1) * 512],
                                PS[d % 4][:],
                                x0r[:, d * 512:(d + 1) * 512], ALU.add)
                            sV.inc(ins)
                            M(("p2r", g, d), sV)
                        W(v, sT, ("p2stats", g, 15))
                        v.drain()
                        v.tensor_scalar_mul(st_a[:], PS[6][0:1, :], INVD)
                        ins = v.tensor_scalar_mul(st_b[:], PS[7][0:1, :], INVD)
                        sV.inc(ins)
                        v.drain()
                        v.tensor_mul(st_c[:], st_a[:], st_a[:])
                        v.drain()
                        v.tensor_sub(st_b[:], st_b[:], st_c[:])
                        v.drain()
                        ins = v.tensor_scalar_add(st_b[:], st_b[:], EPS)
                        sV.inc(ins)
                        M(("p2veps", g), sV)
                        W(v, sA, ("p2sqrt", g))
                        ins = v.reciprocal(st_d[:], st_c[:])
                        sV.inc(ins)
                        for _it in range(2):
                            v.drain()
                            v.tensor_mul(st_e[:], st_b[:], st_d[:])
                            v.drain()
                            v.tensor_mul(st_e[:], st_e[:], st_d[:])
                            v.drain()
                            v.tensor_scalar(st_e[:], st_e[:], -0.5, 1.5,
                                            ALU.mult, ALU.add)
                            v.drain()
                            v.tensor_mul(st_d[:], st_d[:], st_e[:])
                        ins = v.engine_nop()
                        sV.inc(ins)
                        M(("p2lnt", g), sV)
                        W(v, sT, ("p2bcm", g))
                        v.drain()
                        ins = v.tensor_copy(mb_b[:], PS[3][:])
                        sV.inc(ins)
                        M(("p2mbb", g), sV)
                        W(v, sT, ("p2bcy", g))
                        ins = v.tensor_copy(yb_b[:], PS[4][:])
                        sV.inc(ins)
                        M(("p2yb2", g), sV)
                        if g >= 1:
                            W(v, sx1, ("p2x1st", g - 1))
                        for d in range(16):
                            sl = slice(d * 512, (d + 1) * 512)
                            v.drain()
                            v.tensor_sub(x1g[:, sl], x0r[:, sl], mb_b[:])
                            v.drain()
                            ins = v.tensor_mul(x1g[:, sl], x1g[:, sl], yb_b[:])
                            sV.inc(ins)
                        M(("p2x1", g), sV)
                        W(v, sT, ("p2glg", g))
                        v.drain()
                        ins = v.tensor_copy(g8[:], PS[6][0:8, :])
                        sV.inc(ins)
                        M(("p2g8", g), sV)
                        W(v, sT, ("p2gtr", g))
                        ins = v.tensor_copy(lg[:], PS[7][:, 0:32])
                        sV.inc(ins)
                        for tt in range(4):
                            sl = slice(tt * 8, (tt + 1) * 8)
                            if tt == 0 and g >= 1:
                                W(v, stm, ("p2tmst", g - 1, 3))
                            if tt >= 1:
                                W(v, sA, ("p2gexp", g, tt - 1))
                            v.drain()
                            ins = v.reduce_max(gnm[:], lg[:, sl], axis=AX.X,
                                               negate=True)
                            sV.inc(ins)
                            M(("p2gnm", g, tt), sV)
                            W(v, sA, ("p2gexp", g, tt))
                            v.drain()
                            v.reciprocal(grs[:], grs[:])
                            v.drain()
                            v.tensor_scalar_mul(gpr[:, sl], gpr[:, sl], grs[:])
                            v.drain()
                            v.reduce_max(gm1[:, 0:1], gpr[:, sl], axis=AX.X)
                            v.drain()
                            v.match_replace(gp1[:], gm1[:], gpr[:, sl], 0.0)
                            v.drain()
                            v.reduce_max(gm2[:, 0:1], gp1[:], axis=AX.X)
                            v.drain()
                            v.match_replace(gp2[:], gm2[:], gp1[:], 0.0)
                            v.tensor_add(gden[:], gm1[:, 0:1], gm2[:, 0:1])
                            v.drain()
                            v.tensor_sub(gcomb[:, sl], gpr[:, sl], gp2[:])
                            v.reciprocal(gden[:], gden[:])
                            v.drain()
                            ins = v.tensor_scalar_mul(gcomb[:, sl],
                                                      gcomb[:, sl], gden[:])
                            sV.inc(ins)
                            M(("p2comb", g, tt), sV)
                        if g >= 1:
                            W(v, scf, ("p2cfst", g - 1))
                        W(v, sT, ("p2ctr", g))
                        v.drain()
                        ins = v.tensor_copy(cf[:], PS[6][0:8, :])
                        sV.inc(ins)
                        M(("p2cf", g), sV)
                        for tb in range(4):
                            W(v, sT, ("p2tmtr", g, tb))
                            if (g, tb) != (0, 0):
                                pb = (g, tb - 1) if tb else (g - 1, 3)
                                W(v, stm, ("p2tmst",) + pb)
                            v.drain()
                            for q4 in range(4):
                                ins = v.tensor_copy(
                                    x1tmr[tb % 2][:, q4 * 512:(q4 + 1) * 512],
                                    PS[4 + q4][:])
                            sV.inc(ins)
                            M(("p2tmev", g, tb), sV)
                    env.sec("vector", _)

                    def _(gp, g=g):
                        W(gp, sV, ("p2x1", g))
                        for c in range(NCH):
                            ins = gp.dma_start(
                                out=x1_s[c, :, g * 512:(g + 1) * 512],
                                in_=x1g[:, c * 512:(c + 1) * 512])
                            sx1.inc(ins, 16)
                        M(("p2x1st", g), sx1)
                        W(gp, sV, ("p2cf", g))
                        ins = gp.dma_start(out=cfm_s[:, g * 512:(g + 1) * 512],
                                           in_=cf[:])
                        scf.inc(ins, 16)
                        M(("p2cfst", g), scf)
                        for tb in range(4):
                            W(gp, sV, ("p2tmev", g, tb))
                            r0 = g * 512 + tb * 128
                            ins = gp.dma_start(
                                out=x1c_s[r0:r0 + 128, 0:2048],
                                in_=x1tmr[tb % 2][:])
                            stm.inc(ins, 16)
                            ins = gp.dma_start(
                                out=x1c_s[r0:r0 + 128, 2048:2056],
                                in_=gcomb[:, tb * 8:(tb + 1) * 8])
                            stm.inc(ins, 16)
                            M(("p2tmst", g, tb), stm)
                    env.sec("gpsimd", _)
            env.run(ph_p2)
            pes.close()

            if stop_after in ("p2heads", "p2g1"):
                _finish(block, dma_sems)
                return nc

            if stop_after == "p2":
                _finish(block, dma_sems)
                return nc

            iotaP = psb("iotaP", [16, 128])
            iota_row = psb("iota_row", [1, CAP])
            cnt2 = psb("cnt2", [1, 8])
            cntf = psb("cntf", [1, 8])
            cntb = psb("cntb", [16, 1])
            ivf = psb("ivf", [16, 128])
            crow = psb("crow", [1, CAP])
            cf16 = psb("cf16", [16, 128])
            self_ = psb("self_", [16, 128])
            idxf = psb("idxf", [16, 128])
            cnt_u = pes.enter_context(nc.sbuf_tensor("cnt_u", [1, 1], U32))
            idx16 = pes.enter_context(nc.sbuf_tensor("idx16", [16, 128], I16))
            idxr = pes.enter_context(nc.sbuf_tensor("idxr", [128, 8 * 48], I16))
            zrow = psb("zrow", [128, 512])
            xcg = [psb(f"xcg{i}", [128, 3 * XROW]) for i in range(2)]
            otm = psb("otm", [128, 3 * 2048])
            xg_fm = psb("xg_fm", [128, NCH * CAP])
            hpa = psb("hpa", [128, NHC * CAP])
            o_fm = psb("o_fm", [128, NCH * CAP])
            wgt = [psb(f"wgta{i}", [128, NCH * 128]) for i in range(2)]
            wut = [psb(f"wuta{i}", [128, NCH * 128]) for i in range(2)]
            wdt = [psb(f"wdta{i}", [128, 512]) for i in range(2)]
            sgl = psb("sgla", [128, CAP])
            cbig = psb("cbiga", [128, CAP])

            def ph_p3a():
                def _(g):
                    W(g, stm, ("p2tmst", NG - 1, 3))
                    W(g, sx1, ("p2x1st", NG - 1))
                    W(g, scf, ("p2cfst", NG - 1))
                    ins = g.iota(iotaP[:], [[16, 128]], base=1,
                                 channel_multiplier=1,
                                 allow_small_or_imprecise_dtypes=True)
                    sZ.inc(ins)
                    ins = g.iota(iota_row[:], [[1, CAP]], base=0,
                                 channel_multiplier=0,
                                 allow_small_or_imprecise_dtypes=True)
                    sZ.inc(ins)
                    ins = g.memset(cntf[:], 0.0)
                    sZ.inc(ins)
                    ins = g.memset(zrow[:], 0.0)
                    sZ.inc(ins)
                    ins = g.memset(xcg[0][:], 0.0)
                    sZ.inc(ins)
                    ins = g.memset(xcg[1][:], 0.0)
                    sZ.inc(ins)
                    M("a_const", sZ)
                    g.wait_ge(sZ.h, sZ.n)
                env.sec("gpsimd", _)

                for e in range(E):
                    def _(s, e=e):
                        if e == 0:
                            W(s, scf, ("p2cfst", NG - 1))
                            W(s, stm, ("p2tmst", NG - 1, 3))
                            W(s, sx1, ("p2x1st", NG - 1))
                        else:
                            W(s, sV, ("a_self", e - 1))
                            W(s, slmo, ("a_idx", e - 1))
                        for half in range(2):
                            ins = s.dma_start(
                                out=cf16[:, half * 64:(half + 1) * 64],
                                in_=cfm_s[e:e + 1, half * 1024:(half + 1) * 1024]
                                .rearrange("one (f p) -> (one p) f", p=16),
                                allow_slow_non_contiguous=True)
                            slmo.inc(ins, 16)
                        M(("a_cf", e), slmo)
                    env.sec("sync", _)

                    def _(v, e=e):
                        W(v, slmo, ("a_cf", e))
                        if e == 0:
                            W(v, sZ, "a_const")
                        else:
                            W(v, sZ, ("a_sg", e - 1))
                        v.drain()
                        v.tensor_single_scalar(self_[:], cf16[:], 0.0, ALU.is_gt)
                        v.drain()
                        v.tensor_mul(self_[:], self_[:], iotaP[:])
                        v.drain()
                        ins = v.tensor_scalar_add(self_[:], self_[:], -1.0)
                        sV.inc(ins)
                        M(("a_self", e), sV)
                    env.sec("vector", _)

                    def _(g, e=e):
                        if e == 0:
                            W(g, sZ, "a_const")
                            g.load_library(library_config.sparse_gather)
                        W(g, sV, ("a_self", e))
                        if e >= 1:
                            W(g, sV, ("a_psel", e - 1))
                        g.memset(idxf[:], -1.0)
                        g.drain()
                        ins = g.sparse_gather(idxf[:], self_[:],
                                              num_found=cnt_u[:])
                        sZ.inc(ins)
                        M(("a_sg", e), sZ)
                    env.sec("gpsimd", _)

                    def _(v, e=e):
                        W(v, sZ, ("a_sg", e))
                        if e >= 1:
                            W(v, slmo, ("a_idx", e - 1))
                        v.drain()
                        v.tensor_scalar_max(idxf[:], idxf[:], 0.0)
                        v.drain()
                        v.tensor_copy(idx16[:], idxf[:])
                        v.drain()
                        v.tensor_single_scalar(idx16[:], idx16[:], 0.0, ALU.max)
                        v.drain()
                        v.tensor_single_scalar(idx16[:], idx16[:], 2047.0,
                                               ALU.min)
                        v.drain()
                        v.tensor_copy(cntf[0:1, e:e + 1], cnt_u[:])
                        v.drain()
                        ins = v.tensor_scalar_add(cnt2[0:1, e:e + 1],
                                                  cntf[0:1, e:e + 1],
                                                  -float(CAP))
                        sV.inc(ins)
                        M(("a_post", e), sV)
                    env.sec("vector", _)

                    def _(t, e=e):
                        W(t, sV, ("a_post", e))
                        ins = t.matmul(PS[0][0:16, 0:1], ones_row[0:1, 0:16],
                                       cntf[0:1, e:e + 1],
                                       start=True, stop=True)
                        sT.inc(ins)
                        M(("a_cbm", e), sT)
                    env.sec("tensor", _)

                    def _(v, e=e):
                        W(v, sT, ("a_cbm", e))
                        v.drain()
                        v.tensor_copy(cntb[:], PS[0][0:16, 0:1])
                        v.drain()
                        v.tensor_copy(self_[:], idx16[:])
                        v.tensor_scalar(idxf[:], iotaP[:], cntb[:], None,
                                        ALU.is_le)
                        v.tensor_scalar(ivf[:], iotaP[:], cntb[:], None,
                                        ALU.is_gt)
                        v.drain()
                        v.tensor_mul(self_[:], self_[:], idxf[:])
                        v.tensor_single_scalar(ivf[:], ivf[:], float(TOWN),
                                               ALU.mult)
                        v.drain()
                        v.tensor_add(self_[:], self_[:], ivf[:])
                        v.drain()
                        ins = v.tensor_copy(idx16[:], self_[:])
                        sV.inc(ins)
                        M(("a_psel", e), sV)
                    env.sec("vector", _)

                    def _(s, e=e):
                        W(s, sV, ("a_psel", e))
                        for r in range(8):
                            ins = s.dma_start(
                                out=idxr[16 * r:16 * (r + 1),
                                         e * 48:(e + 1) * 48],
                                in_=idx16[:, 0:48])
                            slmo.inc(ins, 16)
                        M(("a_idx", e), slmo)
                    env.sec("sync", _)

                def _(g):
                    g.load_library(library_config.mlp)
                    for r in range(16):
                        for cc in range(4):
                            ins = g.dma_start(
                                out=moe_s[r * 128:(r + 1) * 128,
                                          cc * 512:(cc + 1) * 512],
                                in_=zrow[:])
                            ssc.inc(ins, 16)
                    for r in range(16):
                        ins = g.dma_start(
                            out=x1c_s[r * 128:(r + 1) * 128, 2056:XROW],
                            in_=zrow[:, 0:XROW - 2056])
                        ssc.inc(ins, 16)
                    for cc in range(4):
                        ins = g.dma_start(
                            out=moe_s[TOWN:TOWN + 16, cc * 512:(cc + 1) * 512],
                            in_=zrow[0:16, :])
                        ssc.inc(ins, 16)
                        ins = g.dma_start(
                            out=x1c_s[TOWN:TOWN + 16, cc * 512:(cc + 1) * 512],
                            in_=zrow[0:16, :])
                        ssc.inc(ins, 16)
                    ins = g.dma_start(out=x1c_s[TOWN:TOWN + 16, 2048:XROW],
                                      in_=zrow[0:16, 0:XROW - 2048])
                    ssc.inc(ins, 16)
                    M("a_zero", ssc)
                env.sec("gpsimd", _)

                NK = 2 * E
                for e in range(E):
                    for ch in range(2):
                        k = e * 2 + ch
                        isl = slice(e * 48 + ch * 24, e * 48 + ch * 24 + 24)

                        def _(g, e=e, ch=ch, k=k, isl=isl):
                            if k == 0:
                                W(g, slmo, ("a_idx", E - 1))
                                W(g, ssc, "a_zero")
                                W(g, stm, ("p2tmst", NG - 1, 3))
                            elif k >= 2:
                                W(g, sT, ("a_xtr", k - 2, 47))
                            ins = g.dma_gather(
                                xcg[k % 2].rearrange("p (b r) -> p b r",
                                                     r=XROW),
                                x1c_s[:, :], idxr[:, isl], CAP, CAP, XROW)
                            sgt[k % 2].inc(ins, 16)
                            M(("a_g", k), sgt[k % 2])
                        env.sec("gpsimd", _)

                        if k >= 1:
                            pisl = slice((k - 1) // 2 * 48
                                         + ((k - 1) % 2) * 24,
                                         (k - 1) // 2 * 48
                                         + ((k - 1) % 2) * 24 + 24)

                            def _(g, k=k - 1, isl=pisl):
                                W(g, sV, ("a_otev", k))
                                if k >= 1:
                                    W(g, ssc, ("a_sc", k - 1))
                                ins = g.dma_scatter_add(
                                    moe_s[:, :],
                                    otm.rearrange("p (b r) -> p b r", r=2048),
                                    idxr[:, isl], CAP, CAP, 2048)
                                ssc.inc(ins, 16)
                                M(("a_sc", k), ssc)
                            env.sec("gpsimd", _)

                        def _(t, e=e, k=k):
                            W(t, sgt[k % 2], ("a_g", k))
                            if k >= 1:
                                W(t, sV, ("a_cbe", k - 1))
                            for b in range(3):
                                ins = t.transpose(
                                    PS[5][0:1, b * 128:(b + 1) * 128],
                                    xcg[k % 2][:, b * XROW + 2048 + e:
                                               b * XROW + 2049 + e],
                                    ident[:])
                                sT.inc(ins)
                            M(("a_crt", k), sT)
                        env.sec("tensor", _)

                        def _(t, k=k):
                            W(t, sgt[k % 2], ("a_g", k))
                            for i in range(48):
                                b, c = divmod(i, 16)
                                if i >= 4:
                                    W(t, sV, ("a_xev", k, i - 4))
                                ins = t.transpose(
                                    PS[i % 4][:, ((i // 4) % 4) * 128:
                                              ((i // 4) % 4 + 1) * 128],
                                    xcg[k % 2][:, b * XROW + c * 128:
                                               b * XROW + (c + 1) * 128],
                                    ident[:])
                                sT.inc(ins)
                                M(("a_xtr", k, i), sT)
                        env.sec("tensor", _)

                        def _(v, k=k):
                            for i in range(48):
                                b, c = divmod(i, 16)
                                W(v, sT, ("a_xtr", k, i))
                                ins = v.tensor_copy(
                                    xg_fm[:, c * CAP + b * 128:
                                          c * CAP + (b + 1) * 128],
                                    PS[i % 4][:, ((i // 4) % 4) * 128:
                                              ((i // 4) % 4 + 1) * 128])
                                sV.inc(ins)
                                M(("a_xev", k, i), sV)
                            W(v, sT, ("a_crt", k))
                            v.drain()
                            csrc = cntf if ch == 0 else cnt2
                            v.tensor_scalar(crow[:], iota_row[0:1, 0:CAP],
                                            csrc[0:1, k // 2:k // 2 + 1], None,
                                            ALU.is_lt)
                            v.drain()
                            ins = v.tensor_tensor(crow[:], PS[5][0:1, 0:CAP],
                                                  crow[:], ALU.mult)
                            sV.inc(ins)
                            M(("a_crm", k), sV)
                        env.sec("vector", _)

                        def _(t, k=k):
                            W(t, sV, ("a_crm", k))
                            ins = t.matmul(PS[5][:, 0:CAP], ones_row[:],
                                           crow[:], start=True, stop=True)
                            sT.inc(ins)
                            M(("a_cbc", k), sT)
                        env.sec("tensor", _)

                        def _(v, k=k):
                            W(v, sT, ("a_cbc", k))
                            v.drain()
                            ins = v.tensor_copy(cbig[:], PS[5][:, 0:CAP])
                            sV.inc(ins)
                            M(("a_cbe", k), sV)
                        env.sec("vector", _)

                        def _(s, e=e, k=k):
                            for hc in range(NHC):
                                m = k * NHC + hc
                                if m == 0 and k == 0:
                                    W(s, stm, ("p2tmst", NG - 1, 3))
                                    W(s, sx1, ("p2x1st", NG - 1))
                                    W(s, scf, ("p2cfst", NG - 1))
                                if m >= 2:
                                    pk = divmod(m - 2, NHC)
                                    W(s, sT, ("a_gu", pk[0], pk[1]))
                                ins = s.dma_start(
                                    out=wgt[hc % 2][:],
                                    in_=wg[e][:, hc * D:(hc + 1) * D])
                                slgu[hc % 2].inc(ins, 16)
                                ins = s.dma_start(
                                    out=wut[hc % 2][:],
                                    in_=wu[e][:, hc * D:(hc + 1) * D])
                                slgu[hc % 2].inc(ins, 16)
                                M(("a_wld", k, hc), slgu[hc % 2])
                        env.sec("sync", _)

                        def _(t, k=k):
                            for hc in range(NHC):
                                W(t, slgu[hc % 2], ("a_wld", k, hc))
                                m = k * NHC + hc
                                if hc == 0:
                                    W(t, sV, ("a_xev", k, 47))
                                if m >= 2:
                                    pk = divmod(m - 2, NHC)
                                    W(t, sA, ("a_si",) + pk)
                                if m >= 1:
                                    pk = divmod(m - 1, NHC)
                                    W(t, sV, ("a_t1",) + pk)
                                for c in range(NCH):
                                    ins = t.matmul(
                                        PS[2 + hc % 2][:, 0:CAP],
                                        wgt[hc % 2][:, c * 128:(c + 1) * 128],
                                        xg_fm[:, c * CAP:(c + 1) * CAP],
                                        start=(c == 0), stop=(c == NCH - 1))
                                sT.inc(ins)
                                for c in range(NCH):
                                    ins = t.matmul(
                                        PS[4][:, 0:CAP],
                                        wut[hc % 2][:, c * 128:(c + 1) * 128],
                                        xg_fm[:, c * CAP:(c + 1) * CAP],
                                        start=(c == 0), stop=(c == NCH - 1))
                                sT.inc(ins)
                                M(("a_gu", k, hc), sT)
                        env.sec("tensor", _)

                        def _(a, k=k):
                            for hc in range(NHC):
                                W(a, sT, ("a_gu", k, hc))
                                m = k * NHC + hc
                                if m >= 1:
                                    pk = divmod(m - 1, NHC)
                                    W(a, sV, ("a_t1",) + pk)
                                ins = a.activation(sgl[:], PS[2 + hc % 2][:, 0:CAP],
                                                   AF.Silu, bias=0.0, scale=1.0)
                                sA.inc(ins)
                                M(("a_si", k, hc), sA)
                        env.sec("scalar", _)

                        def _(v, k=k):
                            for hc in range(NHC):
                                W(v, sA, ("a_si", k, hc))
                                if hc == 0 and k >= 1:
                                    W(v, sT, ("a_dn", k - 1))
                                v.drain()
                                v.tensor_tensor(sgl[:], sgl[:], PS[4][:, 0:CAP],
                                                ALU.mult)
                                v.drain()
                                ins = v.tensor_tensor(
                                    hpa[:, hc * CAP:(hc + 1) * CAP],
                                    sgl[:], cbig[:], ALU.mult)
                                sV.inc(ins)
                                M(("a_t1", k, hc), sV)
                        env.sec("vector", _)

                        def _(s, e=e, k=k):
                            for dgrp in range(4):
                                for hc in range(NHC):
                                    md = (k * 4 + dgrp) * NHC + hc
                                    if md == 0 and k == 0:
                                        W(s, stm, ("p2tmst", NG - 1, 3))
                                        W(s, sx1, ("p2x1st", NG - 1))
                                        W(s, scf, ("p2cfst", NG - 1))
                                    if md >= 2:
                                        if hc >= 2:
                                            pk = ("a_dnm", k, dgrp, hc - 2)
                                        elif dgrp >= 1:
                                            pk = ("a_dnm", k, dgrp - 1,
                                                  NHC - 1 - hc)
                                        else:
                                            pk = ("a_dnm", k - 1, 3,
                                                  NHC - 1 - hc)
                                        W(s, sT, pk)
                                    ins = s.dma_start(
                                        out=wdt[hc % 2][:],
                                        in_=wd[e][hc * 128:(hc + 1) * 128,
                                                  dgrp * 512:(dgrp + 1) * 512])
                                    sld[hc % 2].inc(ins, 16)
                                    M(("a_dld", k, dgrp, hc), sld[hc % 2])
                        env.sec("sync", _)

                        def _(t, k=k):
                            for dgrp in range(4):
                                for hc in range(NHC):
                                    W(t, sld[hc % 2], ("a_dld", k, dgrp, hc))
                                    W(t, sV, ("a_t1", k, hc))
                                    if hc == 0:
                                        if dgrp == 0:
                                            W(t, sV, ("a_t1", k, NHC - 1))
                                            if k >= 1:
                                                W(t, sV, ("a_ov", k - 1, 3))
                                        else:
                                            W(t, sV, ("a_ov", k, dgrp - 1))
                                    for dj in range(4):
                                        ins = t.matmul(
                                            PS[4 + dj][:, 0:CAP],
                                            wdt[hc % 2][:, dj * 128:
                                                        (dj + 1) * 128],
                                            hpa[:, hc * CAP:(hc + 1) * CAP],
                                            start=(hc == 0), stop=(hc == NHC - 1))
                                    sT.inc(ins)
                                    M(("a_dnm", k, dgrp, hc), sT)
                            M(("a_dn", k), sT)
                        env.sec("tensor", _)

                        def _(v, k=k):
                            for dgrp in range(4):
                                W(v, sT, ("a_dnm", k, dgrp, NHC - 1))
                                v.drain()
                                for dj in range(4):
                                    d = dgrp * 4 + dj
                                    ins = v.tensor_copy(
                                        o_fm[:, d * CAP:(d + 1) * CAP],
                                        PS[4 + dj][:, 0:CAP])
                                    sV.inc(ins)
                                M(("a_ov", k, dgrp), sV)
                        env.sec("vector", _)

                        def _(t, k=k):
                            for i in range(48):
                                c, b = divmod(i, 3)
                                W(t, sV, ("a_ov", k, c // 4))
                                if i >= 4:
                                    W(t, sV, ("a_otv", k, i - 4))
                                ins = t.transpose(
                                    PS[i % 4][:, ((i // 4) % 4) * 128:
                                              ((i // 4) % 4 + 1) * 128],
                                    o_fm[:, c * CAP + b * 128:
                                         c * CAP + (b + 1) * 128],
                                    ident[:])
                                sT.inc(ins)
                                M(("a_otr", k, i), sT)
                        env.sec("tensor", _)

                        def _(v, k=k):
                            if k >= 1:
                                W(v, ssc, ("a_sc", k - 1))
                            for i in range(48):
                                c, b = divmod(i, 3)
                                W(v, sT, ("a_otr", k, i))
                                ins = v.tensor_copy(
                                    otm[:, b * 2048 + c * 128:
                                        b * 2048 + (c + 1) * 128],
                                    PS[i % 4][:, ((i // 4) % 4) * 128:
                                              ((i // 4) % 4 + 1) * 128])
                                sV.inc(ins)
                                M(("a_otv", k, i), sV)
                            ins = v.engine_nop()
                            sV.inc(ins)
                            M(("a_otev", k), sV)
                        env.sec("vector", _)

                        if k == NK - 1:
                            def _(g, k=k, isl=isl):
                                W(g, sV, ("a_otev", k))
                                W(g, ssc, ("a_sc", k - 1))
                                ins = g.dma_scatter_add(
                                    moe_s[:, :],
                                    otm.rearrange("p (b r) -> p b r", r=2048),
                                    idxr[:, isl], CAP, CAP, 2048)
                                ssc.inc(ins, 16)
                                M(("a_sc", k), ssc)
                            env.sec("gpsimd", _)
            env.run(ph_p3a)
            pes.close()

            st_a = psb("st_a3", [1, 512])
            st_b = psb("st_b3", [1, 512])
            st_c = psb("st_c3", [1, 512])
            st_d = psb("st_d3", [1, 512])
            st_e = psb("st_e3", [1, 512])
            sg1 = psb("sg1", [1, 512])
            sqt = [psb(f"sqt3{i}", [128, 512]) for i in range(2)]
            mb_b = psb("mb_b3", [128, 512])
            yb_b = psb("yb_b3", [128, 512])
            x1t = psb("x1t", [128, NCH * 512])
            cb = psb("cb", [128, 512])
            wgt = [psb(f"wgt{i}", [128, NCH * 128]) for i in range(2)]
            wut = [psb(f"wut{i}", [128, NCH * 128]) for i in range(2)]
            wdt = [psb(f"wdt{i}", [128, 512]) for i in range(2)]
            sgl = [psb("sgl0", [128, 512])] * 2
            t1b = [psb("t1b0", [128, 512])] * 2
            hp = psb("hp", [128, NHC * 512])
            acc = psb("acc", [128, NCH * 512])
            orow = psb("orow", [128, D])
            moet = psb("moet", [128, 2048])

            def ph_p3b():
                for t_i in range(NG):
                    def _(s, t_i=t_i):
                        if t_i == 0:
                            W(s, sx1, ("p2x1st", NG - 1))
                            W(s, ssc, ("a_sc", 2 * E - 1))
                        else:
                            W(s, sT, ("p3otr", t_i - 1, 3))
                        for c in range(NCH):
                            ins = s.dma_start(
                                out=x1t[:, c * 512:(c + 1) * 512],
                                in_=x1_s[c, :, t_i * 512:(t_i + 1) * 512])
                            slx1.inc(ins, 16)
                        M(("p3ld", t_i), slx1)
                    env.sec("sync", _)

                    def _(t, t_i=t_i):
                        W(t, slx1, ("p3ld", t_i))
                        if t_i == 0:
                            W(t, sLc, "init_ld")
                            W(t, sV, ("a_otev", 2 * E - 1))
                        else:
                            W(t, sV, ("p3oev", t_i - 1, 3))
                            W(t, sA, ("p3sig", t_i - 1))
                        for c in range(NCH):
                            ins = t.matmul(PS[6][0:1, :],
                                           sgw_sb[:, c:c + 1],
                                           x1t[:, c * 512:(c + 1) * 512],
                                           start=(c == 0), stop=(c == NCH - 1))
                        sT.inc(ins)
                        M(("p3sgm", t_i), sT)
                        W(t, sA, ("p3sig", t_i))
                        if t_i >= 1:
                            W(t, sV, ("p3cb", t_i - 1))
                        ins = t.matmul(PS[3][:], ones_row[:], sg1[:],
                                       start=True, stop=True)
                        sT.inc(ins)
                        M(("p3cbm", t_i), sT)
                    env.sec("tensor", _)

                    def _(a, t_i=t_i):
                        W(a, sT, ("p3sgm", t_i))
                        ins = a.activation(sg1[:], PS[6][0:1, :], AF.Sigmoid,
                                           bias=0.0, scale=1.0)
                        sA.inc(ins)
                        M(("p3sig", t_i), sA)
                    env.sec("scalar", _)

                    def _(v, t_i=t_i):
                        v.drain()
                        if t_i >= 1:
                            W(v, sV, ("p3t1", t_i - 1, 9, NHC - 1))
                        W(v, sT, ("p3cbm", t_i))
                        ins = v.tensor_copy(cb[:], PS[3][:])
                        sV.inc(ins)
                        M(("p3cb", t_i), sV)
                    env.sec("vector", _)

                    for e in (8, 9):
                        nhc = NHC
                        hoff = (e - 8) * NHC

                        def _(s, t_i=t_i, e=e, hoff=hoff):
                            for hc in range(nhc):
                                m = (t_i * 2 + (e - 8)) * NHC + hc
                                if m >= 2:
                                    pt, r = divmod(m - 2, 2 * NHC)
                                    pe, phc = divmod(r, NHC)
                                    W(s, sT, ("p3gu", pt, pe + 8, phc))
                                ch = hoff + hc
                                ins = s.dma_start(out=wgt[hc % 2][:],
                                                  in_=shg[:, ch * D:(ch + 1) * D])
                                slgu[hc % 2].inc(ins, 16)
                                ins = s.dma_start(out=wut[hc % 2][:],
                                                  in_=shu[:, ch * D:(ch + 1) * D])
                                slgu[hc % 2].inc(ins, 16)
                                M(("p3wld", t_i, e, hc), slgu[hc % 2])
                        env.sec("sync", _)

                        def _(t, t_i=t_i, e=e):
                            for hc in range(nhc):
                                W(t, slgu[hc % 2], ("p3wld", t_i, e, hc))
                                m = (t_i * 2 + (e - 8)) * NHC + hc
                                if e == 8 and hc == 0:
                                    W(t, sV, ("p3cb", t_i))
                                    if t_i == 0:
                                        W(t, sV, ("a_otev", 2 * E - 1))
                                    else:
                                        W(t, sV, ("p3lnt", t_i - 1))
                                if m >= 2:
                                    pt, r = divmod(m - 2, 2 * NHC)
                                    pe, phc = divmod(r, NHC)
                                    W(t, sA, ("p3silu", pt, pe + 8, phc))
                                    W(t, sV, ("p3t1", pt, pe + 8, phc))
                                for c in range(NCH):
                                    ins = t.matmul(
                                        PS[0 + hc % 2][:],
                                        wgt[hc % 2][:, c * 128:(c + 1) * 128],
                                        x1t[:, c * 512:(c + 1) * 512],
                                        start=(c == 0), stop=(c == NCH - 1))
                                sT.inc(ins)
                                for c in range(NCH):
                                    ins = t.matmul(
                                        PS[2 + hc % 2][:],
                                        wut[hc % 2][:, c * 128:(c + 1) * 128],
                                        x1t[:, c * 512:(c + 1) * 512],
                                        start=(c == 0), stop=(c == NCH - 1))
                                sT.inc(ins)
                                M(("p3gu", t_i, e, hc), sT)
                        env.sec("tensor", _)

                        def _(a, t_i=t_i, e=e):
                            for hc in range(nhc):
                                W(a, sT, ("p3gu", t_i, e, hc))
                                m = (t_i * 2 + (e - 8)) * NHC + hc
                                if m >= 1:
                                    pt, r = divmod(m - 1, 2 * NHC)
                                    pe, phc = divmod(r, NHC)
                                    W(a, sV, ("p3t1", pt, pe + 8, phc))
                                ins = a.activation(sgl[hc % 2][:],
                                                   PS[0 + hc % 2][:], AF.Silu,
                                                   bias=0.0, scale=1.0)
                                sA.inc(ins)
                                M(("p3silu", t_i, e, hc), sA)
                        env.sec("scalar", _)

                        def _(v, t_i=t_i, e=e):
                            v.drain()
                            for hc in range(nhc):
                                W(v, sA, ("p3silu", t_i, e, hc))
                                m = (t_i * 2 + (e - 8)) * NHC + hc
                                if m >= 1:
                                    pt, r = divmod(m - 1, 2 * NHC)
                                    pe, phc = divmod(r, NHC)
                                    W(v, sV, ("p3hp", pt, pe + 8, phc))
                                v.drain()
                                v.tensor_tensor(t1b[hc % 2][:],
                                                sgl[hc % 2][:],
                                                PS[2 + hc % 2][:], ALU.mult)
                                v.drain()
                                ins = v.tensor_tensor(
                                    hp[:, hc * 512:(hc + 1) * 512],
                                    t1b[hc % 2][:], cb[:], ALU.mult)
                                sV.inc(ins)
                                M(("p3t1", t_i, e, hc), sV)
                                env.marks[("p3hp", t_i, e, hc)] =                                     env.marks[("p3t1", t_i, e, hc)]
                        env.sec("vector", _)

                        def _(s, t_i=t_i, e=e, hoff=hoff):
                            for dgrp in range(4):
                                for hc in range(nhc):
                                    md = ((t_i * 2 + (e - 8)) * 4 + dgrp) * NHC + hc
                                    pk = None
                                    if hc >= 2:
                                        pk = ("p3dnm", t_i, e, dgrp, hc - 2)
                                    elif dgrp >= 1:
                                        pk = ("p3dnm", t_i, e, dgrp - 1,
                                              nhc - 1 - hc)
                                    elif md >= 2:
                                        pt, pe = (t_i, 8) if e == 9 else                                             (t_i - 1, 9)
                                        pk = ("p3dnm", pt, pe, 3, nhc - 1 - hc)
                                    if pk is not None:
                                        W(s, sT, pk)
                                    ch = hoff + hc
                                    ins = s.dma_start(
                                        out=wdt[hc % 2][:],
                                        in_=shd[ch * 128:(ch + 1) * 128,
                                                dgrp * 512:(dgrp + 1) * 512])
                                    sld[hc % 2].inc(ins, 16)
                                    M(("p3dld", t_i, e, dgrp, hc), sld[hc % 2])
                        env.sec("sync", _)

                        def _(t, t_i=t_i, e=e):
                            for dgrp in range(4):
                                for hc in range(nhc):
                                    W(t, sld[hc % 2], ("p3dld", t_i, e, dgrp, hc))
                                    W(t, sV, ("p3hp", t_i, e, hc))
                                    if hc == 0:
                                        if dgrp > 0:
                                            W(t, sV, ("p3acc", t_i, e, dgrp - 1))
                                        elif e == 9:
                                            W(t, sV, ("p3acc", t_i, 8, 3))
                                        elif t_i > 0:
                                            W(t, sV, ("p3acc", t_i - 1, 9, 3))
                                            W(t, sV, ("p3oev", t_i - 1, 3))
                                    for dj in range(4):
                                        ins = t.matmul(
                                            PS[4 + dj][:],
                                            wdt[hc % 2][:, dj * 128:
                                                        (dj + 1) * 128],
                                            hp[:, hc * 512:(hc + 1) * 512],
                                            start=(hc == 0),
                                            stop=(hc == nhc - 1))
                                    sT.inc(ins)
                                    M(("p3dnm", t_i, e, dgrp, hc), sT)
                            M(("p3dn", t_i, e), sT)
                        env.sec("tensor", _)

                        def _(v, t_i=t_i, e=e):
                            for dgrp in range(4):
                                W(v, sT, ("p3dnm", t_i, e, dgrp, nhc - 1))
                                v.drain()
                                for dj in range(4):
                                    d = dgrp * 4 + dj
                                    sl = slice(d * 512, (d + 1) * 512)
                                    if e == 8:
                                        ins = v.tensor_copy(acc[:, sl],
                                                            PS[4 + dj][:])
                                    else:
                                        ins = v.tensor_tensor(acc[:, sl],
                                                              acc[:, sl],
                                                              PS[4 + dj][:],
                                                              ALU.add)
                                    sV.inc(ins)
                                M(("p3acc", t_i, e, dgrp), sV)
                        env.sec("vector", _)

                    def _(s, t_i=t_i):
                        for tb in range(4):
                            if t_i == 0 and tb == 0:
                                W(s, ssc, ("a_sc", 2 * E - 1))
                            if (t_i, tb) != (0, 0):
                                pb = (t_i, tb - 1) if tb else (t_i - 1, 3)
                                W(s, sT, ("p3mtr",) + pb)
                            ins = s.dma_start(
                                out=moet[:],
                                in_=moe_s[t_i * 512 + tb * 128:
                                          t_i * 512 + (tb + 1) * 128, :])
                            slmo.inc(ins, 16)
                            M(("p3mld", t_i, tb), slmo)
                    env.sec("sync", _)

                    def _(t, t_i=t_i):
                        for tb in range(4):
                            W(t, slmo, ("p3mld", t_i, tb))
                            if tb == 0:
                                W(t, sV, ("p3acc", t_i, 9, 3))
                                if t_i >= 1:
                                    W(t, sV, ("p3ybb", t_i - 1))
                            else:
                                W(t, sV, ("p3mev", t_i, tb - 1))
                            for c in range(NCH):
                                if c >= 2:
                                    W(t, sV, ("p3mevc", t_i, tb, c - 2))
                                ins = t.transpose(
                                    PS[2 + c % 2][:, ((c // 2) % 4) * 128:
                                                  ((c // 2) % 4 + 1) * 128],
                                    moet[:, c * 128:(c + 1) * 128],
                                    ident[:])
                                sT.inc(ins)
                                M(("p3mtrc", t_i, tb, c), sT)
                            M(("p3mtr", t_i, tb), sT)
                    env.sec("tensor", _)

                    def _(v, t_i=t_i):
                        for tb in range(4):
                            for c in range(NCH):
                                W(v, sT, ("p3mtrc", t_i, tb, c))
                                if c == 0:
                                    v.drain()
                                ins = v.tensor_tensor(
                                    acc[:, c * 512 + tb * 128:
                                        c * 512 + (tb + 1) * 128],
                                    acc[:, c * 512 + tb * 128:
                                        c * 512 + (tb + 1) * 128],
                                    PS[2 + c % 2][:, ((c // 2) % 4) * 128:
                                                  ((c // 2) % 4 + 1) * 128],
                                    ALU.add)
                                sV.inc(ins)
                                M(("p3mevc", t_i, tb, c), sV)
                            M(("p3mev", t_i, tb), sV)
                    env.sec("vector", _)

                    def _(v, t_i=t_i):
                        W(v, sV, ("p3mev", t_i, 3))
                        for d in range(16):
                            sl = slice(d * 512, (d + 1) * 512)
                            v.drain()
                            ins = v.tensor_tensor(acc[:, sl], acc[:, sl],
                                                  x1t[:, sl], ALU.add)
                            sV.inc(ins)
                            M(("p3r2", t_i, d), sV)
                    env.sec("vector", _)

                    def _(a, t_i=t_i):
                        if t_i == 0:
                            W(a, sT, ("p2stats", NG - 1, 15))
                        else:
                            W(a, sT, ("p3stats", t_i - 1, 15))
                        for d in range(16):
                            W(a, sV, ("p3r2", t_i, d))
                            if d >= 2:
                                W(a, sT, ("p3stats", t_i, d - 2))
                            ins = a.activation(sqt[d % 2][:],
                                               acc[:, d * 512:(d + 1) * 512],
                                               AF.Square, bias=0.0, scale=1.0)
                            sA.inc(ins)
                            M(("p3sq", t_i, d), sA)
                        W(a, sV, ("p3veps", t_i))
                        ins = a.activation(st_c[:], st_b[:], AF.Sqrt,
                                           bias=0.0, scale=1.0)
                        sA.inc(ins)
                        M(("p3sqrt", t_i), sA)
                    env.sec("scalar", _)

                    def _(t, t_i=t_i):
                        for d in range(16):
                            W(t, sV, ("p3r2", t_i, d))
                            W(t, sA, ("p3sq", t_i, d))
                            t.matmul(PS[0][0:1, :], ones_col[:],
                                     acc[:, d * 512:(d + 1) * 512],
                                     start=(d == 0), stop=(d == 15))
                            ins = t.matmul(PS[1][0:1, :], ones_col[:],
                                           sqt[d % 2][:],
                                           start=(d == 0), stop=(d == 15))
                            sT.inc(ins)
                            M(("p3stats", t_i, d), sT)
                        W(t, sV, ("p3lnt", t_i))
                        ins = t.matmul(PS[2][:], ones_row[:], st_a[:],
                                       start=True, stop=True)
                        sT.inc(ins)
                        M(("p3bcm", t_i), sT)
                        ins = t.matmul(PS[3][:], ones_row[:], st_d[:],
                                       start=True, stop=True)
                        sT.inc(ins)
                        M(("p3bcy", t_i), sT)
                        W(t, sV, ("p3xn", t_i))
                        for tb in range(4):
                            if tb >= 1:
                                W(t, sV, ("p3oev", t_i, tb - 1))
                            for c in range(NCH):
                                ins = t.transpose(
                                    PS[4 + c // 4][:, (c % 4) * 128:
                                                   (c % 4 + 1) * 128],
                                    x1t[:, c * 512 + tb * 128:
                                        c * 512 + (tb + 1) * 128],
                                    ident[:])
                                if c % 4 == 3:
                                    sT.inc(ins)
                            M(("p3otr", t_i, tb), sT)
                    env.sec("tensor", _)

                    def _(v, t_i=t_i):
                        W(v, sT, ("p3stats", t_i, 15))
                        v.drain()
                        v.tensor_scalar_mul(st_a[:], PS[0][0:1, :], INVD)
                        ins = v.tensor_scalar_mul(st_b[:], PS[1][0:1, :], INVD)
                        sV.inc(ins)
                        v.drain()
                        v.tensor_mul(st_c[:], st_a[:], st_a[:])
                        v.drain()
                        v.tensor_sub(st_b[:], st_b[:], st_c[:])
                        v.drain()
                        ins = v.tensor_scalar_add(st_b[:], st_b[:], EPS)
                        sV.inc(ins)
                        M(("p3veps", t_i), sV)
                        W(v, sA, ("p3sqrt", t_i))
                        ins = v.reciprocal(st_d[:], st_c[:])
                        sV.inc(ins)
                        for _it in range(2):
                            v.drain()
                            v.tensor_mul(st_e[:], st_b[:], st_d[:])
                            v.drain()
                            v.tensor_mul(st_e[:], st_e[:], st_d[:])
                            v.drain()
                            v.tensor_scalar(st_e[:], st_e[:], -0.5, 1.5,
                                            ALU.mult, ALU.add)
                            v.drain()
                            v.tensor_mul(st_d[:], st_d[:], st_e[:])
                        ins = v.engine_nop()
                        sV.inc(ins)
                        M(("p3lnt", t_i), sV)
                        W(v, sT, ("p3bcm", t_i))
                        v.drain()
                        ins = v.tensor_copy(mb_b[:], PS[2][:])
                        sV.inc(ins)
                        W(v, sT, ("p3bcy", t_i))
                        ins = v.tensor_copy(yb_b[:], PS[3][:])
                        sV.inc(ins)
                        M(("p3ybb", t_i), sV)
                        W(v, sT, ("p3gu", t_i, 9, NHC - 1))
                        for d in range(16):
                            sl = slice(d * 512, (d + 1) * 512)
                            v.drain()
                            v.tensor_sub(x1t[:, sl], acc[:, sl], mb_b[:])
                            v.drain()
                            ins = v.tensor_mul(x1t[:, sl], x1t[:, sl], yb_b[:])
                            sV.inc(ins)
                        M(("p3xn", t_i), sV)
                        for tb in range(4):
                            W(v, sT, ("p3otr", t_i, tb))
                            if (t_i, tb) != (0, 0):
                                pb = (t_i, tb - 1) if tb >= 1 else (t_i - 1, 3)
                                W(v, sor, ("p3ost",) + pb)
                            v.drain()
                            for q4 in range(4):
                                ins = v.tensor_copy(
                                    orow[:, q4 * 512:(q4 + 1) * 512],
                                    PS[4 + q4][:])
                            sV.inc(ins)
                            M(("p3oev", t_i, tb), sV)
                    env.sec("vector", _)

                    def _(gp, t_i=t_i):
                        for tb in range(4):
                            W(gp, sV, ("p3oev", t_i, tb))
                            ins = gp.dma_start(
                                out=out[t_i * 512 + tb * 128:
                                        t_i * 512 + (tb + 1) * 128, :],
                                in_=orow[:])
                            sor.inc(ins, 16)
                            M(("p3ost", t_i, tb), sor)
                    env.sec("gpsimd", _)
            env.run(ph_p3b)
            pes.close()

            _finish(block, dma_sems)
    lower_extended_insts(nc)
    return nc


def _finish(block, store_sems):
    finals = [(s, s.n) for s in store_sems]

    def _(g):
        for s, n in finals:
            g.wait_ge(s.h, n)
    block.gpsimd(_)


def _in_maps(inputs):
    x = np.asarray(inputs["L0_attn_x"], dtype=np.float32)
    gwT = np.ascontiguousarray(np.asarray(inputs["L0_ffn_gate_w"],
                                          dtype=np.float32).T)
    sgw = np.ascontiguousarray(np.asarray(inputs["L0_ffn_shgate_w"],
                                          dtype=np.float32).reshape(D, 1))
    shared = {
        "wq": np.ascontiguousarray(np.asarray(inputs["L0_attn_Wq"], np.float32)),
        "wk": np.ascontiguousarray(np.asarray(inputs["L0_attn_Wk"], np.float32)),
        "wv": np.ascontiguousarray(np.asarray(inputs["L0_attn_Wv"], np.float32)),
        "wo": np.ascontiguousarray(np.asarray(inputs["L0_attn_Wo"], np.float32)),
        "gwT": gwT,
        "wg": np.ascontiguousarray(
            np.asarray(inputs["L0_ffn_w_gate"], np.float32)
            .reshape(E, NCH, 128, NHC, 128).transpose(0, 2, 3, 1, 4)
            .reshape(E, 128, NHC * D)),
        "wu": np.ascontiguousarray(
            np.asarray(inputs["L0_ffn_w_up"], np.float32)
            .reshape(E, NCH, 128, NHC, 128).transpose(0, 2, 3, 1, 4)
            .reshape(E, 128, NHC * D)),
        "wd": np.ascontiguousarray(np.asarray(inputs["L0_ffn_w_down"], np.float32)),
        "sgw": sgw,
        "shg": np.ascontiguousarray(
            np.asarray(inputs["L0_ffn_shw_gate"], np.float32)
            .reshape(NCH, 128, NSC, 128).transpose(1, 2, 0, 3)
            .reshape(128, NSC * D)),
        "shu": np.ascontiguousarray(
            np.asarray(inputs["L0_ffn_shw_up"], np.float32)
            .reshape(NCH, 128, NSC, 128).transpose(1, 2, 0, 3)
            .reshape(128, NSC * D)),
        "shd": np.ascontiguousarray(np.asarray(inputs["L0_ffn_shw_down"], np.float32)),
    }
    maps = []
    for c in range(8):
        b, hf = divmod(c, 2)
        own = x[b, hf * TOWN:(hf + 1) * TOWN, :]
        other = x[b, (1 - hf) * TOWN:(2 - hf) * TOWN, :]
        xin_np = np.ascontiguousarray(np.concatenate([own, other], axis=0))
        km = np.zeros((128, 32), dtype=np.float32)
        if hf == 1:
            km[:, 16:] = -FMAX
        m = dict(shared)
        m["xin"] = xin_np
        m["kmask"] = km
        maps.append(m)
    return maps


def kernel(**inputs):
    maps = _in_maps(inputs)
    nc = build_nc()
    res = run_bass_kernel_spmd(nc, maps, core_ids=list(range(8)))
    outp = np.empty((4, 2 * TOWN, D), dtype=np.float32)
    for c in range(8):
        b, hf = divmod(c, 2)
        outp[b, hf * TOWN:(hf + 1) * TOWN, :] = res.results[c]["out"]
    return outp
